# version 01
# baseline (speedup 1.0000x reference)
from contextlib import ExitStack

import numpy as np
import ml_dtypes

import concourse.bass as bass
import concourse.mybir as mybir
from concourse.bass_utils import run_bass_kernel_spmd

F32 = mybir.dt.float32
BF16 = mybir.dt.bfloat16
AF = mybir.ActivationFunctionType
ALU = mybir.AluOpType
AX = mybir.AxisListType

NCORE = 8
NSEQ = 2
T = 2048
D = 1024
DC = D // 128
TT = 512
NTT = T // TT
EPS_LN = 1e-5

_DEBUG_TAP = None
_PHASES = 99
_F_ATT_HP = 8
_F_ATT_PART = 99
_B_PART = 99
_B_UNITS = 99
_SKIP_EARLY = False


class Emitter:
    ENGINES = ("sync", "gpsimd", "tensor", "vector", "scalar")

    def __init__(self):
        self.ops = {e: [] for e in self.ENGINES}

    def on(self, eng, fn):
        self.ops[eng].append(fn)

    def replay(self, nc):
        with nc.Block() as blk:
            for eng in self.ENGINES:
                if not self.ops[eng]:
                    continue
                def run(e, _ops=tuple(self.ops[eng])):
                    for f in _ops:
                        f(e)
                getattr(blk, eng)(run)


class Sems:

    _REG = None

    def __init__(self, nc, es, prefix):
        if Sems._REG is None or Sems._REG[0] is not nc:
            Sems._REG = (nc, es, {}, {})
        object.__setattr__(self, "_d", None)

    ALIAS = {}
    for _grp in ("og1 og2 og3|og", "r1 r2 r2b r3 r4 r5 r6 r8 mlg sg|rt",
                 "cv1 cv2 cv3 cv4|cv", "l2s l2m l2r l2c|l2x", "btr gtr|tr3",
                 "bmm mA mai|mai", "gt gr gb gv|gat", "map map2 eap|map",
                 "mui msc|msn", "ktr vtr mwt kbm|trm", "gbm obm bbc|bcm",
                 "gsg orr|sgr", "onm ofin|onf", "elv1 els|elx", "inv mzb|izb",
                 "mst|mp", "mmq mmv mmf|mma", "zfin vz|zf", "mat mkv|mk",
                 "sev uie|su", "esq msq|sq2",
                 "ld lw1 lw2 cl|dls",
                 "ldu wl bl owl onl cw pl|dlg",
                 "cl2|cl2", "ac1 ac2 acc2|acc", "sta stx|st", "evo|ev",
                 "vinit brow sq cast bc var app aff bf stat rstd|mi1",
                 "bsg bme|mi2"):
        _names, _to = _grp.split("|")
        for _n in _names.split():
            ALIAS[_n] = _to

    def sem(self, name):
        name = Sems.ALIAS.get(name, name)
        nc, es, sems, cnt = Sems._REG
        if name not in sems:
            sems[name] = es.enter_context(nc.semaphore(f"s_{name}"))
            cnt[name] = 0
        return sems[name]

    def bump(self, name, k=1):
        name = Sems.ALIAS.get(name, name)
        self.sem(name)
        Sems._REG[3][name] += k
        return Sems._REG[3][name]

    def val(self, name):
        name = Sems.ALIAS.get(name, name)
        self.sem(name)
        return Sems._REG[3][name]


def build_program():
    nc = bass.Bass("TRN2", target_bir_lowering=False, detect_race_conditions=False)
    es = ExitStack()

    def S(name, shape, dtype):
        return es.enter_context(nc.sbuf_tensor(name, shape, dtype))

    def P(name, shape, dtype=F32):
        return es.enter_context(nc.psum_tensor(name, shape, dtype))

    DI = lambda name, shape, dtype=F32: nc.dram_tensor(name, shape, dtype, kind="ExternalInput")
    DS = lambda name, shape, dtype=F32: nc.dram_tensor(name, shape, dtype, kind="Internal")

    x_in = DI("x", [NSEQ, DC, 128, T])
    y_out = nc.dram_tensor("y", [NSEQ, DC, 128, T], F32, kind="ExternalOutput")

    wqkvT = DI("wqkvT", [D, 3 * D])
    woutT = DI("woutT", [D, D])
    w1T = DI("w1T", [D, 4 * D])
    w2T = DI("w2T", [4 * D, D])
    projW = DI("projW", [16, 64, 256])
    gammaA = DI("gammaA", [128, DC]); betaA = DI("betaA", [128, DC])
    b_outA = DI("b_outA", [128, DC]); b1A = DI("b1A", [128, 32]); b2A = DI("b2A", [128, DC])

    Wq = DI("Wq", [D, D]); Wk = DI("Wk", [D, D]); Wv = DI("Wv", [D, D]); Wo = DI("Wo", [D, D])
    bqF = DI("bqF", [128, DC]); bkF = DI("bkF", [128, DC]); bvF = DI("bvF", [128, DC]); boF = DI("boF", [128, DC])
    ln1g = DI("ln1g", [128, DC]); ln1b = DI("ln1b", [128, DC])
    ln2g = DI("ln2g", [128, DC]); ln2b = DI("ln2b", [128, DC])
    WrD = DI("WrD", [D, 8]); brD = DI("brD", [1, 8])
    ew1 = DI("ew1", [8, D, 4 * D]); ew2 = DI("ew2", [8, 4 * D, D])
    eb1 = DI("eb1", [128, 8 * 32]); eb2 = DI("eb2", [128, 8 * 8])

    qwB = DI("qwB", [D, D]); kwB = DI("kwB", [D, D]); vwB = DI("vwB", [D, D]); owB = DI("owB", [D, D])
    bwmix = DI("bwmix", [D, 16])
    convq = DI("convq", [128, DC, 4]); convk = DI("convk", [128, DC, 4]); convv = DI("convv", [128, DC, 4])
    mixb_in = DI("mixb_in", [1, 8]); mixbias_in = DI("mixbias_in", [1, 8])
    onormW = DI("onormW", [128, 1])

    identB = DI("identB", [128, 128], BF16)
    identF = DI("identF", [128, 128])
    mask_kq = DI("mask_kq", [128, 128], BF16)
    maskA_neg = DI("maskA_neg", [128, 128], BF16)
    maskAT_neg = DI("maskAT_neg", [128, 128], BF16)
    maskTriIncT = DI("maskTriIncT", [128, 128], BF16)
    m512 = [DI(f"m512_{m}", [128, 512], BF16) for m in range(4)]

    SHP = [NSEQ, DC, 128, T]
    s_r = DS("s_r", SHP)
    s_u2 = DS("s_u2", SHP)
    s_xA = DS("s_xA", SHP)
    s_ffn_in = DS("s_ffn_in", SHP)
    s_a = DS("s_a", SHP)
    s_xB = DS("s_xB", SHP)
    s_xC = DS("s_xC", SHP)
    ew1b = nc.dram_tensor("ew1b", [8, D, 4 * D], BF16, kind="Internal")
    ew2b = nc.dram_tensor("ew2b", [8, 4 * D, D], BF16, kind="Internal")

    def dtile(dr, s, t0, t1):
        return dr[s, :, :, t0:t1].rearrange("c p t -> p c t")

    ones_col_b = S("ones_col_b", [128, 1], BF16)
    ones_row_b = S("ones_row_b", [1, 128], BF16)
    ones_row_f = S("ones_row_f", [1, 128], F32)
    sb_identB = S("sb_identB", [128, 128], BF16)
    sb_identF = S("sb_identF", [128, 128], F32)
    sb_mask_kq = S("sb_mask_kq", [128, 128], BF16)
    sb_maskA = S("sb_maskA", [128, 128], BF16)
    sb_maskAT = S("sb_maskAT", [128, 128], BF16)
    sb_maskTIT = S("sb_maskTIT", [128, 128], BF16)
    sb_m512 = [S(f"sb_m512_{m}", [128, 512], BF16) for m in range(4)]

    sb_gA = S("sb_gA", [128, DC], F32); sb_bA = S("sb_bA", [128, DC], F32)
    sb_boutA = S("sb_boutA", [128, DC], F32)
    sb_b1A = S("sb_b1A", [128, 32], F32); sb_b2A = S("sb_b2A", [128, DC], F32)
    sb_bqF = S("sb_bqF", [128, DC], F32); sb_bkF = S("sb_bkF", [128, DC], F32)
    sb_bvF = S("sb_bvF", [128, DC], F32); sb_boF = S("sb_boF", [128, DC], F32)
    sb_ln1g = S("sb_ln1g", [128, DC], F32); sb_ln1b = S("sb_ln1b", [128, DC], F32)
    sb_ln2g = S("sb_ln2g", [128, DC], F32); sb_ln2b = S("sb_ln2b", [128, DC], F32)
    sb_Wr = S("sb_Wr", [128, DC, 8], BF16)
    sb_br = S("sb_br", [1, 8], F32)
    sb_br_b = S("sb_br_b", [1, 8], BF16)
    sb_eb1 = S("sb_eb1", [128, 8 * 32], F32)
    sb_eb2 = S("sb_eb2", [128, 8 * 8], F32)
    sb_convq = S("sb_convq", [128, DC, 4], F32)
    sb_convk = S("sb_convk", [128, DC, 4], F32)
    sb_convv = S("sb_convv", [128, DC, 4], F32)
    sb_mix8a = S("sb_mix8a", [1, 8], F32)
    sb_mixsum = S("sb_mixsum", [1, 8], F32)
    sb_onorm = S("sb_onorm", [128, 1], F32)

    gsem = Sems(nc, es, "g")

    const_pool = {}

    def constf(val):
        if val not in const_pool:
            const_pool[val] = S(f"constf_{len(const_pool)}", [128, 1], F32)
        return const_pool[val]

    for _v in (1e-5, 1e-12, 1e-8):
        constf(_v)

    em = Emitter()

    def _const_loads(e):
        loads = [
            (sb_identB[:, :], identB[:, :]), (sb_identF[:, :], identF[:, :]),
            (sb_mask_kq[:, :], mask_kq[:, :]),
            (sb_maskA[:, :], maskA_neg[:, :]), (sb_maskAT[:, :], maskAT_neg[:, :]),
            (sb_maskTIT[:, :], maskTriIncT[:, :]),
            (sb_m512[0][:, :], m512[0][:, :]), (sb_m512[1][:, :], m512[1][:, :]),
            (sb_m512[2][:, :], m512[2][:, :]), (sb_m512[3][:, :], m512[3][:, :]),
            (sb_gA[:, :], gammaA[:, :]), (sb_bA[:, :], betaA[:, :]),
            (sb_boutA[:, :], b_outA[:, :]), (sb_b1A[:, :], b1A[:, :]), (sb_b2A[:, :], b2A[:, :]),
            (sb_bqF[:, :], bqF[:, :]), (sb_bkF[:, :], bkF[:, :]),
            (sb_bvF[:, :], bvF[:, :]), (sb_boF[:, :], boF[:, :]),
            (sb_ln1g[:, :], ln1g[:, :]), (sb_ln1b[:, :], ln1b[:, :]),
            (sb_ln2g[:, :], ln2g[:, :]), (sb_ln2b[:, :], ln2b[:, :]),
            (sb_br[:, :], brD[:, :]),
            (sb_eb1[:, :], eb1[:, :]), (sb_eb2[:, :], eb2[:, :]),
            (sb_convq[:, :, :], convq[:, :, :]), (sb_convk[:, :, :], convk[:, :, :]),
            (sb_convv[:, :, :], convv[:, :, :]),
            (sb_onorm[:, :], onormW[:, :]),
            (sb_mix8a[:, :], mixb_in[:, :]),
            (sb_mixsum[:, :], mixbias_in[:, :]),
        ]
        for dst, src in loads:
            e.dma_start(dst, src).then_inc(gsem.sem("cl"), 16)
            gsem.bump("cl", 16)

    def _cast_loads(e):
        e.dma_start(sb_Wr[:, :, :], WrD.rearrange("(c p) o -> p c o", p=128)).then_inc(gsem.sem("cw"), 16)
        gsem.bump("cw", 16)

    def _const_init(v):
        v.wait_ge(gsem.sem("cl"), gsem.val("cl"))
        v.wait_ge(gsem.sem("cw"), gsem.val("cw"))
        v.memset(ones_col_b[:, :], 1.0)
        v.memset(ones_row_b[:, :], 1.0)
        v.memset(ones_row_f[:, :], 1.0)
        v.tensor_add(sb_mixsum[:, :], sb_mixsum[:, :], sb_mix8a[:, :])
        v.tensor_copy(sb_br_b[:, :], sb_br[:, :])
        for _v, _t in const_pool.items():
            v.memset(_t[:, :], _v)

    em.on("sync", _const_loads)
    em.on("gpsimd", _cast_loads)
    em.on("vector", _const_init)
    em.replay(nc)

    def emit_moe_casts(em, eids):
        def go(e, eids=tuple(eids)):
            for ei in eids:
                e.dma_start(ew1b[ei, :, :], ew1[ei, :, :]).then_inc(gsem.sem("mcast"), 16)
                e.dma_start(ew2b[ei, :, :], ew2[ei, :, :]).then_inc(gsem.sem("mcast"), 16)
                gsem.bump("mcast", 32)
        em.on("gpsimd", go)

    ln_sems = Sems(nc, es, "LN")

    class LNHelper:

        def __init__(self, tag, stack, ext_xb=None, ext_xsq=None):
            Sl = lambda n, sh, dt: stack.enter_context(nc.sbuf_tensor(n, sh, dt))
            Pl = lambda n, sh: stack.enter_context(nc.psum_tensor(n, sh, F32))
            self.ps_stats = Pl(f"lnps_{tag}", [1, 2 * TT])
            self.ps_ab = Pl(f"lnab_{tag}", [128, 2 * TT])
            if ext_xb is None:
                t = Sl(f"lnxb_{tag}", [128, DC * TT], BF16)
                ext_xb = (t[:, :], lambda c, t=t: t[:, c * TT:(c + 1) * TT])
            if ext_xsq is None:
                t2 = Sl(f"lnxq_{tag}", [128, DC * TT], BF16)
                ext_xsq = (t2[:, :], lambda c, t2=t2: t2[:, c * TT:(c + 1) * TT])
            self.xb_full, self.xb_c = ext_xb
            self.xsq_full, self.xsq_c = ext_xsq
            self.rows = Sl(f"lnrw_{tag}", [1, 2 * TT], F32)
            self.mu = Sl(f"lnmu_{tag}", [1, TT], F32)
            self.var = Sl(f"lnvr_{tag}", [1, TT], F32)
            self.sem = ln_sems

        def emit(self, em, xin, rt, store_bf=None, gain=None, bias=None,
                 rt_ready_wait=None, xin_wait=None):
            sem = self.sem
            c_stat = sem.val("stat")
            c_var = sem.val("var")
            c_rstd = sem.val("rstd")
            c_app = sem.val("app")
            c_aff = sem.val("aff")

            def v_cast(v):
                if xin_wait is not None:
                    xin_wait(v)
                v.wait_ge(sem.sem("stat"), c_stat)
                v.tensor_copy(self.xb_full, xin.rearrange("p c t -> p (c t)")
                              ).then_inc(sem.sem("cast"), 1)
            em.on("vector", v_cast)
            n_cast = sem.bump("cast")

            def s_sq(s):
                if xin_wait is not None:
                    xin_wait(s)
                s.wait_ge(sem.sem("stat"), c_stat)
                s.square(self.xsq_full, xin.rearrange("p c t -> p (c t)")
                         ).then_inc(sem.sem("sq"), 1)
            em.on("scalar", s_sq)
            n_sq = sem.bump("sq")

            def t_stats(t):
                t.wait_ge(sem.sem("cast"), n_cast)
                t.wait_ge(sem.sem("sq"), n_sq)
                t.wait_ge(sem.sem("var"), c_var)
                for c in range(DC):
                    t.matmul(self.ps_stats[0:1, 0:TT], ones_col_b[:, 0:1],
                             self.xb_c(c), start=(c == 0), stop=(c == DC - 1))
                for c in range(DC):
                    i = t.matmul(self.ps_stats[0:1, TT:2 * TT], ones_col_b[:, 0:1],
                                 self.xsq_c(c), start=(c == 0), stop=(c == DC - 1))
                i.then_inc(sem.sem("stat"), 1)
            em.on("tensor", t_stats)
            n_stat = sem.bump("stat")

            def v_rows(v):
                v.wait_ge(sem.sem("stat"), n_stat)
                v.wait_ge(sem.sem("rstd"), c_rstd)
                v.tensor_single_scalar(self.mu[0:1, :], self.ps_stats[0:1, 0:TT], 1.0 / D, ALU.mult)
                v.tensor_single_scalar(self.var[0:1, :], self.ps_stats[0:1, TT:2 * TT], 1.0 / D, ALU.mult)
                v.tensor_mul(self.rows[0:1, 0:TT], self.mu[0:1, :], self.mu[0:1, :])
                v.tensor_sub(self.var[0:1, :], self.var[0:1, :], self.rows[0:1, 0:TT]
                             ).then_inc(sem.sem("var"), 1)
            em.on("vector", v_rows)
            n_var = sem.bump("var")

            def s_rstd(s):
                s.wait_ge(sem.sem("var"), n_var)
                s.activation(self.rows[0:1, 0:TT], self.var[0:1, :], AF.Ln, bias=constf(EPS_LN)[0:1, 0:1])
                s.activation(self.rows[0:1, 0:TT], self.rows[0:1, 0:TT], AF.Exp, scale=-0.5
                             ).then_inc(sem.sem("rstd"), 1)
            em.on("scalar", s_rstd)
            n_rstd = sem.bump("rstd")

            def v_brow(v):
                v.wait_ge(sem.sem("rstd"), n_rstd)
                v.tensor_mul(self.rows[0:1, TT:2 * TT], self.mu[0:1, :], self.rows[0:1, 0:TT]
                             ).then_inc(sem.sem("brow"), 1)
            em.on("vector", v_brow)
            n_brow = sem.bump("brow")

            def t_bc(t):
                t.wait_ge(sem.sem("brow"), n_brow)
                t.wait_ge(sem.sem("app"), c_app)
                t.matmul(self.ps_ab[:, 0:TT], ones_row_f[0:1, :], self.rows[0:1, 0:TT],
                         start=True, stop=True)
                t.matmul(self.ps_ab[:, TT:2 * TT], ones_row_f[0:1, :], self.rows[0:1, TT:2 * TT],
                         start=True, stop=True).then_inc(sem.sem("bc"), 1)
            em.on("tensor", t_bc)
            n_bc = sem.bump("bc")

            fin_on_vector = (gain is None)

            def v_apply(v):
                v.wait_ge(sem.sem("bc"), n_bc)
                if gain is not None:
                    v.wait_ge(sem.sem("aff"), c_aff)
                if rt_ready_wait is not None:
                    rt_ready_wait(v)
                last = None
                for c in range(DC):
                    v.tensor_mul(rt[:, c, :], xin[:, c, :], self.ps_ab[:, 0:TT])
                    last = v.tensor_sub(rt[:, c, :], rt[:, c, :], self.ps_ab[:, TT:2 * TT])
                    if fin_on_vector and store_bf is not None:
                        last = v.tensor_copy(store_bf(c), rt[:, c, :])
                last.then_inc(sem.sem("app"), 1)
            em.on("vector", v_apply)
            n_app = sem.bump("app")
            if gain is None:
                return (sem.sem("app"), n_app)

            if gain is not None:
                def s_aff(s):
                    s.wait_ge(sem.sem("app"), n_app)
                    last = None
                    for c in range(DC):
                        last = s.activation(rt[:, c, :], rt[:, c, :], AF.Identity,
                                            bias=bias[:, c:c + 1], scale=gain[:, c:c + 1])
                    last.then_inc(sem.sem("aff"), 1)
                em.on("scalar", s_aff)
                n_aff = sem.bump("aff")
                if store_bf is None:
                    return (sem.sem("aff"), n_aff)
                def v_bf(v):
                    v.wait_ge(sem.sem("aff"), n_aff)
                    last = None
                    for c in range(DC):
                        last = v.tensor_copy(store_bf(c), rt[:, c, :])
                    last.then_inc(sem.sem("bf"), 1)
                em.on("vector", v_bf)
                n_bf = sem.bump("bf")
                return (sem.sem("bf"), n_bf)

    def ln_phase(tag, src_dram, dst_dram, xbf_sb, gain=None, bias=None):
        pes = ExitStack()
        ph = Sems(nc, es, tag)
        em = Emitter()
        ln = LNHelper(tag, pes)
        Sl = lambda n, sh, dt: pes.enter_context(nc.sbuf_tensor(n, sh, dt))
        xin = [Sl(f"{tag}_xin{j}", [128, DC, TT], F32) for j in range(2)]
        rt = [Sl(f"{tag}_rt{j}", [128, DC, TT], F32) for j in range(2)] \
            if dst_dram is not None else xin
        tiles = [(s, tt) for s in range(NSEQ) for tt in range(NTT)]
        done_handles = []
        for i, (s, tt) in enumerate(tiles):
            j = i % 2
            def dma_in(e, s=s, tt=tt, i=i, j=j, done_handles=done_handles):
                if i >= 2:
                    h = done_handles[i - 2]
                    e.wait_ge(h[0], h[1])
                e.dma_start(xin[j][:, :, :], dtile(src_dram, s, tt * TT, (tt + 1) * TT)
                            ).then_inc(ph.sem("ld"), 16)
            em.on("sync", dma_in)
            n_ld = ph.bump("ld", 16)

            def xin_wait(eng, n_ld=n_ld):
                eng.wait_ge(ph.sem("ld"), n_ld)

            c_st_r = ph.val("st")

            def rt_wait(eng, i=i, c_st_r=c_st_r):
                if i >= 2 and dst_dram is not None:
                    eng.wait_ge(ph.sem("st"), c_st_r - 16)

            store_bf = None
            if xbf_sb is not None:
                def store_bf(c, s=s, tt=tt):
                    return xbfA_ap(s, c, tt * TT, (tt + 1) * TT)

            h = ln.emit(em, xin[j][:, :, :], rt[j][:, :, :], store_bf=store_bf,
                        gain=gain, bias=bias, xin_wait=xin_wait, rt_ready_wait=rt_wait)
            done_handles.append(h)

            def dma_out(e, j=j, s=s, tt=tt, h=h):
                e.wait_ge(h[0], h[1])
                if dst_dram is not None:
                    e.dma_start(dtile(dst_dram, s, tt * TT, (tt + 1) * TT), rt[j][:, :, :]
                                ).then_inc(ph.sem("st"), 16)
            em.on("sync", dma_out)
            if dst_dram is not None:
                ph.bump("st", 16)
        def fin(e):
            if dst_dram is not None:
                e.wait_ge(ph.sem("st"), ph.val("st"))
        em.on("sync", fin)
        em.replay(nc)
        pes.close()

    XA = [None, None]
    XB = [None, None]

    def openA(tag):
        st = ExitStack()
        XA[0] = st
        XA[1] = st.enter_context(nc.sbuf_tensor(f"xa_{tag}", [128, NSEQ * DC * T], BF16,
                                                side="left"))

    def closeA():
        XA[0].close(); XA[0] = XA[1] = None

    def openB(tag):
        st = ExitStack()
        XB[0] = st
        XB[1] = st.enter_context(nc.sbuf_tensor(f"xb_{tag}", [128, NSEQ * DC * T], BF16,
                                                side="right"))

    def closeB():
        XB[0].close(); XB[0] = XB[1] = None

    def xbfA_ap(s, c, t0, t1):
        base = (s * DC + c) * T
        return XA[1][:, base + t0: base + t1]

    if not _SKIP_EARLY:
        openA("a1")
        ln_phase("aln", x_in, s_r, "XA")

    def xbfB_ap(s, c, t0, t1):
        base = (s * DC + c) * T
        return XB[1][:, base + t0: base + t1]

    def phase_a_attn():
        pes = ExitStack()
        Sl = lambda n, sh, dt: pes.enter_context(nc.sbuf_tensor(n, sh, dt))
        Pl = lambda n, sh: pes.enter_context(nc.psum_tensor(n, sh, F32))
        ph = Sems(nc, es, "aat")
        em = Emitter()
        NH = 16
        sb_proj = Sl("sb_proj", [128, 16, 256], BF16)

        def proj_load(e):
            e.dma_start(sb_proj[0:64, :, :], projW.rearrange("h d f -> d h f")).then_inc(ph.sem("pl"), 16)
            e.dma_start(sb_proj[64:128, :, :], projW.rearrange("h d f -> d h f")).then_inc(ph.sem("pl"), 16)
        em.on("gpsimd", proj_load)
        n_pl = ph.bump("pl", 32)

        def proj_gate(t, n_pl=n_pl):
            t.wait_ge(ph.sem("pl"), n_pl)
        em.on("tensor", proj_gate)

        wqk = [Sl(f"aat_wqk{j}", [128, DC, 128], BF16) for j in range(2)]
        wv = [Sl(f"aat_wv{j}", [128, DC, 64], BF16) for j in range(2)]
        qkT = Sl("aat_qkT", [128, T], BF16)
        v_aug = Sl("aat_vaug", [128, 16, 65], BF16)
        qfT = Sl("aat_qfT", [128, 2, T], BF16)
        kfN = Sl("aat_kfN", [128, 16, 256], BF16)
        kv_sb = Sl("aat_kv", [128, 2, 65], BF16)
        elu_m = Sl("aat_em", [128, 512], F32)
        elu_r = Sl("aat_er", [128, 512], F32)
        elu_e = Sl("aat_ee", [128, 512], F32)
        zr_t = Sl("aat_zrt", [1, 512], F32)
        zr = Sl("aat_zr", [1, 512], F32)
        zb_sb = Sl("aat_zb", [128, 512], F32)
        ps_qk = [Pl(f"aat_psqk{j}", [128, 512]) for j in range(2)]
        ps_v = Pl("aat_psv", [128, 64])
        ps_f = [Pl("aat_psf0", [128, 512])]
        ps_kv = Pl("aat_pskv", [128, 65])
        ps_at = Pl("aat_psat", [128, 512])
        ps_z = Pl("aat_psz", [1, 512])
        ps_zb = Pl("aat_pszb", [128, 512])

        def vaug_init(v):
            v.memset(v_aug[:, :, :], 0.0)
            for nt in range(16):
                v.memset(v_aug[:, nt, 64:65], 1.0)
            v.nop().then_inc(ph.sem("vinit"), 1)
        em.on("vector", vaug_init)
        n_vinit = ph.bump("vinit")

        base_wf = ph.val("wfree")
        for h in range(NH):
            j = h % 2
            def wload(e, h=h, j=j, base_wf=base_wf):
                if h >= 2:
                    e.wait_ge(ph.sem("wfree"), base_wf + h - 1)
                e.dma_start(wqk[j][:, :, 0:64],
                            wqkvT[:, 64 * h:64 * h + 64].rearrange("(c p) o -> p c o", p=128)
                            ).then_inc(ph.sem("wld"), 16)
                e.dma_start(wqk[j][:, :, 64:128],
                            wqkvT[:, D + 64 * h:D + 64 * h + 64].rearrange("(c p) o -> p c o", p=128)
                            ).then_inc(ph.sem("wld"), 16)
                e.dma_start(wv[j][:, :, :],
                            wqkvT[:, 2 * D + 64 * h:2 * D + 64 * h + 64].rearrange("(c p) o -> p c o", p=128)
                            ).then_inc(ph.sem("wld"), 16)
            em.on("gpsimd", wload)
            n_wld = ph.bump("wld", 48)

            for s in range(NSEQ):
                u = h * NSEQ + s
                c_mmf_g = ph.val("mmf")
                for tt in range(NTT):
                    b = tt % 2
                    c_evq = ph.val("evq")

                    def t_qk(t, h=h, s=s, tt=tt, b=b, n_wld=n_wld, c_evq=c_evq, j=j):
                        t.wait_ge(ph.sem("wld"), n_wld)
                        if c_evq >= 2:
                            t.wait_ge(ph.sem("evq"), c_evq - 1)
                        for c in range(DC):
                            i = t.matmul(ps_qk[b][:, :], wqk[j][:, c, :],
                                         xbfA_ap(s, c, tt * TT, (tt + 1) * TT),
                                         start=(c == 0), stop=(c == DC - 1))
                        i.then_inc(ph.sem("mmq"), 1)
                    em.on("tensor", t_qk)
                    n_mmq = ph.bump("mmq")

                    def v_evq(v, tt=tt, b=b, n_mmq=n_mmq, c_mmf_g=c_mmf_g):
                        v.wait_ge(ph.sem("mmq"), n_mmq)
                        v.wait_ge(ph.sem("mmf"), c_mmf_g)
                        v.tensor_copy(qkT[:, tt * TT:(tt + 1) * TT], ps_qk[b][:, :]
                                      ).then_inc(ph.sem("evq"), 1)
                    em.on("vector", v_evq)
                    ph.bump("evq")
                n_evq_done = ph.val("evq")

                c_mkv_v = ph.val("mkv")
                for nt in range(16):
                    c_evv = ph.val("evv")

                    def t_v(t, s=s, nt=nt, c_evv=c_evv, j=j, u=u):
                        if nt == 0 and u == 0:
                            t.wait_ge(ph.sem("vinit"), n_vinit)
                        if c_evv >= 1:
                            t.wait_ge(ph.sem("evv"), c_evv)
                        for c in range(DC):
                            i = t.matmul(ps_v[:, :], xbfA_ap(s, c, nt * 128, (nt + 1) * 128),
                                         wv[j][:, c, :], start=(c == 0), stop=(c == DC - 1))
                        i.then_inc(ph.sem("mmv"), 1)
                    em.on("tensor", t_v)
                    n_mmv = ph.bump("mmv")

                    def v_evv(v, nt=nt, n_mmv=n_mmv, c_mkv_v=c_mkv_v):
                        v.wait_ge(ph.sem("mmv"), n_mmv)
                        v.wait_ge(ph.sem("mkv"), c_mkv_v)
                        v.tensor_copy(v_aug[:, nt, 0:64], ps_v[:, :]).then_inc(ph.sem("evv"), 1)
                    em.on("vector", v_evv)
                    ph.bump("evv")
                if s == NSEQ - 1:
                    def t_wfree(t, n=ph.val("mmv")):
                        t.nop().then_inc(ph.sem("wfree"), 1)
                    em.on("tensor", t_wfree)
                    ph.bump("wfree")

                def emit_elu(ps, n_mm, out_ap, width, out_guard=None):
                    c_els = ph.val("els")
                    c_elf = ph.val("elf")

                    def v_1(v):
                        v.wait_ge(ph.sem("mmf"), n_mm)
                        v.wait_ge(ph.sem("els"), c_els)
                        v.tensor_single_scalar(elu_m[:, 0:width], ps, 0.0, ALU.min)
                        v.tensor_sub(elu_r[:, 0:width], ps, elu_m[:, 0:width]
                                     ).then_inc(ph.sem("elv1"), 1)
                    em.on("vector", v_1)
                    n1 = ph.bump("elv1")

                    def s_1(sc):
                        sc.wait_ge(ph.sem("elv1"), n1)
                        sc.wait_ge(ph.sem("elf"), c_elf)
                        sc.activation(elu_e[:, 0:width], elu_m[:, 0:width], AF.Exp
                                      ).then_inc(ph.sem("els"), 1)
                    em.on("scalar", s_1)
                    n2 = ph.bump("els")

                    def v_2(v):
                        v.wait_ge(ph.sem("els"), n2)
                        if out_guard is not None and out_guard[1] >= 1:
                            v.wait_ge(out_guard[0], out_guard[1])
                        v.tensor_add(out_ap, elu_e[:, 0:width], elu_r[:, 0:width]
                                     ).then_inc(ph.sem("elf"), 1)
                    em.on("vector", v_2)
                    return ph.bump("elf")

                c_mat_g = ph.val("mat")
                for fc in range(2):
                    for tt in range(NTT):
                        b = 0
                        c_elf = ph.val("elf")

                        def t_qf(t, h=h, fc=fc, tt=tt, b=b, c_elf=c_elf, n_evq_done=n_evq_done):
                            t.wait_ge(ph.sem("evq"), n_evq_done)
                            if c_elf >= 1:
                                t.wait_ge(ph.sem("elf"), c_elf)
                            t.matmul(ps_f[b][:, :], sb_proj[0:64, h, fc * 128:(fc + 1) * 128],
                                     qkT[0:64, tt * TT:(tt + 1) * TT], start=True, stop=True
                                     ).then_inc(ph.sem("mmf"), 1)
                        em.on("tensor", t_qf)
                        n_mmf = ph.bump("mmf")
                        emit_elu(ps_f[b][:, :], n_mmf, qfT[:, fc, tt * TT:(tt + 1) * TT], 512,
                                 out_guard=(ph.sem("mat"), c_mat_g))

                c_mkv_g = ph.val("mkv")
                for nt in range(16):
                    b = 0
                    c_elf = ph.val("elf")

                    def t_kf(t, h=h, nt=nt, b=b, c_elf=c_elf, n_evq_done=n_evq_done):
                        t.wait_ge(ph.sem("evq"), n_evq_done)
                        if c_elf >= 1:
                            t.wait_ge(ph.sem("elf"), c_elf)
                        t.matmul(ps_f[b][:, 0:256], qkT[64:128, nt * 128:(nt + 1) * 128],
                                 sb_proj[64:128, h, :], start=True, stop=True
                                 ).then_inc(ph.sem("mmf"), 1)
                    em.on("tensor", t_kf)
                    n_mmf = ph.bump("mmf")
                    emit_elu(ps_f[b][:, 0:256], n_mmf, kfN[:, nt, :], 256,
                             out_guard=(ph.sem("mkv"), c_mkv_g))
                n_kf_done = ph.val("elf")

                for fc in range(2):
                    c_ekv = ph.val("ekv")

                    def t_kv(t, fc=fc, n_kf=n_kf_done, c_ekv=c_ekv):
                        t.wait_ge(ph.sem("elf"), n_kf)
                        if c_ekv >= 1:
                            t.wait_ge(ph.sem("ekv"), c_ekv)
                        for nt in range(16):
                            i = t.matmul(ps_kv[:, :], kfN[:, nt, fc * 128:(fc + 1) * 128],
                                         v_aug[:, nt, :], start=(nt == 0), stop=(nt == 15))
                        i.then_inc(ph.sem("mkv"), 1)
                    em.on("tensor", t_kv)
                    n_mkv = ph.bump("mkv")

                    def v_ekv(v, fc=fc, n_mkv=n_mkv, c_mat_k=ph.val("mat")):
                        v.wait_ge(ph.sem("mkv"), n_mkv)
                        v.wait_ge(ph.sem("mat"), c_mat_k)
                        v.tensor_copy(kv_sb[:, fc, :], ps_kv[:, :]).then_inc(ph.sem("ekv"), 1)
                    em.on("vector", v_ekv)
                    ph.bump("ekv")
                n_kv_done = ph.val("ekv")

                b0 = (h % 2) * 64
                for tt in range(NTT):
                    c_zf = ph.val("zfin")

                    def t_at(t, tt=tt, n_kv=n_kv_done, c_zf=c_zf, b0=b0):
                        t.wait_ge(ph.sem("ekv"), n_kv)
                        if c_zf >= 1:
                            t.wait_ge(ph.sem("zfin"), c_zf)
                        for fc in range(2):
                            t.matmul(ps_at[b0:b0 + 64, :], kv_sb[:, fc, 0:64],
                                     qfT[:, fc, tt * TT:(tt + 1) * TT],
                                     start=(fc == 0), stop=(fc == 1), tile_position=(0, b0))
                        for fc in range(2):
                            i = t.matmul(ps_z[0:1, :], kv_sb[:, fc, 64:65],
                                         qfT[:, fc, tt * TT:(tt + 1) * TT],
                                         start=(fc == 0), stop=(fc == 1))
                        i.then_inc(ph.sem("mat"), 1)
                    em.on("tensor", t_at)
                    n_mat = ph.bump("mat")

                    def v_z(v, n_mat=n_mat):
                        v.wait_ge(ph.sem("mat"), n_mat)
                        v.tensor_single_scalar(zr_t[0:1, :], ps_z[0:1, :], 1e-8, ALU.add)
                        v.reciprocal(zr[0:1, :], zr_t[0:1, :]).then_inc(ph.sem("vz"), 1)
                    em.on("vector", v_z)
                    n_vz = ph.bump("vz")

                    def t_zb(t, n_vz=n_vz, c_zf=c_zf, b0=b0):
                        t.wait_ge(ph.sem("vz"), n_vz)
                        if c_zf >= 1:
                            t.wait_ge(ph.sem("zfin"), c_zf)
                        t.matmul(ps_zb[b0:b0 + 64, :], ones_row_f[0:1, 0:64], zr[0:1, :],
                                 start=True, stop=True, tile_position=(0, b0)
                                 ).then_inc(ph.sem("mzb"), 1)
                    em.on("tensor", t_zb)
                    n_mzb = ph.bump("mzb")

                    def v_fin(v, h=h, s=s, tt=tt, n_mzb=n_mzb, b0=b0):
                        v.wait_ge(ph.sem("mzb"), n_mzb)
                        v.tensor_copy(zb_sb[b0:b0 + 64, :], ps_zb[b0:b0 + 64, :])
                        c = h // 2
                        dst = XB[1][b0:b0 + 64,
                                   (s * DC + c) * T + tt * TT:(s * DC + c) * T + (tt + 1) * TT]
                        v.tensor_mul(dst, ps_at[b0:b0 + 64, :], zb_sb[b0:b0 + 64, :]
                                     ).then_inc(ph.sem("zfin"), 1)
                    em.on("vector", v_fin)
                    ph.bump("zfin")

        emit_moe_casts(em, range(0, 3))
        em.replay(nc)
        pes.close()

    if not _SKIP_EARLY:
        openB("b1")
        phase_a_attn()
        closeA()

    def phase_a_out():
        pes = ExitStack()
        Sl = lambda n, sh, dt: pes.enter_context(nc.sbuf_tensor(n, sh, dt))
        Pl = lambda n, sh: pes.enter_context(nc.psum_tensor(n, sh, F32))
        ph = Sems(nc, es, "aou")
        em = Emitter()
        lnM = LNHelper("m", pes)
        wout_sb = Sl("aou_wout", [128, DC, D], BF16)
        u1 = [Sl("aou_u1_0", [128, DC, TT], F32)]
        u2 = u1
        ps_u = [Pl(f"aou_ps{j}", [128, 512]) for j in range(2)]

        def wload(e):
            e.dma_start(wout_sb[:, :, :], woutT.rearrange("(c p) o -> p c o", p=128)
                        ).then_inc(ph.sem("wld"), 16)
        em.on("gpsimd", wload)
        n_wld_a = ph.bump("wld", 16)

        tiles = [(s, tt) for s in range(NSEQ) for tt in range(NTT)]
        done_handles = []
        for i, (s, tt) in enumerate(tiles):
            j = 0
            for o in range(DC):
                b = o % 2
                c_ev = ph.val("ev")

                def t_mm(t, s=s, tt=tt, o=o, b=b, c_ev=c_ev, i=i, n_wld_a=n_wld_a):
                    t.wait_ge(ph.sem("wld"), n_wld_a)
                    if c_ev >= 2:
                        t.wait_ge(ph.sem("ev"), c_ev - 1)
                    for c in range(DC):
                        k = t.matmul(ps_u[b][:, :], wout_sb[:, c, o * 128:(o + 1) * 128],
                                     xbfB_ap(s, c, tt * TT, (tt + 1) * TT),
                                     start=(c == 0), stop=(c == DC - 1))
                    k.then_inc(ph.sem("mm"), 1)
                em.on("tensor", t_mm)
                n_mm = ph.bump("mm")

                def s_ev(sc, o=o, b=b, j=j, n_mm=n_mm, i=i, done_handles=done_handles,
                         c_st=ph.val("st")):
                    sc.wait_ge(ph.sem("mm"), n_mm)
                    if o == 0 and i >= 1:
                        h0 = done_handles[i - 1]
                        sc.wait_ge(h0[0], h0[1])
                        sc.wait_ge(ph.sem("st"), c_st)
                    sc.activation(u1[j][:, o, :], ps_u[b][:, :], AF.Identity,
                                  bias=sb_boutA[:, o:o + 1]).then_inc(ph.sem("ev"), 1)
                em.on("scalar", s_ev)
                ph.bump("ev")
            n_ev = ph.val("ev")

            def dma_racc(e, s=s, tt=tt, j=j, n_ev=n_ev):
                e.wait_ge(ph.sem("ev"), n_ev)
                e.dma_start(u1[j][:, :, :], dtile(s_r, s, tt * TT, (tt + 1) * TT),
                            accum_op=ALU.add).then_inc(ph.sem("acc"), 16)
            em.on("gpsimd", dma_racc)
            n_acc = ph.bump("acc", 16)

            def xin_wait(eng, n_acc=n_acc):
                eng.wait_ge(ph.sem("acc"), n_acc)

            def rt_wait(eng, i=i, c_st=ph.val("st")):
                if i >= 1:
                    eng.wait_ge(ph.sem("st"), c_st)

            hdl = lnM.emit(em, u1[j][:, :, :], u2[j][:, :, :], store_bf=None,
                           gain=sb_gA, bias=sb_bA, xin_wait=xin_wait, rt_ready_wait=rt_wait)
            done_handles.append(hdl)

            def dma_out(e, j=j, s=s, tt=tt, hdl=hdl):
                e.wait_ge(hdl[0], hdl[1])
                e.dma_start(dtile(s_u2, s, tt * TT, (tt + 1) * TT), u2[j][:, :, :]
                            ).then_inc(ph.sem("st"), 16)
            em.on("sync", dma_out)
            ph.bump("st", 16)

        def fin(e):
            e.wait_ge(ph.sem("st"), ph.val("st"))
        em.on("sync", fin)
        emit_moe_casts(em, range(3, 6))
        em.replay(nc)
        pes.close()

    if not _SKIP_EARLY:
        phase_a_out()
        closeB()

    def phase_a_ffn():
        pes = ExitStack()
        Sl = lambda n, sh, dt: pes.enter_context(nc.sbuf_tensor(n, sh, dt))
        Pl = lambda n, sh: pes.enter_context(nc.psum_tensor(n, sh, F32))
        ph = Sems(nc, es, "aff")
        em = Emitter()
        midsb = Sl("aff_mid", [128, 32, TT], BF16)
        u2bf = Sl("aff_u2b", [128, DC, TT], BF16)
        u3 = Sl("aff_u3", [128, DC, TT], F32)
        ps_mid = [Pl(f"aff_pm{j}", [128, 512]) for j in range(2)]
        ps_x2 = [Pl(f"aff_px{j}", [128, 512]) for j in range(2)]
        lnF = LNHelper("f", pes,
                       ext_xb=(midsb[:, 0:DC, :].rearrange("p a b -> p (a b)"),
                               lambda c: midsb[:, c, :]),
                       ext_xsq=(midsb[:, DC:2 * DC, :].rearrange("p a b -> p (a b)"),
                                lambda c: midsb[:, DC + c, :]))

        def w1v(c, f):
            return XA[1][:, c * 4096 + f * 128: c * 4096 + (f + 1) * 128]

        def w2v(fr, o):
            return XB[1][:, fr * 1024 + o * 128: fr * 1024 + (o + 1) * 128]

        def wload(e):
            e.dma_start(XA[1][:, :].rearrange("p (c f) -> p c f", f=4096),
                        w1T.rearrange("(c p) f -> p c f", p=128)).then_inc(ph.sem("wld"), 16)
            e.dma_start(XB[1][:, :].rearrange("p (c o) -> p c o", o=1024),
                        w2T.rearrange("(c p) o -> p c o", p=128)).then_inc(ph.sem("wld"), 16)
        em.on("gpsimd", wload)
        n_wld_f = ph.bump("wld", 32)

        tiles = [(s, tt) for s in range(NSEQ) for tt in range(NTT)]
        done_handles = []
        for i, (s, tt) in enumerate(tiles):
            c_mid_prev = ph.val("mmid")

            def dma_u2(e, s=s, tt=tt, c_mid_prev=c_mid_prev):
                if c_mid_prev:
                    e.wait_ge(ph.sem("mmid"), c_mid_prev)
                e.dma_start(u2bf[:, :, :], dtile(s_u2, s, tt * TT, (tt + 1) * TT)
                            ).then_inc(ph.sem("ldu"), 16)
            em.on("gpsimd", dma_u2)
            n_ldu = ph.bump("ldu", 16)

            c_stat_g = ln_sems.val("stat")
            for f in range(32):
                b = f % 2
                c_ev = ph.val("evm")

                def t_mid(t, f=f, b=b, c_ev=c_ev, n_ldu=n_ldu, n_wld_f=n_wld_f):
                    t.wait_ge(ph.sem("wld"), n_wld_f)
                    t.wait_ge(ph.sem("ldu"), n_ldu)
                    if c_ev >= 2:
                        t.wait_ge(ph.sem("evm"), c_ev - 1)
                    for c in range(DC):
                        k = t.matmul(ps_mid[b][:, :], w1v(c, f), u2bf[:, c, :],
                                     start=(c == 0), stop=(c == DC - 1))
                    k.then_inc(ph.sem("mmid"), 1)
                em.on("tensor", t_mid)
                n_mm = ph.bump("mmid")

                def s_gelu(sc, f=f, b=b, n_mm=n_mm, c_stat_g=c_stat_g, i=i):
                    sc.wait_ge(ph.sem("mmid"), n_mm)
                    if f < 2 * DC and i > 0:
                        sc.wait_ge(ln_sems.sem("stat"), c_stat_g)
                    sc.activation(midsb[:, f, :], ps_mid[b][:, :], AF.Gelu,
                                  bias=sb_b1A[:, f:f + 1]).then_inc(ph.sem("evm"), 1)
                em.on("scalar", s_gelu)
                ph.bump("evm")
            n_evm = ph.val("evm")

            for o in range(DC):
                b = o % 2
                c_ev = ph.val("evx")

                def t_x2(t, o=o, b=b, c_ev=c_ev, n_evm=n_evm):
                    t.wait_ge(ph.sem("evm"), n_evm)
                    if c_ev >= 2:
                        t.wait_ge(ph.sem("evx"), c_ev - 1)
                    for fr in range(32):
                        k = t.matmul(ps_x2[b][:, :], w2v(fr, o), midsb[:, fr, :],
                                     start=(fr == 0), stop=(fr == 31))
                    k.then_inc(ph.sem("mx2"), 1)
                em.on("tensor", t_x2)
                n_mx = ph.bump("mx2")

                def s_evx(sc, o=o, b=b, n_mx=n_mx, i=i, done_handles=done_handles,
                          c_st=ph.val("st")):
                    sc.wait_ge(ph.sem("mx2"), n_mx)
                    if o == 0 and i >= 1:
                        h0 = done_handles[i - 1]
                        sc.wait_ge(h0[0], h0[1])
                        sc.wait_ge(ph.sem("st"), c_st)
                    sc.activation(u3[:, o, :], ps_x2[b][:, :], AF.Identity,
                                  bias=sb_b2A[:, o:o + 1]).then_inc(ph.sem("evx"), 1)
                em.on("scalar", s_evx)
                ph.bump("evx")
            n_evx = ph.val("evx")

            def dma_uacc(e, s=s, tt=tt, n_evx=n_evx):
                e.wait_ge(ph.sem("evx"), n_evx)
                e.dma_start(u3[:, :, :], dtile(s_u2, s, tt * TT, (tt + 1) * TT),
                            accum_op=ALU.add).then_inc(ph.sem("acc"), 16)
            em.on("gpsimd", dma_uacc)
            n_acc = ph.bump("acc", 16)

            def xin_wait(eng, n_acc=n_acc):
                eng.wait_ge(ph.sem("acc"), n_acc)

            hdl = lnF.emit(em, u3[:, :, :], u3[:, :, :], store_bf=None,
                           gain=sb_gA, bias=sb_bA, xin_wait=xin_wait)
            done_handles.append(hdl)

            def dma_xacc(e, s=s, tt=tt, hdl=hdl):
                e.wait_ge(hdl[0], hdl[1])
                e.dma_start(u3[:, :, :],
                            x_in[s, :, :, tt * TT:(tt + 1) * TT].rearrange("c p t -> p c t"),
                            accum_op=ALU.add).then_inc(ph.sem("acc2"), 16)
            em.on("gpsimd", dma_xacc)
            n_acc2 = ph.bump("acc2", 16)

            def dma_st(e, s=s, tt=tt, n_acc2=n_acc2):
                e.wait_ge(ph.sem("acc2"), n_acc2)
                e.dma_start(dtile(s_xA, s, tt * TT, (tt + 1) * TT), u3[:, :, :]
                            ).then_inc(ph.sem("st"), 16)
            em.on("sync", dma_st)
            ph.bump("st", 16)

        def fin(e):
            e.wait_ge(ph.sem("st"), ph.val("st"))
        em.on("sync", fin)
        emit_moe_casts(em, range(6, 8))
        em.replay(nc)
        pes.close()

    if not _SKIP_EARLY:
        openA("a2"); openB("b2")
        phase_a_ffn()
        closeA(); closeB()

    def phase_f_ln(tag, src_x):
        pes = ExitStack()
        Sl = lambda n, sh, dt: pes.enter_context(nc.sbuf_tensor(n, sh, dt))
        ph = Sems(nc, es, tag)
        em = Emitter()
        ln = LNHelper(tag, pes)
        xin = Sl(f"{tag}_xin", [128, DC, TT], F32)
        r2b = Sl(f"{tag}_r2", [128, DC, TT], F32)
        tiles = [(s, tt) for s in range(NSEQ) for tt in range(NTT)]
        h_prev = None
        for i, (s, tt) in enumerate(tiles):
            def dma_in(e, s=s, tt=tt, i=i, h_prev=h_prev):
                if h_prev is not None:
                    e.wait_ge(h_prev[0], h_prev[1])
                e.dma_start(xin[:, :, :], dtile(src_x, s, tt * TT, (tt + 1) * TT)
                            ).then_inc(ph.sem("ld"), 16)
            em.on("sync", dma_in)
            n_ld = ph.bump("ld", 16)

            def xin_wait(eng, n_ld=n_ld):
                eng.wait_ge(ph.sem("ld"), n_ld)

            def rt_wait(eng, i=i, h_prev=h_prev, c_st=ph.val("st")):
                if i >= 1:
                    eng.wait_ge(ph.sem("st"), c_st)
                if h_prev is not None:
                    eng.wait_ge(h_prev[0], h_prev[1])

            h1 = ln.emit(em, xin[:, :, :], r2b[:, :, :], store_bf=None,
                         xin_wait=xin_wait, rt_ready_wait=rt_wait)

            def dma_r2(e, s=s, tt=tt, h1=h1):
                e.wait_ge(h1[0], h1[1])
                e.dma_start(dtile(s_ffn_in, s, tt * TT, (tt + 1) * TT), r2b[:, :, :]
                            ).then_inc(ph.sem("st"), 16)
            em.on("sync", dma_r2)
            n_st = ph.bump("st", 16)

            def r2_wait(eng, n_st=n_st):
                eng.wait_ge(ph.sem("st"), n_st)

            def store_bf(c, s=s, tt=tt):
                return xbfA_ap(s, c, tt * TT, (tt + 1) * TT)

            def xin2_wait(eng, h1=h1):
                eng.wait_ge(h1[0], h1[1])

            h2 = ln.emit(em, r2b[:, :, :], r2b[:, :, :], store_bf=store_bf,
                         gain=sb_ln1g, bias=sb_ln1b, xin_wait=xin2_wait,
                         rt_ready_wait=r2_wait)
            h_prev = h2
        def fin(e, h_prev=h_prev):
            e.wait_ge(h_prev[0], h_prev[1])
        em.on("sync", fin)
        em.replay(nc)
        pes.close()

    def phase_f_att(tag):
        pes = ExitStack()
        Sl = lambda n, sh, dt: pes.enter_context(nc.sbuf_tensor(n, sh, dt))
        Pl = lambda n, sh: pes.enter_context(nc.psum_tensor(n, sh, F32))
        ph = Sems(nc, es, tag)
        em = Emitter()
        wq_p = [Sl(f"{tag}_wq{j}", [128, DC, 128], BF16) for j in range(2)]
        wk_p = [Sl(f"{tag}_wk{j}", [128, DC, 128], BF16) for j in range(2)]
        wv_p = [Sl(f"{tag}_wv{j}", [128, DC, 128], BF16) for j in range(2)]
        qTp = Sl(f"{tag}_qT", [128, T], BF16)
        kTp = Sl(f"{tag}_kT", [128, T], BF16)
        vN = Sl(f"{tag}_vN", [128, 16, 128], BF16)
        expT = [Sl(f"{tag}_ex{j}", [128, 512], BF16) for j in range(2)]
        zrec = Sl(f"{tag}_zr", [1, 512], F32)
        zb_sb = Sl(f"{tag}_zb", [128, 512], F32)
        ot_f = Sl(f"{tag}_otf", [128, 512], F32)
        ps_p = [Pl(f"{tag}_pp{j}", [128, 512]) for j in range(2)]
        ps_oT = [Pl(f"{tag}_po{j}", [128, 512]) for j in range(2)]
        ps_z = Pl(f"{tag}_pz", [1, 512])
        ps_zb = Pl(f"{tag}_pzb", [128, 512])

        base_wf = ph.val("wfree")
        for hp in range(_F_ATT_HP):
            j = hp % 2

            def wload(e, hp=hp, j=j, base_wf=base_wf):
                if hp >= 2:
                    e.wait_ge(ph.sem("wfree"), base_wf + hp - 1)
                for W, dst in ((Wq, wq_p[j]), (Wk, wk_p[j]), (Wv, wv_p[j])):
                    e.dma_start(dst[:, :, :],
                                W[:, 128 * hp:128 * hp + 128].rearrange("(c p) o -> p c o", p=128)
                                ).then_inc(ph.sem("wld"), 16)
            em.on("gpsimd", wload)
            n_wld = ph.bump("wld", 48)

            for s in range(NSEQ):
                for which, wbuf, bias_sb, dst in ((
                        (0, wq_p[j], sb_bqF, qTp), (1, wk_p[j], sb_bkF, kTp))
                        if _F_ATT_PART >= 1 else ()):
                    for tt in range(NTT):
                        b = tt % 2
                        c_ev = ph.val("evp")
                        c_cons = ph.val("mst")
                        c_expg = ph.val("exp")

                        def t_p(t, s=s, tt=tt, b=b, wbuf=wbuf, c_ev=c_ev, n_wld=n_wld,
                                c_expg=c_expg):
                            t.wait_ge(ph.sem("wld"), n_wld)
                            t.wait_ge(ph.sem("exp"), c_expg)
                            if c_ev >= 2:
                                t.wait_ge(ph.sem("evp"), c_ev - 1)
                            for c in range(DC):
                                k = t.matmul(ps_p[b][:, :], wbuf[:, c, :],
                                             xbfA_ap(s, c, tt * TT, (tt + 1) * TT),
                                             start=(c == 0), stop=(c == DC - 1))
                            k.then_inc(ph.sem("mp"), 1)
                        em.on("tensor", t_p)
                        n_mp = ph.bump("mp")

                        def s_evp(sc, tt=tt, b=b, hp=hp, bias_sb=bias_sb, dst=dst,
                                  n_mp=n_mp, c_cons=c_cons):
                            sc.wait_ge(ph.sem("mp"), n_mp)
                            sc.wait_ge(ph.sem("mst"), c_cons)
                            sc.activation(dst[:, tt * TT:(tt + 1) * TT], ps_p[b][:, :],
                                          AF.Identity, bias=bias_sb[:, hp:hp + 1]
                                          ).then_inc(ph.sem("evp"), 1)
                        em.on("scalar", s_evp)
                        ph.bump("evp")

                n_evp_done = ph.val("evp")
                for nt in range(16 if _F_ATT_PART >= 2 else 0):
                    b = nt % 2
                    c_ev = ph.val("evv")
                    c_cons = ph.val("mot")

                    def t_v(t, s=s, nt=nt, b=b, j=j, c_ev=c_ev, n_wld=n_wld,
                            n_evp_done=n_evp_done):
                        t.wait_ge(ph.sem("wld"), n_wld)
                        t.wait_ge(ph.sem("evp"), n_evp_done)
                        if c_ev >= 2:
                            t.wait_ge(ph.sem("evv"), c_ev - 1)
                        for c in range(DC):
                            k = t.matmul(ps_p[b][:, 0:128], xbfA_ap(s, c, nt * 128, (nt + 1) * 128),
                                         wv_p[j][:, c, :], start=(c == 0), stop=(c == DC - 1))
                        k.then_inc(ph.sem("mp"), 1)
                    em.on("tensor", t_v)
                    n_mp = ph.bump("mp")

                    def v_evv(v, nt=nt, b=b, n_mp=n_mp, c_cons=c_cons):
                        v.wait_ge(ph.sem("mp"), n_mp)
                        v.wait_ge(ph.sem("mot"), c_cons)
                        v.tensor_copy(vN[:, nt, :], ps_p[b][:, 0:128]).then_inc(ph.sem("evv"), 1)
                    em.on("vector", v_evv)
                    ph.bump("evv")
                n_proj = (ph.val("evp"), ph.val("evv"))

                if s == NSEQ - 1:
                    def t_wfree(t):
                        t.nop().then_inc(ph.sem("wfree"), 1)
                    em.on("tensor", t_wfree)
                    ph.bump("wfree")

                for hh in range(2 if _F_ATT_PART >= 3 else 0):
                    h = 2 * hp + hh
                    b0 = hh * 64
                    for qc in range(NTT):
                        bo = qc % 2
                        c_fin = ph.val("fin")
                        c_inv0 = ph.val("inv")

                        for kt in range(4 * qc + 4):
                            b = kt % 2
                            c_exp = ph.val("exp")

                            def t_st(t, s=s, hh=hh, qc=qc, kt=kt, b=b, b0=b0,
                                     n_proj=n_proj, c_exp=c_exp):
                                t.wait_ge(ph.sem("evp"), n_proj[0])
                                t.wait_ge(ph.sem("evv"), n_proj[1])
                                if c_exp >= 2:
                                    t.wait_ge(ph.sem("exp"), c_exp - 1)
                                t.matmul(ps_p[b][:, :], kTp[b0:b0 + 64, kt * 128:(kt + 1) * 128],
                                         qTp[b0:b0 + 64, qc * TT:(qc + 1) * TT],
                                         start=True, stop=True).then_inc(ph.sem("mst"), 1)
                            em.on("tensor", t_st)
                            n_mst = ph.bump("mst")

                            def s_exp(sc, b=b, kt=kt, n_mst=n_mst,
                                      c_mot=max(0, ph.val("mot") - 1)):
                                sc.wait_ge(ph.sem("mst"), n_mst)
                                sc.wait_ge(ph.sem("mot"), c_mot)
                                sc.activation(expT[b][:, :], ps_p[b][:, :], AF.Exp,
                                              scale=0.125).then_inc(ph.sem("exp"), 1)
                            em.on("scalar", s_exp)
                            n_exp = ph.bump("exp")

                            if _F_ATT_PART < 4:
                                continue
                            need_mask = (kt >= 4 * qc)
                            if need_mask:
                                def v_mask(v, b=b, m=kt - 4 * qc, n_exp=n_exp):
                                    v.wait_ge(ph.sem("exp"), n_exp)
                                    v.tensor_mul(expT[b][:, :], expT[b][:, :],
                                                 sb_m512[m][:, :]).then_inc(ph.sem("msk"), 1)
                                em.on("vector", v_mask)
                                n_gate = ("msk", ph.bump("msk"))
                            else:
                                n_gate = ("exp", n_exp)

                            def t_ot(t, s=s, kt=kt, qc=qc, b=b, b0=b0, bo=bo, hh=hh,
                                     n_gate=n_gate, c_fin=c_fin, c_inv0=c_inv0):
                                t.wait_ge(ph.sem(n_gate[0]), n_gate[1])
                                if kt == 0 and c_fin >= 2:
                                    t.wait_ge(ph.sem("fin"), c_fin - 1)
                                if kt == 0 and c_inv0 >= 1:
                                    t.wait_ge(ph.sem("inv"), c_inv0)
                                t.matmul(ps_oT[bo][b0:b0 + 64, :], vN[:, kt, b0:b0 + 64],
                                         expT[b][:, :], start=(kt == 0), stop=(kt == 4 * qc + 3),
                                         tile_position=(0, b0))
                                t.matmul(ps_z[0:1, :], ones_col_b[:, 0:1], expT[b][:, :],
                                         start=(kt == 0), stop=(kt == 4 * qc + 3)
                                         ).then_inc(ph.sem("mot"), 1)
                            em.on("tensor", t_ot)
                            ph.bump("mot")
                        n_mot = ph.val("mot")

                        if _F_ATT_PART < 4:
                            continue

                        def v_inv(v, n_mot=n_mot):
                            v.wait_ge(ph.sem("mot"), n_mot)
                            v.reciprocal(zrec[0:1, :], ps_z[0:1, :]).then_inc(ph.sem("inv"), 1)
                        em.on("vector", v_inv)
                        n_inv = ph.bump("inv")

                        def t_zb(t, b0=b0, n_inv=n_inv, c_fin=c_fin):
                            t.wait_ge(ph.sem("inv"), n_inv)
                            if c_fin >= 1:
                                t.wait_ge(ph.sem("fin"), c_fin)
                            t.matmul(ps_zb[b0:b0 + 64, :], ones_row_f[0:1, 0:64], zrec[0:1, :],
                                     start=True, stop=True, tile_position=(0, b0)
                                     ).then_inc(ph.sem("mzb"), 1)
                        em.on("tensor", t_zb)
                        n_mzb = ph.bump("mzb")

                        def v_fin(v, h=h, s=s, qc=qc, bo=bo, b0=b0, hp=hp, n_mzb=n_mzb):
                            v.wait_ge(ph.sem("mzb"), n_mzb)
                            v.tensor_copy(zb_sb[b0:b0 + 64, :], ps_zb[b0:b0 + 64, :])
                            v.tensor_mul(ot_f[b0:b0 + 64, :], ps_oT[bo][b0:b0 + 64, :],
                                         zb_sb[b0:b0 + 64, :])
                            c = h // 2
                            dst = XB[1][b0:b0 + 64,
                                       (s * DC + c) * T + qc * TT:(s * DC + c) * T + (qc + 1) * TT]
                            v.tensor_scalar_add(dst, ot_f[b0:b0 + 64, :],
                                                sb_bvF[b0:b0 + 64, hp:hp + 1]
                                                ).then_inc(ph.sem("fin"), 1)
                        em.on("vector", v_fin)
                        ph.bump("fin")

        em.replay(nc)
        pes.close()

    def phase_f_out(tag, wnat_sb):
        pes = ExitStack()
        Sl = lambda n, sh, dt: pes.enter_context(nc.sbuf_tensor(n, sh, dt))
        Pl = lambda n, sh: pes.enter_context(nc.psum_tensor(n, sh, F32))
        ph = Sems(nc, es, tag)
        em = Emitter()
        lnO = LNHelper(tag + "l", pes)
        wo_sb = Sl(f"{tag}_wo", [128, DC, D], BF16)
        u1 = Sl(f"{tag}_u1", [128, DC, TT], F32)
        ps_u = [Pl(f"{tag}_ps{jj}", [128, 512]) for jj in range(2)]
        ps_lg = Pl(f"{tag}_plg", [128, 8])
        lg = Sl(f"{tag}_lg", [128, 8], F32)
        lgt = Sl(f"{tag}_lgt", [128, 8], F32)
        lgu = Sl(f"{tag}_lgu", [128, 8], F32)
        lgs = Sl(f"{tag}_lgs", [128, 6], F32)

        def wload(e):
            e.dma_start(wo_sb[:, :, :], Wo.rearrange("(c p) o -> p c o", p=128)
                        ).then_inc(ph.sem("wld"), 16)
        em.on("gpsimd", wload)
        n_wld_o = ph.bump("wld", 16)

        tiles = [(s, tt) for s in range(NSEQ) for tt in range(NTT)]
        done_handles = []
        for i, (s, tt) in enumerate(tiles):
            for o in range(DC):
                b = o % 2
                c_ev = ph.val("ev")

                def t_mm(t, s=s, tt=tt, o=o, b=b, c_ev=c_ev, n_wld_o=n_wld_o):
                    t.wait_ge(ph.sem("wld"), n_wld_o)
                    if c_ev >= 2:
                        t.wait_ge(ph.sem("ev"), c_ev - 1)
                    for c in range(DC):
                        k = t.matmul(ps_u[b][:, :], wo_sb[:, c, o * 128:(o + 1) * 128],
                                     xbfB_ap(s, c, tt * TT, (tt + 1) * TT),
                                     start=(c == 0), stop=(c == DC - 1))
                    k.then_inc(ph.sem("mm"), 1)
                em.on("tensor", t_mm)
                n_mm = ph.bump("mm")

                def s_ev(sc, o=o, b=b, n_mm=n_mm, i=i, done_handles=done_handles,
                         c_sta=ph.val("sta")):
                    sc.wait_ge(ph.sem("mm"), n_mm)
                    if o == 0 and i >= 1:
                        h0 = done_handles[i - 1]
                        sc.wait_ge(h0[0], h0[1])
                        sc.wait_ge(ph.sem("sta"), c_sta)
                    sc.activation(u1[:, o, :], ps_u[b][:, :], AF.Identity,
                                  bias=sb_boF[:, o:o + 1]).then_inc(ph.sem("ev"), 1)
                em.on("scalar", s_ev)
                ph.bump("ev")
            n_ev = ph.val("ev")

            def dma_racc(e, s=s, tt=tt, n_ev=n_ev):
                e.wait_ge(ph.sem("ev"), n_ev)
                e.dma_start(u1[:, :, :], dtile(s_ffn_in, s, tt * TT, (tt + 1) * TT),
                            accum_op=ALU.add).then_inc(ph.sem("acc"), 16)
            em.on("gpsimd", dma_racc)
            n_acc = ph.bump("acc", 16)

            def dma_sta(e, s=s, tt=tt, n_acc=n_acc):
                e.wait_ge(ph.sem("acc"), n_acc)
                e.dma_start(dtile(s_a, s, tt * TT, (tt + 1) * TT), u1[:, :, :]
                            ).then_inc(ph.sem("sta"), 16)
            em.on("sync", dma_sta)
            n_sta = ph.bump("sta", 16)

            def xin_wait(eng, n_acc=n_acc):
                eng.wait_ge(ph.sem("acc"), n_acc)

            def rt_wait(eng, n_sta=n_sta):
                eng.wait_ge(ph.sem("sta"), n_sta)

            def store_bf(c, s=s, tt=tt):
                return xbfA_ap(s, c, tt * TT, (tt + 1) * TT)

            hdl = lnO.emit(em, u1[:, :, :], u1[:, :, :], store_bf=store_bf,
                           gain=sb_ln2g, bias=sb_ln2b, xin_wait=xin_wait,
                           rt_ready_wait=rt_wait)
            done_handles.append(hdl)

            for st4 in range(4):
                gi = (s * NTT + tt) * 4 + st4
                t0 = tt * TT + st4 * 128

                def t_lg(t, s=s, t0=t0, hdl=hdl, c_rtv=ph.val("rtv")):
                    t.wait_ge(hdl[0], hdl[1])
                    if c_rtv >= 1:
                        t.wait_ge(ph.sem("rtv"), c_rtv)
                    for c in range(DC):
                        t.matmul(ps_lg[:, :], xbfA_ap(s, c, t0, t0 + 128),
                                 sb_Wr[:, c, :], start=(c == 0), stop=False)
                    k = t.matmul(ps_lg[:, :], ones_row_b[0:1, :], sb_br_b[0:1, :],
                                 start=False, stop=True)
                    k.then_inc(ph.sem("mlg"), 1)
                em.on("tensor", t_lg)
                n_mlg = ph.bump("mlg")

                def v_r1(v, n_mlg=n_mlg, c_r2=ph.val("r2"), c_sg=ph.val("sg")):
                    v.wait_ge(ph.sem("mlg"), n_mlg)
                    v.wait_ge(ph.sem("sg"), c_sg)
                    v.wait_ge(ph.sem("r2"), c_r2)
                    v.tensor_copy(lg[:, :], ps_lg[:, :])
                    v.reduce_max(lgs[:, 0:1], ps_lg[:, :], axis=AX.X).then_inc(ph.sem("r1"), 1)
                em.on("vector", v_r1)
                n_r1 = ph.bump("r1")

                def g_r2(gp, n_r1=n_r1):
                    gp.wait_ge(ph.sem("r1"), n_r1)
                    gp.tensor_scalar(lgt[:, :], lg[:, :], lgs[:, 0:1], None, ALU.is_ge
                                     ).then_inc(ph.sem("r2"), 1)
                em.on("gpsimd", g_r2)
                n_r2 = ph.bump("r2")

                def v_r2b(v, n_r2=n_r2):
                    v.wait_ge(ph.sem("r2"), n_r2)
                    v.scalar_tensor_tensor(lgu[:, :], lgt[:, :], -1e30, lg[:, :],
                                           ALU.mult, ALU.add).then_inc(ph.sem("r2b"), 1)
                em.on("vector", v_r2b)
                n_r2b = ph.bump("r2b")

                def v_r3(v, n_r2b=n_r2b):
                    v.wait_ge(ph.sem("r2b"), n_r2b)
                    v.reduce_sum(lgs[:, 2:3], lgt[:, :], axis=AX.X)
                    v.reduce_max(lgs[:, 1:2], lgu[:, :], axis=AX.X).then_inc(ph.sem("r3"), 1)
                em.on("vector", v_r3)
                n_r3 = ph.bump("r3")

                def g_r4(gp, n_r3=n_r3):
                    gp.wait_ge(ph.sem("r3"), n_r3)
                    gp.tensor_scalar(lgs[:, 3:4], lgs[:, 2:3], 2.0, None, ALU.is_ge)
                    gp.tensor_sub(lgs[:, 4:5], lgs[:, 0:1], lgs[:, 1:2]
                                  ).then_inc(ph.sem("r4"), 1)
                em.on("gpsimd", g_r4)
                n_r4 = ph.bump("r4")

                def v_r5(v, n_r4=n_r4):
                    v.wait_ge(ph.sem("r4"), n_r4)
                    v.scalar_tensor_tensor(lgs[:, 5:6], lgs[:, 3:4], lgs[:, 4:5],
                                           lgs[:, 1:2], ALU.mult, ALU.add).then_inc(ph.sem("r5"), 1)
                em.on("vector", v_r5)
                n_r5 = ph.bump("r5")

                def g_r6(gp, n_r5=n_r5):
                    gp.wait_ge(ph.sem("r5"), n_r5)
                    gp.tensor_scalar(lgt[:, :], lg[:, :], lgs[:, 5:6], None, ALU.is_ge)
                    gp.tensor_single_scalar(lgs[:, 0:1], lgs[:, 0:1], -1.0, ALU.mult
                                            ).then_inc(ph.sem("r6"), 1)
                em.on("gpsimd", g_r6)
                n_r6 = ph.bump("r6")

                def s_exp(sc, n_r6=n_r6):
                    sc.wait_ge(ph.sem("r6"), n_r6)
                    sc.activation(lg[:, :], lg[:, :], AF.Exp, bias=lgs[:, 0:1]
                                  ).then_inc(ph.sem("sg"), 1)
                em.on("scalar", s_exp)
                n_sg = ph.bump("sg")

                def v_r7(v, n_sg=n_sg):
                    v.wait_ge(ph.sem("sg"), n_sg)
                    v.tensor_mul(lg[:, :], lg[:, :], lgt[:, :])
                    v.tensor_copy(lgu[:, 1:2], lgs[:, 0:1])
                    v.reduce_sum(lgs[:, 4:5], lg[:, :], axis=AX.X)
                    v.tensor_copy(lgu[:, 2:3], lgs[:, 0:1])
                    v.reciprocal(lgu[:, 0:1], lgs[:, 4:5]).then_inc(ph.sem("r8"), 1)
                em.on("vector", v_r7)
                n_r8 = ph.bump("r8")

                def g_r9(gp, gi=gi, n_r8=n_r8):
                    gp.wait_ge(ph.sem("r8"), n_r8)
                    gp.tensor_single_scalar(wnat_sb[:, gi, :], lg[:, :], lgu[:, 0:1],
                                            ALU.mult).then_inc(ph.sem("rtv"), 1)
                em.on("gpsimd", g_r9)
                ph.bump("rtv")

        def fin(e):
            e.wait_ge(ph.sem("sta"), ph.val("sta"))
        em.on("sync", fin)

        def fin_v(v):
            v.wait_ge(ph.sem("rtv"), ph.val("rtv"))
        em.on("vector", fin_v)
        em.replay(nc)
        pes.close()

    def dump_xbfB(dst):
        ph3 = Sems(nc, es, "dmpB")
        em3 = Emitter()
        def cv(v):
            pass
        base_n3 = ph3.val("st")

        def go(e, base_n3=base_n3):
            n = base_n3
            for s in range(NSEQ):
                for c in range(DC):
                    e.dma_start(dst[s, c, :, :], XB[1][:, (s * DC + c) * T:(s * DC + c + 1) * T]
                                ).then_inc(ph3.sem("st"), 16)
                    n += 16
            e.wait_ge(ph3.sem("st"), n)
        em3.on("gpsimd", go)
        for _ in range(NSEQ * DC):
            ph3.bump("st", 16)
        em3.replay(nc)


    def phase_f_moe(tag, wnat_sb, src_xprev, dst_x):
        pes = ExitStack()
        Sl = lambda n, sh, dt: pes.enter_context(nc.sbuf_tensor(n, sh, dt))
        Pl = lambda n, sh: pes.enter_context(nc.psum_tensor(n, sh, F32))
        ph = Sems(nc, es, tag)
        em = Emitter()
        moe = Sl(f"{tag}_moe", [128, DC, 1024], F32)
        w1b = [Sl(f"{tag}_w1{j}", [128, DC, 512], BF16) for j in range(2)]
        w2b = [Sl(f"{tag}_w2{j}", [128, DC, 256], BF16) for j in range(2)]
        w_eb = Sl(f"{tag}_web", [128, 1024], F32)
        w_row = Sl(f"{tag}_wrow", [1, 1024], F32)
        eo_f = Sl(f"{tag}_eof", [128, 1024], F32)
        ps_mid = [Pl(f"{tag}_pm{j}", [128, 1024]) for j in range(2)]
        ps_w = Pl(f"{tag}_pw", [128, 1024])

        def midv(f):
            return XB[1][:, f * 1024:(f + 1) * 1024]

        def gate0(t):
            t.wait_ge(gsem.sem("mcast"), gsem.val("mcast"))
        em.on("tensor", gate0)

        for g in range(4):
            s, half = g // 2, g % 2
            t0 = half * 1024

            for e in range(8):
                for pc in range(8):
                    j = (e * 8 + pc) % 2
                    c_mm = ph.val("mmid")

                    def dma_w1(eng, e=e, pc=pc, j=j, c_mm=c_mm):
                        if c_mm >= 8:
                            eng.wait_ge(ph.sem("mmid"), c_mm - 4)
                        eng.dma_start(w1b[j][:, :, :],
                                      ew1b[e, :, pc * 512:(pc + 1) * 512]
                                      .rearrange("(c p) f -> p c f", p=128)
                                      ).then_inc(ph.sem("lw1"), 16)
                    em.on("sync", dma_w1)
                    n_lw1 = ph.bump("lw1", 16)

                    for fi in range(4):
                        f = pc * 4 + fi
                        b = f % 2
                        c_ev = ph.val("evm")

                        def t_mid(t, s=s, t0=t0, f=f, fi=fi, b=b, j=j, c_ev=c_ev,
                                  n_lw1=n_lw1, c_evo_rel=ph.val("evo")):
                            t.wait_ge(ph.sem("lw1"), n_lw1)
                            t.wait_ge(ph.sem("evo"), c_evo_rel)
                            if c_ev >= 2:
                                t.wait_ge(ph.sem("evm"), c_ev - 1)
                            for half2 in range(2):
                                for c in range(DC):
                                    k = t.matmul(
                                        ps_mid[b][:, half2 * 512:(half2 + 1) * 512],
                                        w1b[j][:, c, fi * 128:(fi + 1) * 128],
                                        xbfA_ap(s, c, t0 + half2 * 512, t0 + (half2 + 1) * 512),
                                        start=(c == 0), stop=(c == DC - 1))
                            k.then_inc(ph.sem("mmid"), 1)
                        em.on("tensor", t_mid)
                        n_mm = ph.bump("mmid")

                        def s_gelu(sc, e=e, f=f, b=b, n_mm=n_mm, c_eo=ph.val("meo")):
                            sc.wait_ge(ph.sem("mmid"), n_mm)
                            sc.wait_ge(ph.sem("meo"), c_eo)
                            sc.activation(midv(f), ps_mid[b][:, :], AF.Gelu,
                                          bias=sb_eb1[:, e * 32 + f:e * 32 + f + 1]
                                          ).then_inc(ph.sem("evm"), 1)
                        em.on("scalar", s_gelu)
                        ph.bump("evm")
                n_evm = ph.val("evm")

                c_gv = ph.val("gv")

                def t_gate(t, s=s, half=half, e=e, c_gv=c_gv):
                    if c_gv >= 1:
                        t.wait_ge(ph.sem("gv"), c_gv)
                    for st4 in range(8):
                        gi = s * 16 + half * 8 + st4
                        k = t.transpose(ps_w[0:1, st4 * 128:st4 * 128 + 128],
                                        wnat_sb[:, gi, e:e + 1], sb_identF[:, :])
                    k.then_inc(ph.sem("gt"), 1)
                em.on("tensor", t_gate)
                n_gt = ph.bump("gt")

                def v_grow(v, n_gt=n_gt):
                    v.wait_ge(ph.sem("gt"), n_gt)
                    v.tensor_copy(w_row[0:1, :], ps_w[0:1, :]).then_inc(ph.sem("gr"), 1)
                em.on("vector", v_grow)
                n_gr = ph.bump("gr")

                def t_gbc(t, n_gr=n_gr):
                    t.wait_ge(ph.sem("gr"), n_gr)
                    t.matmul(ps_w[:, 0:512], ones_row_f[0:1, :], w_row[0:1, 0:512],
                             start=True, stop=True)
                    t.matmul(ps_w[:, 512:1024], ones_row_f[0:1, :], w_row[0:1, 512:1024],
                             start=True, stop=True).then_inc(ph.sem("gb"), 1)
                em.on("tensor", t_gbc)
                n_gb = ph.bump("gb")

                def v_geb(v, n_gb=n_gb, c_mo=ph.val("mo")):
                    v.wait_ge(ph.sem("gb"), n_gb)
                    v.wait_ge(ph.sem("mo"), c_mo)
                    v.tensor_copy(w_eb[:, :], ps_w[:, :]).then_inc(ph.sem("gv"), 1)
                em.on("vector", v_geb)
                n_gv = ph.bump("gv")

                for oq in range(4):
                    for rc in range(4):
                        j2 = (oq * 4 + rc) % 2
                        c_meo = ph.val("meo")

                        def dma_w2(eng, e=e, oq=oq, rc=rc, j2=j2, c_meo=c_meo):
                            if c_meo >= 4:
                                eng.wait_ge(ph.sem("meo"), c_meo - 2)
                            eng.dma_start(w2b[j2][:, :, :],
                                          ew2b[e, rc * 1024:(rc + 1) * 1024,
                                               oq * 256:(oq + 1) * 256]
                                          .rearrange("(c p) o -> p c o", p=128)
                                          ).then_inc(ph.sem("lw2"), 16)
                        em.on("sync", dma_w2)
                        n_lw2 = ph.bump("lw2", 16)

                        for oi in range(2):
                            def t_eo(t, oi=oi, rc=rc, j2=j2, n_lw2=n_lw2, n_evm=n_evm,
                                     c_evo=ph.val("evo"), c_gelu_done=n_evm):
                                t.wait_ge(ph.sem("lw2"), n_lw2)
                                t.wait_ge(ph.sem("evm"), n_evm)
                                if rc == 0 and c_evo >= 2:
                                    t.wait_ge(ph.sem("evo"), c_evo)
                                for fc in range(DC):
                                    f = rc * 8 + fc
                                    for half2 in range(2):
                                        k = t.matmul(
                                            ps_mid[oi][:, half2 * 512:(half2 + 1) * 512],
                                            w2b[j2][:, fc, oi * 128:(oi + 1) * 128],
                                            midv(f)[:, half2 * 512:(half2 + 1) * 512],
                                            start=(f == 0), stop=(f == 31))
                                k.then_inc(ph.sem("meo"), 1)
                            em.on("tensor", t_eo)
                            ph.bump("meo")
                    n_meo = ph.val("meo")

                    for oi in range(2):
                        ot = oq * 2 + oi

                        def s_evo(sc, e=e, ot=ot, oi=oi, n_meo=n_meo,
                                  c_mo=ph.val("mo")):
                            sc.wait_ge(ph.sem("meo"), n_meo)
                            sc.wait_ge(ph.sem("mo"), c_mo)
                            sc.activation(eo_f[:, :], ps_mid[oi][:, :], AF.Identity,
                                          bias=sb_eb2[:, e * 8 + ot:e * 8 + ot + 1]
                                          ).then_inc(ph.sem("evo"), 1)
                        em.on("scalar", s_evo)
                        n_evo = ph.bump("evo")

                        def v_moe(v, e=e, ot=ot, n_evo=n_evo, n_gv=n_gv, g=g,
                                  c_stm=ph.val("stx")):
                            v.wait_ge(ph.sem("evo"), n_evo)
                            v.wait_ge(ph.sem("gv"), n_gv)
                            if e == 0 and c_stm:
                                v.wait_ge(ph.sem("stx"), c_stm)
                            if e == 0:
                                v.tensor_mul(moe[:, ot, :], eo_f[:, :], w_eb[:, :]
                                             ).then_inc(ph.sem("mo"), 1)
                            else:
                                v.tensor_mul(eo_f[:, :], eo_f[:, :], w_eb[:, :])
                                v.tensor_add(moe[:, ot, :], moe[:, ot, :], eo_f[:, :]
                                             ).then_inc(ph.sem("mo"), 1)
                        em.on("vector", v_moe)
                        ph.bump("mo")
            n_mo_g = ph.val("mo")

            def dma_acc_a(eng, s=s, t0=t0, n_mo_g=n_mo_g):
                eng.wait_ge(ph.sem("mo"), n_mo_g)
                eng.dma_start(moe[:, :, :], dtile(s_a, s, t0, t0 + 1024),
                              accum_op=ALU.add).then_inc(ph.sem("ac1"), 16)
            em.on("gpsimd", dma_acc_a)
            n_ac1 = ph.bump("ac1", 16)

            def dma_acc_x(eng, s=s, t0=t0, n_ac1=n_ac1):
                eng.wait_ge(ph.sem("ac1"), n_ac1)
                eng.dma_start(moe[:, :, :], dtile(src_xprev, s, t0, t0 + 1024),
                              accum_op=ALU.add).then_inc(ph.sem("ac2"), 16)
            em.on("gpsimd", dma_acc_x)
            n_ac2 = ph.bump("ac2", 16)

            def dma_stx(eng, s=s, t0=t0, n_ac2=n_ac2):
                eng.wait_ge(ph.sem("ac2"), n_ac2)
                eng.dma_start(dtile(dst_x, s, t0, t0 + 1024), moe[:, :, :]
                              ).then_inc(ph.sem("stx"), 16)
            em.on("sync", dma_stx)
            ph.bump("stx", 16)

        def fin(e):
            e.wait_ge(ph.sem("stx"), ph.val("stx"))
        em.on("sync", fin)
        em.replay(nc)
        pes.close()
    wnatF = S("wnatF", [128, NSEQ * 16, 8], F32)
    if not _SKIP_EARLY:
        openA("a3")
        phase_f_ln("f1l", s_xA)
    if not _SKIP_EARLY and _PHASES >= 2:
        openB("b3")
        phase_f_att("f1a")
        if _DEBUG_TAP == "xbfB":
            dump_xbfB(s_xC)
    if not _SKIP_EARLY and _PHASES >= 3:
        phase_f_out("f1o", wnatF)
        if _DEBUG_TAP == "wnat":
            ph4 = Sems(nc, es, "dmpw")
            em4 = Emitter()
            base_n4 = ph4.val("st")

            def gow(e, base_n4=base_n4):
                e.dma_start(s_xC[0, 0, :, 0:256], wnatF[:, :, :].rearrange("p g e -> p (g e)")
                            ).then_inc(ph4.sem("st"), 16)
                e.wait_ge(ph4.sem("st"), base_n4 + 16)
            em4.on("sync", gow)
            ph4.bump("st", 16)
            em4.replay(nc)
    if not _SKIP_EARLY and _PHASES >= 4:
        phase_f_moe("f1m", wnatF, s_xA, s_xB)
        closeA(); closeB()
    if _PHASES >= 5:
        openA("a4")
        ln_phase("bln", s_xB, None, "XA")
        _build_b = True
    else:
        _build_b = False

    def phase_b_main():
        pes = ExitStack()
        Sl = lambda n, sh, dt: pes.enter_context(nc.sbuf_tensor(n, sh, dt))
        Pl = lambda n, sh, dt=F32: pes.enter_context(nc.psum_tensor(n, sh, dt))
        ph = Sems(nc, es, "bm")
        em = Emitter()
        NH = 8
        BL = 16
        wqkv = [Sl(f"bm_w{j}", [128, DC, 128], BF16) for j in range(6)]
        bwm_sb = Sl("bm_bwm", [128, DC, 16], BF16)
        f32w = Sl("bm_f32w", [128, T], F32)
        vf32 = Sl("bm_vf32", [128, T], F32)
        xp = Sl("bm_xp", [128, T + 3], F32)
        cvw = Sl("bm_cvw", [128, T], F32)
        b16s = Sl("bm_b16s", [128, T], BF16)
        vb = Sl("bm_vb", [128, T], BF16)
        qn = Sl("bm_qn", [128, T], BF16)
        kn = Sl("bm_kn", [128, T], BF16)
        kbfm = Sl("bm_kbfm", [128, T], BF16)
        vkb = Sl("bm_vkb", [128, BL, 256], BF16)
        ktm = Sl("bm_ktm", [128, BL, 128], BF16)
        utm = Sl("bm_utm", [128, BL, 128], BF16)
        wTf = Sl("bm_wT", [128, BL, 128], BF16)
        aiT = Sl("bm_aiT", [128, BL, 128], BF16)
        odel = Sl("bm_odel", [128, T], F32)
        gbb = Sl("bm_gb", [128, T], F32)
        bcb = Sl("bm_bcb", [128, T], F32)
        bmn = Sl("bm_bmn", [128, BL, 16], F32)
        bsg = Sl("bm_bsg", [128, BL, 8], F32)
        rowA = Sl("bm_rowA", [1, T], F32)
        rowB = Sl("bm_rowB", [1, T], F32)
        Ssb = Sl("bm_S", [128, 128], BF16)
        Sf = Sl("bm_Sf", [128, 128], F32)
        tsb = Sl("bm_tsb", [128, 256], BF16)
        uis = Sl("bm_uis", [128, 128], BF16)
        Xp_ = [Sl(f"bm_X{j}", [128, 128], BF16) for j in range(2)]
        Yp_ = [Sl(f"bm_Y{j}", [128, 128], BF16) for j in range(6)]
        wtmp = Sl("bm_wtmp", [128, 128], BF16)
        psA = Pl("bm_psA", [128, 512])
        psB = Pl("bm_psB", [128, 512])
        psC = Pl("bm_psC", [128, 256])
        psD = Pl("bm_psD", [128, 128])
        psE = Pl("bm_psE", [128, 128], BF16)
        psF = Pl("bm_psF", [128, 128])
        psG = Pl("bm_psG", [1, 512])

        def pg(name, allow=0):
            return (name, ph.val(name) - allow)

        def pg_wait(t, w):
            if w[1] > 0:
                t.wait_ge(ph.sem(w[0]), w[1])

        def init(v):
            v.memset(xp[:, 0:3], 0.0)
        em.on("vector", init)

        def bwm_load(e):
            e.dma_start(bwm_sb[:, :, :], bwmix.rearrange("(c p) o -> p c o", p=128)
                        ).then_inc(ph.sem("bl"), 16)
        em.on("gpsimd", bwm_load)
        n_bl = ph.bump("bl", 16)

        units = [(s, h) for s in range(NSEQ) for h in range(NH)]
        for ui_, (s, h) in enumerate(units):
            if _B_PART < 1 or ui_ >= _B_UNITS:
                continue
            j3 = ui_ % 2

            def wload(e, s=s, h=h, j3=j3, c_pr=ph.val("prj")):
                if c_pr >= 24:
                    e.wait_ge(ph.sem("prj"), c_pr - 12)
                for wi, W in enumerate((qwB, kwB, vwB)):
                    e.dma_start(wqkv[3 * j3 + wi][:, :, :],
                                W[:, 128 * h:128 * h + 128].rearrange("(c p) o -> p c o", p=128)
                                ).then_inc(ph.sem("wl"), 16)
            em.on("gpsimd", wload)
            n_wl = ph.bump("wl", 48)

            if h == 0:
                for bt in range(BL):
                    def t_bm(t, s=s, bt=bt, c_uie=ph.val("uie"), n_bl=n_bl,
                             w_dd=pg("dd")):
                        t.wait_ge(ph.sem("bl"), n_bl)
                        t.wait_ge(ph.sem("uie"), c_uie)
                        pg_wait(t, w_dd)
                        for c in range(DC):
                            k = t.matmul(psD[:, 0:16],
                                         xbfA_ap(s, c, bt * 128, (bt + 1) * 128),
                                         bwm_sb[:, c, :], start=(c == 0), stop=(c == DC - 1))
                        k.then_inc(ph.sem("bmm"), 1)
                    em.on("tensor", t_bm)
                    n_bmm = ph.bump("bmm")

                    def v_bme(v, bt=bt, n_bmm=n_bmm):
                        v.wait_ge(ph.sem("bmm"), n_bmm)
                        v.tensor_copy(bmn[:, bt, :], psD[:, 0:16]).then_inc(ph.sem("dd"), 1)
                    em.on("vector", v_bme)
                    ph.bump("dd")
                n_dd = ph.val("dd")

                def s_bsg(sc, n_dd=n_dd):
                    sc.wait_ge(ph.sem("dd"), n_dd)
                    sc.activation(bsg[:, :, :], bmn[:, :, 0:8], AF.Sigmoid
                                  ).then_inc(ph.sem("bsg"), 1)
                em.on("scalar", s_bsg)
                ph.bump("bsg")
            n_bsg_u = ph.val("bsg")

            def proj_conv(widx, outf32, outbf):
                for tt in range(NTT):
                    ps = psA if tt % 2 == 0 else psB

                    def t_pj(t, s=s, tt=tt, ps=ps, widx=widx, n_wl=n_wl,
                             w_ab=pg("ab", 1)):
                        t.wait_ge(ph.sem("wl"), n_wl)
                        pg_wait(t, w_ab)
                        for c in range(DC):
                            k = t.matmul(ps[:, :], wqkv[widx][:, c, :],
                                         xbfA_ap(s, c, tt * TT, (tt + 1) * TT),
                                         start=(c == 0), stop=(c == DC - 1))
                        k.then_inc(ph.sem("prj"), 1)
                    em.on("tensor", t_pj)
                    n_pj = ph.bump("prj")

                    def v_pje(v, tt=tt, ps=ps, n_pj=n_pj, c_cv=ph.val("cv4")):
                        v.wait_ge(ph.sem("prj"), n_pj)
                        v.wait_ge(ph.sem("cv4"), c_cv)
                        v.tensor_copy(xp[:, 3 + tt * TT:3 + (tt + 1) * TT], ps[:, :]
                                      ).then_inc(ph.sem("ab"), 1)
                    em.on("vector", v_pje)
                    ph.bump("ab")
                n_ab = ph.val("ab")

                cw = {0: sb_convq, 1: sb_convk, 2: sb_convv}[widx % 3]

                def v_c0(v, n_ab=n_ab, h=h, cw=cw, c_sl=ph.val("sl")):
                    v.wait_ge(ph.sem("ab"), n_ab)
                    v.wait_ge(ph.sem("sl"), c_sl)
                    v.tensor_single_scalar(cvw[:, :], xp[:, 0:T], cw[:, h, 0:1], ALU.mult
                                           ).then_inc(ph.sem("cv1"), 1)
                em.on("vector", v_c0)
                n_c1 = ph.bump("cv1")

                def v_c123(v, h=h, cw=cw, n_c1=n_c1):
                    v.wait_ge(ph.sem("cv1"), n_c1)
                    v.scalar_tensor_tensor(cvw[:, :], xp[:, 1:T + 1], cw[:, h, 1:2],
                                           cvw[:, :], ALU.mult, ALU.add)
                    v.scalar_tensor_tensor(cvw[:, :], xp[:, 2:T + 2], cw[:, h, 2:3],
                                           cvw[:, :], ALU.mult, ALU.add)
                    v.scalar_tensor_tensor(cvw[:, :], xp[:, 3:T + 3], cw[:, h, 3:4],
                                           cvw[:, :], ALU.mult, ALU.add
                                           ).then_inc(ph.sem("cv4"), 1)
                em.on("vector", v_c123)
                n_c4 = ph.bump("cv4")

                def s_silu(sc, outf32=outf32, outbf=outbf, n_c4=n_c4):
                    sc.wait_ge(ph.sem("cv4"), n_c4)
                    last = sc.activation(outf32, cvw[:, :], AF.Silu)
                    if outbf is not None:
                        last = sc.activation(outbf, cvw[:, :], AF.Silu)
                    last.then_inc(ph.sem("sl"), 1)
                em.on("scalar", s_silu)
                return ph.bump("sl")

            def emit_l2(srcf, dstb, n_sl):
                def s_sq(sc, n_sl=n_sl, srcf=srcf, c_gm=ph.val("l2m")):
                    sc.wait_ge(ph.sem("sl"), n_sl)
                    sc.wait_ge(ph.sem("l2m"), c_gm)
                    sc.activation(b16s[:, :], srcf, AF.Square).then_inc(ph.sem("l2s"), 1)
                em.on("scalar", s_sq)
                n_sq = ph.bump("l2s")

                for ct in range(NTT):
                    def t_sum(t, ct=ct, n_sq=n_sq, w_gg=pg("gg")):
                        t.wait_ge(ph.sem("l2s"), n_sq)
                        pg_wait(t, w_gg)
                        t.matmul(psG[0:1, :], ones_col_b[:, 0:1],
                                 b16s[:, ct * TT:(ct + 1) * TT], start=True, stop=True
                                 ).then_inc(ph.sem("l2m"), 1)
                    em.on("tensor", t_sum)
                    n_sm = ph.bump("l2m")

                    def v_row(v, ct=ct, n_sm=n_sm, c_rs=ph.val("l2r")):
                        v.wait_ge(ph.sem("l2m"), n_sm)
                        if ct == 0:
                            v.wait_ge(ph.sem("l2r"), c_rs)
                        v.tensor_copy(rowA[0:1, ct * TT:(ct + 1) * TT], psG[0:1, :]
                                      ).then_inc(ph.sem("gg"), 1)
                    em.on("vector", v_row)
                    ph.bump("gg")
                n_gg = ph.val("gg")

                def s_rstd(sc, n_gg=n_gg, c_bc=ph.val("l2c")):
                    sc.wait_ge(ph.sem("gg"), n_gg)
                    sc.wait_ge(ph.sem("l2c"), c_bc)
                    sc.activation(rowB[0:1, :], rowA[0:1, :], AF.Ln,
                                  bias=constf(1e-12)[0:1, 0:1])
                    sc.activation(rowB[0:1, :], rowB[0:1, :], AF.Exp, scale=-0.5
                                  ).then_inc(ph.sem("l2r"), 1)
                em.on("scalar", s_rstd)
                n_rs = ph.bump("l2r")

                for ct in range(NTT):
                    ps = psA if ct % 2 == 0 else psB

                    def t_bc(t, ct=ct, ps=ps, n_rs=n_rs, w_ab=pg("ab", 1)):
                        t.wait_ge(ph.sem("l2r"), n_rs)
                        pg_wait(t, w_ab)
                        t.matmul(ps[:, :], ones_row_f[0:1, :], rowB[0:1, ct * TT:(ct + 1) * TT],
                                 start=True, stop=True).then_inc(ph.sem("l2c"), 1)
                    em.on("tensor", t_bc)
                    n_bc = ph.bump("l2c")

                    def v_mul(v, ct=ct, ps=ps, n_bc=n_bc, srcf=srcf, dstb=dstb):
                        v.wait_ge(ph.sem("l2c"), n_bc)
                        v.tensor_mul(dstb[:, ct * TT:(ct + 1) * TT],
                                     srcf[:, ct * TT:(ct + 1) * TT], ps[:, :]
                                     ).then_inc(ph.sem("ab"), 1)
                    em.on("vector", v_mul)
                    ph.bump("ab")
                return ph.val("ab")

            n_v = proj_conv(3 * j3 + 2, vf32[:, :], vb[:, :])
            proj_conv(3 * j3 + 0, f32w[:, :], None)
            n_qn = emit_l2(f32w[:, :], qn[:, :], ph.val("sl"))
            proj_conv(3 * j3 + 1, f32w[:, :], None)
            n_kn = emit_l2(f32w[:, :], kn[:, :], ph.val("sl"))
            if _B_PART < 2:
                continue

            c_l2r_b = ph.val("l2r")
            for grp in range(4):
                def t_brow(t, s=s, h=h, grp=grp, n_bsg_u=n_bsg_u, c_l2r_b=c_l2r_b,
                           w_gg=pg("gg")):
                    t.wait_ge(ph.sem("bsg"), n_bsg_u)
                    t.wait_ge(ph.sem("l2r"), c_l2r_b)
                    pg_wait(t, w_gg)
                    for q4 in range(4):
                        bt = grp * 4 + q4
                        k = t.transpose(psG[0:1, q4 * 128:q4 * 128 + 128],
                                        bsg[:, bt, h:h + 1], sb_identF[:, :])
                    k.then_inc(ph.sem("btr"), 1)
                em.on("tensor", t_brow)
                n_btr = ph.bump("btr")

                def v_brow(v, grp=grp, n_btr=n_btr):
                    v.wait_ge(ph.sem("btr"), n_btr)
                    v.tensor_copy(rowA[0:1, grp * 512:(grp + 1) * 512], psG[0:1, :]
                                  ).then_inc(ph.sem("gg"), 1)
                em.on("vector", v_brow)
                ph.bump("gg")
            n_brow = ph.val("gg")

            for ct in range(NTT):
                ps = psA if ct % 2 == 0 else psB

                def t_bbc(t, ct=ct, ps=ps, n_brow=n_brow, w_ab=pg("ab", 1)):
                    t.wait_ge(ph.sem("gg"), n_brow)
                    pg_wait(t, w_ab)
                    t.matmul(ps[:, :], ones_row_f[0:1, :], rowA[0:1, ct * TT:(ct + 1) * TT],
                             start=True, stop=True).then_inc(ph.sem("bbc"), 1)
                em.on("tensor", t_bbc)
                n_bbc = ph.bump("bbc")

                def v_kbf(v, ct=ct, ps=ps, n_bbc=n_bbc, n_kn=n_kn):
                    v.wait_ge(ph.sem("bbc"), n_bbc)
                    v.tensor_mul(kbfm[:, ct * TT:(ct + 1) * TT],
                                 kn[:, ct * TT:(ct + 1) * TT], ps[:, :]
                                 ).then_inc(ph.sem("ab"), 1)
                em.on("vector", v_kbf)
                ph.bump("ab")
            n_kbf = ph.val("ab")

            for bl in range(BL):
                def t_tr(t, bl=bl, n_kn=n_kn, n_v=n_v, w_ee=pg("ee")):
                    t.wait_ge(ph.sem("ab"), n_kn)
                    t.wait_ge(ph.sem("sl"), n_v)
                    pg_wait(t, w_ee)
                    t.transpose(psE[:, :], kn[:, bl * 128:(bl + 1) * 128], sb_identB[:, :]
                                ).then_inc(ph.sem("ktr"), 1)
                em.on("tensor", t_tr)
                n_ktr = ph.bump("ktr")

                def v_ktr(v, bl=bl, n_ktr=n_ktr, c_kb=ph.val("kbm")):
                    v.wait_ge(ph.sem("ktr"), n_ktr)
                    v.wait_ge(ph.sem("kbm"), c_kb)
                    v.tensor_copy(ktm[:, bl, :], psE[:, :]).then_inc(ph.sem("ee"), 1)
                em.on("vector", v_ktr)
                n_ee = ph.bump("ee")

                def g_kbm(gp, s=s, h=h, bl=bl, n_ee=n_ee):
                    gp.wait_ge(ph.sem("ee"), n_ee)
                    gp.tensor_single_scalar(vkb[:, bl, 128:256], ktm[:, bl, :],
                                            bsg[:, bl, h:h + 1], ALU.mult
                                            ).then_inc(ph.sem("kbm"), 1)
                em.on("gpsimd", g_kbm)
                ph.bump("kbm")

                def t_tr2(t, bl=bl, w_ee=pg("ee"), n_v=n_v):
                    t.wait_ge(ph.sem("sl"), n_v)
                    pg_wait(t, w_ee)
                    t.transpose(psE[:, :], vb[:, bl * 128:(bl + 1) * 128], sb_identB[:, :]
                                ).then_inc(ph.sem("vtr"), 1)
                em.on("tensor", t_tr2)
                n_vtr = ph.bump("vtr")

                def v_vtr(v, bl=bl, n_vtr=n_vtr, c_kb=ph.val("kbm")):
                    v.wait_ge(ph.sem("vtr"), n_vtr)
                    v.wait_ge(ph.sem("kbm"), c_kb)
                    v.tensor_copy(wtmp[:, :], psE[:, :]).then_inc(ph.sem("ee"), 1)
                em.on("vector", v_vtr)
                n_ee2 = ph.bump("ee")

                def g_vbm(gp, s=s, h=h, bl=bl, n_ee2=n_ee2):
                    gp.wait_ge(ph.sem("ee"), n_ee2)
                    gp.tensor_single_scalar(vkb[:, bl, 0:128], wtmp[:, :],
                                            bsg[:, bl, h:h + 1], ALU.mult
                                            ).then_inc(ph.sem("kbm"), 1)
                em.on("gpsimd", g_vbm)
                ph.bump("kbm")
            n_tm_done = ph.val("kbm")

            if _B_PART < 3:
                continue
            for bl in range(BL):
                def t_A(t, bl=bl, n_kbf=n_kbf, c_eapA=ph.val("eap")):
                    t.wait_ge(ph.sem("ab"), n_kbf)
                    t.wait_ge(ph.sem("eap"), c_eapA)
                    t.matmul(psC[:, 0:128], kbfm[:, bl * 128:(bl + 1) * 128],
                             kn[:, bl * 128:(bl + 1) * 128], start=True, stop=True)
                    t.matmul(psC[:, 128:256], kn[:, bl * 128:(bl + 1) * 128],
                             kbfm[:, bl * 128:(bl + 1) * 128], start=True, stop=True
                             ).then_inc(ph.sem("mA"), 1)
                em.on("tensor", t_A)
                n_mA = ph.bump("mA")

                def v_A(v, n_mA=n_mA, c_sq=ph.val("msq"), c_ap2=ph.val("map2")):
                    v.wait_ge(ph.sem("mA"), n_mA)
                    v.wait_ge(ph.sem("msq"), c_sq)
                    v.wait_ge(ph.sem("map2"), c_ap2)
                    v.tensor_mul(Xp_[0][:, :], psC[:, 0:128], sb_maskA[:, :])
                    v.tensor_mul(Yp_[0][:, :], psC[:, 128:256], sb_maskAT[:, :]
                                 ).then_inc(ph.sem("cc"), 1)
                em.on("vector", v_A)
                n_cc = ph.bump("cc")

                def t_ai(t, bl=bl, n_kn=n_kn, w_dd=pg("dd"), c_uie=ph.val("uie")):
                    t.wait_ge(ph.sem("ab"), n_kn)
                    t.wait_ge(ph.sem("uie"), c_uie)
                    pg_wait(t, w_dd)
                    t.matmul(psD[:, :], kn[:, bl * 128:(bl + 1) * 128],
                             qn[:, bl * 128:(bl + 1) * 128], start=True, stop=True
                             ).then_inc(ph.sem("mai"), 1)
                em.on("tensor", t_ai)
                n_mai = ph.bump("mai")

                def v_ai(v, bl=bl, n_mai=n_mai):
                    v.wait_ge(ph.sem("mai"), n_mai)
                    v.tensor_mul(aiT[:, bl, :], psD[:, :], sb_maskTIT[:, :]
                                 ).then_inc(ph.sem("dd"), 1)
                em.on("vector", v_ai)
                ph.bump("dd")

                for lev in range(5):
                    def t_sq(t, lev=lev, n_cc=n_cc, c_esq=ph.val("esq")):
                        if lev == 0:
                            t.wait_ge(ph.sem("cc"), n_cc)
                        else:
                            t.wait_ge(ph.sem("esq"), c_esq)
                        xs = Xp_[lev % 2]
                        ys = Yp_[lev]
                        t.matmul(psC[:, 0:128], ys[:, :], xs[:, :], start=True, stop=True)
                        t.matmul(psC[:, 128:256], xs[:, :], ys[:, :], start=True, stop=True
                                 ).then_inc(ph.sem("msq"), 1)
                    em.on("tensor", t_sq)
                    n_msq = ph.bump("msq")

                    def v_esq(v, lev=lev, n_msq=n_msq):
                        v.wait_ge(ph.sem("msq"), n_msq)
                        v.tensor_copy(Xp_[(lev + 1) % 2][:, :], psC[:, 0:128])
                        v.tensor_copy(Yp_[lev + 1][:, :], psC[:, 128:256]
                                      ).then_inc(ph.sem("esq"), 1)
                    em.on("vector", v_esq)
                    ph.bump("esq")
                n_esq = ph.val("esq")

                def v_eap0(v, bl=bl, n_tm=n_tm_done, c_ap=ph.val("map2")):
                    v.wait_ge(ph.sem("kbm"), n_tm)
                    v.wait_ge(ph.sem("map2"), c_ap)
                    v.tensor_copy(tsb[:, :], vkb[:, bl, :]).then_inc(ph.sem("eap"), 1)
                em.on("vector", v_eap0)
                n_eap = ph.bump("eap")

                for lev in range(6):
                    ysrc = Yp_[0] if lev == 0 else Yp_[lev]

                    def t_ap(t, lev=lev, ysrc=ysrc, n_eap=n_eap, n_esq=n_esq,
                             c_eapc=ph.val("eap")):
                        t.wait_ge(ph.sem("eap"), n_eap)
                        t.wait_ge(ph.sem("esq"), n_esq)
                        t.matmul(psC[:, :], ysrc[:, :], tsb[:, :], start=True, stop=True
                                 ).then_inc(ph.sem("map2"), 1)
                    em.on("tensor", t_ap)
                    n_map2 = ph.bump("map2")

                    def v_eap(v, lev=lev, bl=bl, n_map2=n_map2):
                        v.wait_ge(ph.sem("map2"), n_map2)
                        v.tensor_add(tsb[:, :], tsb[:, :], psC[:, :])
                        if lev == 5:
                            v.tensor_copy(utm[:, bl, :], tsb[:, 0:128])
                            v.tensor_copy(wtmp[:, :], tsb[:, 128:256]).then_inc(ph.sem("eap"), 1)
                        else:
                            v.engine_nop().then_inc(ph.sem("eap"), 1)
                    em.on("vector", v_eap)
                    n_eap = ph.bump("eap")

                def t_wT(t, bl=bl, n_eap=n_eap, w_ee=pg("ee")):
                    t.wait_ge(ph.sem("eap"), n_eap)
                    pg_wait(t, w_ee)
                    t.transpose(psE[:, :], wtmp[:, :], sb_identB[:, :]).then_inc(ph.sem("mwt"), 1)
                em.on("tensor", t_wT)
                n_mwt = ph.bump("mwt")

                def v_wT(v, bl=bl, n_mwt=n_mwt):
                    v.wait_ge(ph.sem("mwt"), n_mwt)
                    v.tensor_copy(wTf[:, bl, :], psE[:, :]).then_inc(ph.sem("ee"), 1)
                em.on("vector", v_wT)
                ph.bump("ee")
            n_blocks = (ph.val("ee"), ph.val("dd"), ph.val("eap"))

            if _B_PART < 4:
                continue
            c_ab_scan = ph.val("ab")
            for cc2 in range(2 * BL):
                bl, hh2 = cc2 // 2, cc2 % 2
                b0 = hh2 * 64
                po = psA if (cc2 // 8) % 2 == 0 else psB
                col = (cc2 % 8) * 64

                def t_ui(t, bl=bl, b0=b0, cc2=cc2, n_blocks=n_blocks,
                         c_sev=ph.val("scp"), c_uie=ph.val("uie")):
                    if cc2 == 0:
                        t.wait_ge(ph.sem("ee"), n_blocks[0])
                        t.wait_ge(ph.sem("dd"), n_blocks[1])
                        t.wait_ge(ph.sem("eap"), n_blocks[2])
                        return
                    t.wait_ge(ph.sem("scp"), c_sev)
                    t.wait_ge(ph.sem("uie"), c_uie)
                    t.matmul(psD[b0:b0 + 64, :], wTf[:, bl, b0:b0 + 64], Ssb[:, :],
                             start=True, stop=True, tile_position=(0, b0)
                             ).then_inc(ph.sem("mui"), 1)
                em.on("tensor", t_ui)
                n_mui = ph.bump("mui") if cc2 > 0 else ph.val("mui")

                def v_ui(v, bl=bl, b0=b0, cc2=cc2, n_mui=n_mui):
                    if cc2 > 0:
                        v.wait_ge(ph.sem("mui"), n_mui)
                        v.tensor_sub(uis[b0:b0 + 64, :], utm[b0:b0 + 64, bl, :],
                                     psD[b0:b0 + 64, :]).then_inc(ph.sem("uie"), 1)
                    else:
                        v.tensor_copy(uis[b0:b0 + 64, :], utm[b0:b0 + 64, bl, :]
                                      ).then_inc(ph.sem("uie"), 1)
                em.on("vector", v_ui)
                n_uie = ph.bump("uie")

                def t_sc(t, bl=bl, b0=b0, cc2=cc2, po=po, col=col, n_uie=n_uie,
                         w_oev=pg("oev", 1), n_kbf=n_kbf, c_ab_scan=c_ab_scan):
                    t.wait_ge(ph.sem("uie"), n_uie)
                    if cc2 == 0:
                        t.wait_ge(ph.sem("ab"), c_ab_scan)
                    if col == 0:
                        pg_wait(t, w_oev)
                    if cc2 > 0:
                        t.matmul(po[:, col:col + 64], Ssb[:, :],
                                 qn[:, cc2 * 64:(cc2 + 1) * 64], start=True, stop=False)
                    t.matmul(po[:, col:col + 64], uis[b0:b0 + 64, :],
                             aiT[b0:b0 + 64, bl, b0:b0 + 64], start=(cc2 == 0), stop=True)
                    t.matmul(psF[:, :], ktm[b0:b0 + 64, bl, :], uis[b0:b0 + 64, :],
                             start=True, stop=True).then_inc(ph.sem("msc"), 1)
                em.on("tensor", t_sc)
                n_msc = ph.bump("msc")

                def v_sev(v, cc2=cc2, n_msc=n_msc):
                    v.wait_ge(ph.sem("msc"), n_msc)
                    if cc2 == 0:
                        v.tensor_copy(Sf[:, :], psF[:, :]).then_inc(ph.sem("sev"), 1)
                    else:
                        v.tensor_add(Sf[:, :], Sf[:, :], psF[:, :]).then_inc(ph.sem("sev"), 1)
                em.on("vector", v_sev)
                n_sev = ph.bump("sev")

                def s_scp(sc, n_sev=n_sev):
                    sc.wait_ge(ph.sem("sev"), n_sev)
                    sc.activation(Ssb[:, :], Sf[:, :], AF.Identity).then_inc(ph.sem("scp"), 1)
                em.on("scalar", s_scp)
                ph.bump("scp")

                if cc2 % 8 == 7:
                    def v_oev(v, cc2=cc2, po=po, n_msc=ph.val("msc"),
                              c_ost=ph.val("ost")):
                        v.wait_ge(ph.sem("msc"), n_msc)
                        v.wait_ge(ph.sem("ost"), c_ost)
                        v.tensor_copy(odel[:, (cc2 - 7) * 64:(cc2 + 1) * 64], po[:, :]
                                      ).then_inc(ph.sem("oev"), 1)
                    em.on("vector", v_oev)
                    ph.bump("oev")
            n_oev = ph.val("oev")

            if _B_PART < 5:
                continue
            c_bbc_g = ph.val("bbc")
            for grp in range(4):
                def t_grow(t, s=s, h=h, grp=grp, c_bbc_g=c_bbc_g, w_gg=pg("gg")):
                    t.wait_ge(ph.sem("bbc"), c_bbc_g)
                    pg_wait(t, w_gg)
                    for q4 in range(4):
                        bt = grp * 4 + q4
                        k = t.transpose(psG[0:1, q4 * 128:q4 * 128 + 128],
                                        bmn[:, bt, 8 + h:9 + h], sb_identF[:, :])
                    k.then_inc(ph.sem("gtr"), 1)
                em.on("tensor", t_grow)
                n_gtr = ph.bump("gtr")

                def v_grow(v, grp=grp, n_gtr=n_gtr):
                    v.wait_ge(ph.sem("gtr"), n_gtr)
                    v.tensor_copy(rowA[0:1, grp * 512:(grp + 1) * 512], psG[0:1, :]
                                  ).then_inc(ph.sem("gg"), 1)
                em.on("vector", v_grow)
                ph.bump("gg")
            n_gr2 = ph.val("gg")

            def s_gsig(sc, h=h, n_gr2=n_gr2):
                sc.wait_ge(ph.sem("gg"), n_gr2)
                sc.activation(rowB[0:1, :], rowA[0:1, :], AF.Sigmoid,
                              bias=sb_mixsum[0:1, h:h + 1]).then_inc(ph.sem("gsg"), 1)
            em.on("scalar", s_gsig)
            n_gsg = ph.bump("gsg")

            for ct in range(NTT):
                ps = psA if ct % 2 == 0 else psB

                def t_gbc(t, ct=ct, ps=ps, n_gsg=n_gsg, n_oev=n_oev,
                          w_ab=pg("ab", 1)):
                    t.wait_ge(ph.sem("gsg"), n_gsg)
                    t.wait_ge(ph.sem("oev"), n_oev)
                    pg_wait(t, w_ab)
                    t.matmul(ps[:, :], ones_row_f[0:1, :], rowB[0:1, ct * TT:(ct + 1) * TT],
                             start=True, stop=True).then_inc(ph.sem("gbm"), 1)
                em.on("tensor", t_gbc)
                n_gbm = ph.bump("gbm")

                def v_gbe(v, ct=ct, ps=ps, n_gbm=n_gbm):
                    v.wait_ge(ph.sem("gbm"), n_gbm)
                    v.tensor_copy(gbb[:, ct * TT:(ct + 1) * TT], ps[:, :]
                                  ).then_inc(ph.sem("ab"), 1)
                em.on("vector", v_gbe)
                ph.bump("ab")
            n_gb = ph.val("ab")

            def v_og1(v, n_gb=n_gb, n_v=n_v):
                v.wait_ge(ph.sem("ab"), n_gb)
                v.tensor_sub(odel[:, :], odel[:, :], vf32[:, :]).then_inc(ph.sem("og1"), 1)
            em.on("vector", v_og1)
            n_og1 = ph.bump("og1")

            def g_og2(gp, n_og1=n_og1):
                gp.wait_ge(ph.sem("og1"), n_og1)
                gp.tensor_mul(odel[:, :], odel[:, :], gbb[:, :]).then_inc(ph.sem("og2"), 1)
            em.on("gpsimd", g_og2)
            n_og2 = ph.bump("og2")

            def v_og3(v, n_og2=n_og2):
                v.wait_ge(ph.sem("og2"), n_og2)
                v.tensor_add(odel[:, :], odel[:, :], vf32[:, :]).then_inc(ph.sem("og3"), 1)
            em.on("vector", v_og3)
            n_og3 = ph.bump("og3")

            def s_osq(sc, n_og3=n_og3, c_gm=ph.val("l2m")):
                sc.wait_ge(ph.sem("og3"), n_og3)
                sc.wait_ge(ph.sem("l2m"), c_gm)
                sc.activation(b16s[:, :], odel[:, :], AF.Square).then_inc(ph.sem("l2s"), 1)
            em.on("scalar", s_osq)
            n_osq = ph.bump("l2s")

            for ct in range(NTT):
                def t_osum(t, ct=ct, n_osq=n_osq, w_gg=pg("gg")):
                    t.wait_ge(ph.sem("l2s"), n_osq)
                    pg_wait(t, w_gg)
                    t.matmul(psG[0:1, :], ones_col_b[:, 0:1],
                             b16s[:, ct * TT:(ct + 1) * TT], start=True, stop=True
                             ).then_inc(ph.sem("l2m"), 1)
                em.on("tensor", t_osum)
                n_osm = ph.bump("l2m")

                def v_orow(v, ct=ct, n_osm=n_osm, c_rs=ph.val("orr")):
                    v.wait_ge(ph.sem("l2m"), n_osm)
                    if ct == 0:
                        v.wait_ge(ph.sem("orr"), c_rs)
                    v.tensor_copy(rowA[0:1, ct * TT:(ct + 1) * TT], psG[0:1, :]
                                  ).then_inc(ph.sem("gg"), 1)
                em.on("vector", v_orow)
                ph.bump("gg")
            n_ogg = ph.val("gg")

            def s_orstd(sc, n_ogg=n_ogg):
                sc.wait_ge(ph.sem("gg"), n_ogg)
                sc.activation(rowB[0:1, :], rowA[0:1, :], AF.Ln, scale=1.0 / 128,
                              bias=constf(EPS_LN)[0:1, 0:1])
                sc.activation(rowB[0:1, :], rowB[0:1, :], AF.Exp, scale=-0.5
                              ).then_inc(ph.sem("orr"), 1)
            em.on("scalar", s_orstd)
            n_orr = ph.bump("orr")

            for ct in range(NTT):
                ps = psA if ct % 2 == 0 else psB

                def t_obc(t, ct=ct, ps=ps, n_orr=n_orr, c_of=ph.val("ofin"),
                          w_ab=pg("ab", 1)):
                    t.wait_ge(ph.sem("orr"), n_orr)
                    t.wait_ge(ph.sem("ofin"), c_of)
                    pg_wait(t, w_ab)
                    t.matmul(ps[:, :], ones_row_f[0:1, :], rowB[0:1, ct * TT:(ct + 1) * TT],
                             start=True, stop=True).then_inc(ph.sem("obm"), 1)
                em.on("tensor", t_obc)
                n_obm = ph.bump("obm")

                def v_onm(v, ct=ct, ps=ps, n_obm=n_obm):
                    v.wait_ge(ph.sem("obm"), n_obm)
                    v.tensor_mul(bcb[:, ct * TT:(ct + 1) * TT],
                                 odel[:, ct * TT:(ct + 1) * TT], ps[:, :]
                                 ).then_inc(ph.sem("onm"), 1)
                em.on("vector", v_onm)
                ph.bump("onm")
            n_onm = ph.val("onm")

            def g_ofin(gp, s=s, h=h, n_onm=n_onm):
                gp.wait_ge(ph.sem("onm"), n_onm)
                gp.tensor_single_scalar(odel[:, :], bcb[:, :], sb_onorm[:, 0:1],
                                        ALU.mult).then_inc(ph.sem("ofin"), 1)
            em.on("gpsimd", g_ofin)
            n_ofin = ph.bump("ofin")

            def dma_on(e, s=s, h=h, n_ofin=n_ofin):
                e.wait_ge(ph.sem("ofin"), n_ofin)
                e.dma_start(s_r[s, h, :, :], odel[:, :]).then_inc(ph.sem("ost"), 16)
            em.on("sync", dma_on)
            ph.bump("ost", 16)

        def fin_s(e):
            e.wait_ge(ph.sem("ost"), ph.val("ost"))
        em.on("sync", fin_s)
        em.replay(nc)
        pes.close()
        return

    if _build_b:
        phase_b_main()

    def phase_b_out():
        pes = ExitStack()
        Sl = lambda n, sh, dt: pes.enter_context(nc.sbuf_tensor(n, sh, dt))
        Pl = lambda n, sh: pes.enter_context(nc.psum_tensor(n, sh, F32))
        ph = Sems(nc, es, "bo")
        em = Emitter()
        ow_sb = Sl("bo_ow", [128, DC, D], BF16)
        u1 = Sl("bo_u1", [128, DC, TT], F32)
        onb = [Sl(f"bo_on{j}", [128, DC, TT], BF16) for j in range(2)]
        ps_u = [Pl(f"bo_ps{j}", [128, 512]) for j in range(2)]

        def wload(e):
            e.dma_start(ow_sb[:, :, :], owB.rearrange("(c p) o -> p c o", p=128)
                        ).then_inc(ph.sem("wld"), 16)
        em.on("gpsimd", wload)
        n_wld_b = ph.bump("wld", 16)

        tiles = [(s, tt) for s in range(NSEQ) for tt in range(NTT)]
        for i, (s, tt) in enumerate(tiles):
            jb = i % 2

            def dma_on(e, s=s, tt=tt, i=i, jb=jb, c_mm=ph.val("mm")):
                if i >= 2:
                    e.wait_ge(ph.sem("mm"), c_mm - DC)
                e.dma_start(onb[jb][:, :, :], dtile(s_r, s, tt * TT, (tt + 1) * TT)
                            ).then_inc(ph.sem("onl"), 16)
            em.on("gpsimd", dma_on)
            n_onl = ph.bump("onl", 16)

            for o in range(DC):
                b = o % 2

                def t_mm(t, s=s, tt=tt, o=o, b=b, jb=jb, n_onl=n_onl, c_ev=ph.val("ev"),
                         n_wld_b=n_wld_b):
                    t.wait_ge(ph.sem("wld"), n_wld_b)
                    t.wait_ge(ph.sem("onl"), n_onl)
                    if c_ev >= 2:
                        t.wait_ge(ph.sem("ev"), c_ev - 1)
                    for c in range(DC):
                        k = t.matmul(ps_u[b][:, :], ow_sb[:, c, o * 128:(o + 1) * 128],
                                     onb[jb][:, c, :], start=(c == 0), stop=(c == DC - 1))
                    k.then_inc(ph.sem("mm"), 1)
                em.on("tensor", t_mm)
                n_mm = ph.bump("mm")

                def v_ev(v, o=o, b=b, n_mm=n_mm, i=i, c_acc=ph.val("acc"),
                         c_st=ph.val("st")):
                    v.wait_ge(ph.sem("mm"), n_mm)
                    if o == 0 and i >= 1:
                        v.wait_ge(ph.sem("acc"), c_acc)
                        v.wait_ge(ph.sem("st"), c_st)
                    v.tensor_copy(u1[:, o, :], ps_u[b][:, :]).then_inc(ph.sem("ev"), 1)
                em.on("vector", v_ev)
                ph.bump("ev")
            n_ev = ph.val("ev")

            def dma_acc(e, s=s, tt=tt, n_ev=n_ev):
                e.wait_ge(ph.sem("ev"), n_ev)
                e.dma_start(u1[:, :, :], dtile(s_xB, s, tt * TT, (tt + 1) * TT),
                            accum_op=ALU.add).then_inc(ph.sem("acc"), 16)
            em.on("gpsimd", dma_acc)
            n_acc = ph.bump("acc", 16)

            def dma_st(e, s=s, tt=tt, n_acc=n_acc):
                e.wait_ge(ph.sem("acc"), n_acc)
                e.dma_start(dtile(s_xC, s, tt * TT, (tt + 1) * TT), u1[:, :, :]
                            ).then_inc(ph.sem("st"), 16)
            em.on("sync", dma_st)
            ph.bump("st", 16)

        def fin(e):
            e.wait_ge(ph.sem("st"), ph.val("st"))
        em.on("sync", fin)
        em.replay(nc)
        pes.close()

    if _build_b:
        phase_b_out()
        closeA()

    if _PHASES >= 6:
        openA("a5")
        phase_f_ln("f2l", s_xC)
        openB("b5")
        phase_f_att("f2a")
        phase_f_out("f2o", wnatF)
        phase_f_moe("f2m", wnatF, s_xC, y_out)
        closeA(); closeB()


    _dram_by_name = {"s_r": s_r, "s_u2": s_u2, "s_xA": s_xA, "s_ffn_in": s_ffn_in,
                     "s_a": s_a, "s_xB": s_xB, "s_xC": s_xC, "xbfB": s_xC, "wnat": s_xC}

    def emit_copy_dram_to_y(src):
        pes2 = ExitStack()
        ph2 = Sems(nc, es, "fin")
        em2 = Emitter()
        buf = pes2.enter_context(nc.sbuf_tensor("finbuf", [128, DC, TT], F32))
        tiles = [(s, tt) for s in range(NSEQ) for tt in range(NTT)]
        base_n = ph2.val("st")

        def go(e, base_n=base_n):
            n = base_n
            for i, (s, tt) in enumerate(tiles):
                e.dma_start(buf[:, :, :], dtile(src, s, tt * TT, (tt + 1) * TT)
                            ).then_inc(ph2.sem("st"), 16)
                n += 16
                e.wait_ge(ph2.sem("st"), n)
                e.dma_start(dtile(y_out, s, tt * TT, (tt + 1) * TT), buf[:, :, :]
                            ).then_inc(ph2.sem("st"), 16)
                n += 16
            e.wait_ge(ph2.sem("st"), n)
        em2.on("sync", go)
        for _ in range(2 * len(tiles)):
            ph2.bump("st", 16)
        em2.replay(nc)
        pes2.close()

    if _DEBUG_TAP is not None:
        emit_copy_dram_to_y(_dram_by_name[_DEBUG_TAP])
        return nc

    if _PHASES < 6:
        emit_copy_dram_to_y(s_r)
    return nc



def _host_inputs(inputs):
    f32 = lambda a: np.ascontiguousarray(np.asarray(a), dtype=np.float32)
    shared = {}
    shared["wqkvT"] = f32(np.asarray(inputs["L0_attn_a_w_qkv"]).T)
    shared["woutT"] = f32(np.asarray(inputs["L0_attn_a_w_out"]).T)
    shared["w1T"] = f32(np.asarray(inputs["L0_attn_a_w1"]).T)
    shared["w2T"] = f32(np.asarray(inputs["L0_attn_a_w2"]).T)
    shared["projW"] = f32(inputs["L0_attn_a_proj"])
    vec2d = lambda v, c: f32(np.asarray(v).reshape(c, 128).T)
    shared["gammaA"] = vec2d(inputs["L0_attn_a_gamma"], DC)
    shared["betaA"] = vec2d(inputs["L0_attn_a_beta"], DC)
    shared["b_outA"] = vec2d(inputs["L0_attn_a_b_out"], DC)
    shared["b1A"] = vec2d(inputs["L0_attn_a_b1"], 32)
    shared["b2A"] = vec2d(inputs["L0_attn_a_b2"], DC)
    shared["Wq"] = f32(inputs["L0_ffn_Wq"]); shared["Wk"] = f32(inputs["L0_ffn_Wk"])
    shared["Wv"] = f32(inputs["L0_ffn_Wv"]); shared["Wo"] = f32(inputs["L0_ffn_Wo"])
    shared["bqF"] = vec2d(inputs["L0_ffn_bq"], DC); shared["bkF"] = vec2d(inputs["L0_ffn_bk"], DC)
    shared["bvF"] = vec2d(inputs["L0_ffn_bv"], DC); shared["boF"] = vec2d(inputs["L0_ffn_bo"], DC)
    shared["ln1g"] = vec2d(inputs["L0_ffn_ln1_g"], DC); shared["ln1b"] = vec2d(inputs["L0_ffn_ln1_b"], DC)
    shared["ln2g"] = vec2d(inputs["L0_ffn_ln2_g"], DC); shared["ln2b"] = vec2d(inputs["L0_ffn_ln2_b"], DC)
    shared["WrD"] = f32(inputs["L0_ffn_Wr"])
    shared["brD"] = f32(np.asarray(inputs["L0_ffn_br"]).reshape(1, 8))
    shared["ew1"] = f32(inputs["L0_ffn_e_w1"])
    shared["ew2"] = f32(inputs["L0_ffn_e_w2"])
    shared["eb1"] = f32(np.asarray(inputs["L0_ffn_e_b1"]).reshape(8, 32, 128).transpose(2, 0, 1).reshape(128, 256))
    shared["eb2"] = f32(np.asarray(inputs["L0_ffn_e_b2"]).reshape(8, 8, 128).transpose(2, 0, 1).reshape(128, 64))
    shared["qwB"] = f32(inputs["L0_attn_b_q_w"]); shared["kwB"] = f32(inputs["L0_attn_b_k_w"])
    shared["vwB"] = f32(inputs["L0_attn_b_v_w"]); shared["owB"] = f32(inputs["L0_attn_b_o_w"])
    shared["bwmix"] = f32(np.concatenate([np.asarray(inputs["L0_attn_b_b_w"]),
                                          np.asarray(inputs["L0_attn_b_mix_w"])], axis=1))
    cw = lambda v: f32(np.asarray(v).reshape(DC, 128, 4).transpose(1, 0, 2))
    shared["convq"] = cw(inputs["L0_attn_b_conv_q_w"])
    shared["convk"] = cw(inputs["L0_attn_b_conv_k_w"])
    shared["convv"] = cw(inputs["L0_attn_b_conv_v_w"])
    shared["mixb_in"] = f32(np.asarray(inputs["L0_attn_b_mix_b"]).reshape(1, 8))
    shared["mixbias_in"] = f32(np.asarray(inputs["L0_attn_b_mix_bias"]).reshape(1, 8))
    shared["onormW"] = f32(np.asarray(inputs["L0_attn_b_o_norm_w"]).reshape(128, 1))

    bf = ml_dtypes.bfloat16
    shared["identB"] = np.eye(128, dtype=bf)
    shared["identF"] = np.eye(128, dtype=np.float32)
    kq = (np.arange(128)[None, :] >= np.arange(128)[:, None])
    shared["mask_kq"] = kq.astype(bf)
    blk = (np.arange(128)[:, None] // 64) == (np.arange(128)[None, :] // 64)
    strict_low = blk & (np.arange(128)[None, :] < np.arange(128)[:, None])
    shared["maskA_neg"] = (-1.0 * strict_low).astype(bf)
    shared["maskAT_neg"] = (-1.0 * strict_low.T).astype(bf)
    shared["maskTriIncT"] = (blk & (np.arange(128)[:, None] <= np.arange(128)[None, :])).astype(bf)
    for m in range(4):
        shared[f"m512_{m}"] = (np.arange(512)[None, :] >= (m * 128 + np.arange(128)[:, None])).astype(bf)

    x = np.asarray(inputs["L0_attn_a_x"], dtype=np.float32)
    in_maps = []
    for c in range(NCORE):
        m = dict(shared)
        xs = np.ascontiguousarray(x[2 * c:2 * c + 2].transpose(0, 2, 1))
        m["x"] = xs.reshape(NSEQ, DC, 128, T)
        in_maps.append(m)
    return in_maps


def _assemble(res):
    outs = []
    for c in range(NCORE):
        y = res.results[c]["y"].reshape(NSEQ, D, T)
        outs.append(y.transpose(0, 2, 1))
    return np.ascontiguousarray(np.concatenate(outs, axis=0), dtype=np.float32)


def kernel(**inputs):
    in_maps = _host_inputs(inputs)
    nc = build_program()
    res = run_bass_kernel_spmd(nc, in_maps, core_ids=list(range(NCORE)))
    return _assemble(res)


# version 02
# speedup vs baseline: 1.0781x; 1.0781x over previous
from contextlib import ExitStack

import numpy as np
import ml_dtypes

import concourse.bass as bass
import concourse.mybir as mybir
from concourse.bass_utils import run_bass_kernel_spmd

F32 = mybir.dt.float32
BF16 = mybir.dt.bfloat16
AF = mybir.ActivationFunctionType
ALU = mybir.AluOpType
AX = mybir.AxisListType

NCORE = 8
NSEQ = 2
T = 2048
D = 1024
DC = D // 128
TT = 512
NTT = T // TT
EPS_LN = 1e-5

_DEBUG_TAP = None
_PHASES = 99
_F_ATT_HP = 8
_F_ATT_PART = 99
_B_PART = 99
_B_UNITS = 99
_SKIP_EARLY = False


class Emitter:
    ENGINES = ("sync", "gpsimd", "tensor", "vector", "scalar")

    def __init__(self):
        self.ops = {e: [] for e in self.ENGINES}

    def on(self, eng, fn):
        self.ops[eng].append(fn)

    def replay(self, nc):
        with nc.Block() as blk:
            for eng in self.ENGINES:
                if not self.ops[eng]:
                    continue
                def run(e, _ops=tuple(self.ops[eng])):
                    for f in _ops:
                        f(e)
                getattr(blk, eng)(run)


class Sems:

    _REG = None

    def __init__(self, nc, es, prefix):
        if Sems._REG is None or Sems._REG[0] is not nc:
            Sems._REG = (nc, es, {}, {})
        object.__setattr__(self, "_d", None)

    ALIAS = {}
    for _grp in ("og1 og2 og3|og", "r1 r2 r2b r3 r4 r5 r6 r8 mlg sg|rt",
                 "cv1 cv2 cv3 cv4|cv", "l2s l2m l2r l2c|l2x", "btr gtr|tr3",
                 "bmm mA mai|mai", "gt gr gb gv|gat", "map map2 eap|map",
                 "mui msc|msn", "ktr vtr mwt kbm|trm", "gbm obm bbc|bcm",
                 "gsg orr|sgr", "onm ofin|onf", "elv1 els|elx", "inv mzb|izb",
                 "mst|mp", "mmq mmv mmf|mma", "zfin vz|zf", "mat mkv|mk",
                 "sev uie|su", "esq msq|sq2",
                 "ld lw1 lw2 cl|dls",
                 "ldu wl bl owl onl cw pl|dlg",
                 "cl2|cl2", "ac1 ac2 acc2|acc", "sta stx|st", "evo|ev",
                 "vinit brow sq cast bc var app aff bf stat rstd|mi1",
                 "bsg bme|mi2"):
        _names, _to = _grp.split("|")
        for _n in _names.split():
            ALIAS[_n] = _to

    def sem(self, name):
        name = Sems.ALIAS.get(name, name)
        nc, es, sems, cnt = Sems._REG
        if name not in sems:
            sems[name] = es.enter_context(nc.semaphore(f"s_{name}"))
            cnt[name] = 0
        return sems[name]

    def bump(self, name, k=1):
        name = Sems.ALIAS.get(name, name)
        self.sem(name)
        Sems._REG[3][name] += k
        return Sems._REG[3][name]

    def val(self, name):
        name = Sems.ALIAS.get(name, name)
        self.sem(name)
        return Sems._REG[3][name]


def build_program():
    nc = bass.Bass("TRN2", target_bir_lowering=False, detect_race_conditions=False)
    es = ExitStack()

    def S(name, shape, dtype):
        return es.enter_context(nc.sbuf_tensor(name, shape, dtype))

    def P(name, shape, dtype=F32):
        return es.enter_context(nc.psum_tensor(name, shape, dtype))

    DI = lambda name, shape, dtype=F32: nc.dram_tensor(name, shape, dtype, kind="ExternalInput")
    DS = lambda name, shape, dtype=F32: nc.dram_tensor(name, shape, dtype, kind="Internal")

    x_in = DI("x", [NSEQ, DC, 128, T])
    y_out = nc.dram_tensor("y", [NSEQ, DC, 128, T], F32, kind="ExternalOutput")

    wqkvT = DI("wqkvT", [D, 3 * D])
    woutT = DI("woutT", [D, D])
    w1T = DI("w1T", [D, 4 * D])
    w2T = DI("w2T", [4 * D, D])
    projW = DI("projW", [16, 64, 256])
    gammaA = DI("gammaA", [128, DC]); betaA = DI("betaA", [128, DC])
    b_outA = DI("b_outA", [128, DC]); b1A = DI("b1A", [128, 32]); b2A = DI("b2A", [128, DC])

    Wq = DI("Wq", [D, D]); Wk = DI("Wk", [D, D]); Wv = DI("Wv", [D, D]); Wo = DI("Wo", [D, D])
    bqF = DI("bqF", [128, DC]); bkF = DI("bkF", [128, DC]); bvF = DI("bvF", [128, DC]); boF = DI("boF", [128, DC])
    ln1g = DI("ln1g", [128, DC]); ln1b = DI("ln1b", [128, DC])
    ln2g = DI("ln2g", [128, DC]); ln2b = DI("ln2b", [128, DC])
    WrD = DI("WrD", [D, 8]); brD = DI("brD", [1, 8])
    ew1 = DI("ew1", [8, D, 4 * D]); ew2 = DI("ew2", [8, 4 * D, D])
    eb1 = DI("eb1", [128, 8 * 32]); eb2 = DI("eb2", [128, 8 * 8])

    qwB = DI("qwB", [D, D]); kwB = DI("kwB", [D, D]); vwB = DI("vwB", [D, D]); owB = DI("owB", [D, D])
    bwmix = DI("bwmix", [D, 16])
    convq = DI("convq", [128, DC, 4]); convk = DI("convk", [128, DC, 4]); convv = DI("convv", [128, DC, 4])
    mixb_in = DI("mixb_in", [1, 8]); mixbias_in = DI("mixbias_in", [1, 8])
    onormW = DI("onormW", [128, 1])

    identB = DI("identB", [128, 128], BF16)
    identF = DI("identF", [128, 128])
    mask_kq = DI("mask_kq", [128, 128], BF16)
    maskA_neg = DI("maskA_neg", [128, 128], BF16)
    maskAT_neg = DI("maskAT_neg", [128, 128], BF16)
    maskTriIncT = DI("maskTriIncT", [128, 128], BF16)
    m512 = [DI(f"m512_{m}", [128, 512], BF16) for m in range(4)]

    SHP = [NSEQ, DC, 128, T]
    s_r = DS("s_r", SHP)
    s_u2 = DS("s_u2", SHP)
    s_xA = DS("s_xA", SHP)
    s_ffn_in = DS("s_ffn_in", SHP)
    s_a = DS("s_a", SHP)
    s_xB = DS("s_xB", SHP)
    s_xC = DS("s_xC", SHP)
    ew1b = nc.dram_tensor("ew1b", [8, D, 4 * D], BF16, kind="Internal")
    ew2b = nc.dram_tensor("ew2b", [8, 4 * D, D], BF16, kind="Internal")

    def dtile(dr, s, t0, t1):
        return dr[s, :, :, t0:t1].rearrange("c p t -> p c t")

    ones_col_b = S("ones_col_b", [128, 1], BF16)
    ones_row_b = S("ones_row_b", [1, 128], BF16)
    ones_row_f = S("ones_row_f", [1, 128], F32)
    sb_identB = S("sb_identB", [128, 128], BF16)
    sb_identF = S("sb_identF", [128, 128], F32)
    sb_mask_kq = S("sb_mask_kq", [128, 128], BF16)
    sb_maskA = S("sb_maskA", [128, 128], BF16)
    sb_maskAT = S("sb_maskAT", [128, 128], BF16)
    sb_maskTIT = S("sb_maskTIT", [128, 128], BF16)
    sb_m512 = [S(f"sb_m512_{m}", [128, 512], BF16) for m in range(4)]

    sb_gA = S("sb_gA", [128, DC], F32); sb_bA = S("sb_bA", [128, DC], F32)
    sb_boutA = S("sb_boutA", [128, DC], F32)
    sb_b1A = S("sb_b1A", [128, 32], F32); sb_b2A = S("sb_b2A", [128, DC], F32)
    sb_bqF = S("sb_bqF", [128, DC], F32); sb_bkF = S("sb_bkF", [128, DC], F32)
    sb_bvF = S("sb_bvF", [128, DC], F32); sb_boF = S("sb_boF", [128, DC], F32)
    sb_ln1g = S("sb_ln1g", [128, DC], F32); sb_ln1b = S("sb_ln1b", [128, DC], F32)
    sb_ln2g = S("sb_ln2g", [128, DC], F32); sb_ln2b = S("sb_ln2b", [128, DC], F32)
    sb_Wr = S("sb_Wr", [128, DC, 8], BF16)
    sb_br = S("sb_br", [1, 8], F32)
    sb_br_b = S("sb_br_b", [1, 8], BF16)
    sb_eb1 = S("sb_eb1", [128, 8 * 32], F32)
    sb_eb2 = S("sb_eb2", [128, 8 * 8], F32)
    sb_convq = S("sb_convq", [128, DC, 4], F32)
    sb_convk = S("sb_convk", [128, DC, 4], F32)
    sb_convv = S("sb_convv", [128, DC, 4], F32)
    sb_mix8a = S("sb_mix8a", [1, 8], F32)
    sb_mixsum = S("sb_mixsum", [1, 8], F32)
    sb_onorm = S("sb_onorm", [128, 1], F32)

    gsem = Sems(nc, es, "g")

    const_pool = {}

    def constf(val):
        if val not in const_pool:
            const_pool[val] = S(f"constf_{len(const_pool)}", [128, 1], F32)
        return const_pool[val]

    for _v in (1e-5, 1e-12, 1e-8):
        constf(_v)

    em = Emitter()

    def _const_loads(e):
        loads = [
            (sb_identB[:, :], identB[:, :]), (sb_identF[:, :], identF[:, :]),
            (sb_mask_kq[:, :], mask_kq[:, :]),
            (sb_maskA[:, :], maskA_neg[:, :]), (sb_maskAT[:, :], maskAT_neg[:, :]),
            (sb_maskTIT[:, :], maskTriIncT[:, :]),
            (sb_m512[0][:, :], m512[0][:, :]), (sb_m512[1][:, :], m512[1][:, :]),
            (sb_m512[2][:, :], m512[2][:, :]), (sb_m512[3][:, :], m512[3][:, :]),
            (sb_gA[:, :], gammaA[:, :]), (sb_bA[:, :], betaA[:, :]),
            (sb_boutA[:, :], b_outA[:, :]), (sb_b1A[:, :], b1A[:, :]), (sb_b2A[:, :], b2A[:, :]),
            (sb_bqF[:, :], bqF[:, :]), (sb_bkF[:, :], bkF[:, :]),
            (sb_bvF[:, :], bvF[:, :]), (sb_boF[:, :], boF[:, :]),
            (sb_ln1g[:, :], ln1g[:, :]), (sb_ln1b[:, :], ln1b[:, :]),
            (sb_ln2g[:, :], ln2g[:, :]), (sb_ln2b[:, :], ln2b[:, :]),
            (sb_br[:, :], brD[:, :]),
            (sb_eb1[:, :], eb1[:, :]), (sb_eb2[:, :], eb2[:, :]),
            (sb_convq[:, :, :], convq[:, :, :]), (sb_convk[:, :, :], convk[:, :, :]),
            (sb_convv[:, :, :], convv[:, :, :]),
            (sb_onorm[:, :], onormW[:, :]),
            (sb_mix8a[:, :], mixb_in[:, :]),
            (sb_mixsum[:, :], mixbias_in[:, :]),
        ]
        for dst, src in loads:
            e.dma_start(dst, src).then_inc(gsem.sem("cl"), 16)
            gsem.bump("cl", 16)

    def _cast_loads(e):
        e.dma_start(sb_Wr[:, :, :], WrD.rearrange("(c p) o -> p c o", p=128)).then_inc(gsem.sem("cw"), 16)
        gsem.bump("cw", 16)

    def _const_init(v):
        v.wait_ge(gsem.sem("cl"), gsem.val("cl"))
        v.wait_ge(gsem.sem("cw"), gsem.val("cw"))
        v.memset(ones_col_b[:, :], 1.0)
        v.memset(ones_row_b[:, :], 1.0)
        v.memset(ones_row_f[:, :], 1.0)
        v.tensor_add(sb_mixsum[:, :], sb_mixsum[:, :], sb_mix8a[:, :])
        v.tensor_copy(sb_br_b[:, :], sb_br[:, :])
        for _v, _t in const_pool.items():
            v.memset(_t[:, :], _v)

    em.on("sync", _const_loads)
    em.on("gpsimd", _cast_loads)
    em.on("vector", _const_init)
    em.replay(nc)

    def emit_moe_casts(em, eids):
        def go(e, eids=tuple(eids)):
            for ei in eids:
                e.dma_start(ew1b[ei, :, :], ew1[ei, :, :]).then_inc(gsem.sem("mcast"), 16)
                e.dma_start(ew2b[ei, :, :], ew2[ei, :, :]).then_inc(gsem.sem("mcast"), 16)
                gsem.bump("mcast", 32)
        em.on("gpsimd", go)

    ln_sems = Sems(nc, es, "LN")

    class LNHelper:

        def __init__(self, tag, stack, ext_xb=None, ext_xsq=None):
            Sl = lambda n, sh, dt: stack.enter_context(nc.sbuf_tensor(n, sh, dt))
            Pl = lambda n, sh: stack.enter_context(nc.psum_tensor(n, sh, F32))
            self.ps_stats = Pl(f"lnps_{tag}", [1, 2 * TT])
            self.ps_ab = Pl(f"lnab_{tag}", [128, 2 * TT])
            if ext_xb is None:
                t = Sl(f"lnxb_{tag}", [128, DC * TT], BF16)
                ext_xb = (t[:, :], lambda c, t=t: t[:, c * TT:(c + 1) * TT])
            if ext_xsq is None:
                t2 = Sl(f"lnxq_{tag}", [128, DC * TT], BF16)
                ext_xsq = (t2[:, :], lambda c, t2=t2: t2[:, c * TT:(c + 1) * TT])
            self.xb_full, self.xb_c = ext_xb
            self.xsq_full, self.xsq_c = ext_xsq
            self.rows = Sl(f"lnrw_{tag}", [1, 2 * TT], F32)
            self.mu = Sl(f"lnmu_{tag}", [1, TT], F32)
            self.var = Sl(f"lnvr_{tag}", [1, TT], F32)
            self.sem = ln_sems

        def emit(self, em, xin, rt, store_bf=None, gain=None, bias=None,
                 rt_ready_wait=None, xin_wait=None):
            sem = self.sem
            c_stat = sem.val("stat")
            c_var = sem.val("var")
            c_rstd = sem.val("rstd")
            c_app = sem.val("app")
            c_aff = sem.val("aff")

            def v_cast(v):
                if xin_wait is not None:
                    xin_wait(v)
                v.wait_ge(sem.sem("stat"), c_stat)
                v.tensor_copy(self.xb_full, xin.rearrange("p c t -> p (c t)")
                              ).then_inc(sem.sem("cast"), 1)
            em.on("vector", v_cast)
            n_cast = sem.bump("cast")

            def s_sq(s):
                if xin_wait is not None:
                    xin_wait(s)
                s.wait_ge(sem.sem("stat"), c_stat)
                s.square(self.xsq_full, xin.rearrange("p c t -> p (c t)")
                         ).then_inc(sem.sem("sq"), 1)
            em.on("scalar", s_sq)
            n_sq = sem.bump("sq")

            def t_stats(t):
                t.wait_ge(sem.sem("cast"), n_cast)
                t.wait_ge(sem.sem("sq"), n_sq)
                t.wait_ge(sem.sem("var"), c_var)
                for c in range(DC):
                    t.matmul(self.ps_stats[0:1, 0:TT], ones_col_b[:, 0:1],
                             self.xb_c(c), start=(c == 0), stop=(c == DC - 1))
                for c in range(DC):
                    i = t.matmul(self.ps_stats[0:1, TT:2 * TT], ones_col_b[:, 0:1],
                                 self.xsq_c(c), start=(c == 0), stop=(c == DC - 1))
                i.then_inc(sem.sem("stat"), 1)
            em.on("tensor", t_stats)
            n_stat = sem.bump("stat")

            def v_rows(v):
                v.wait_ge(sem.sem("stat"), n_stat)
                v.wait_ge(sem.sem("rstd"), c_rstd)
                v.tensor_single_scalar(self.mu[0:1, :], self.ps_stats[0:1, 0:TT], 1.0 / D, ALU.mult)
                v.tensor_single_scalar(self.var[0:1, :], self.ps_stats[0:1, TT:2 * TT], 1.0 / D, ALU.mult)
                v.tensor_mul(self.rows[0:1, 0:TT], self.mu[0:1, :], self.mu[0:1, :])
                v.tensor_sub(self.var[0:1, :], self.var[0:1, :], self.rows[0:1, 0:TT]
                             ).then_inc(sem.sem("var"), 1)
            em.on("vector", v_rows)
            n_var = sem.bump("var")

            def s_rstd(s):
                s.wait_ge(sem.sem("var"), n_var)
                s.activation(self.rows[0:1, 0:TT], self.var[0:1, :], AF.Ln, bias=constf(EPS_LN)[0:1, 0:1])
                s.activation(self.rows[0:1, 0:TT], self.rows[0:1, 0:TT], AF.Exp, scale=-0.5
                             ).then_inc(sem.sem("rstd"), 1)
            em.on("scalar", s_rstd)
            n_rstd = sem.bump("rstd")

            def v_brow(v):
                v.wait_ge(sem.sem("rstd"), n_rstd)
                v.tensor_mul(self.rows[0:1, TT:2 * TT], self.mu[0:1, :], self.rows[0:1, 0:TT]
                             ).then_inc(sem.sem("brow"), 1)
            em.on("vector", v_brow)
            n_brow = sem.bump("brow")

            def t_bc(t):
                t.wait_ge(sem.sem("brow"), n_brow)
                t.wait_ge(sem.sem("app"), c_app)
                t.matmul(self.ps_ab[:, 0:TT], ones_row_f[0:1, :], self.rows[0:1, 0:TT],
                         start=True, stop=True)
                t.matmul(self.ps_ab[:, TT:2 * TT], ones_row_f[0:1, :], self.rows[0:1, TT:2 * TT],
                         start=True, stop=True).then_inc(sem.sem("bc"), 1)
            em.on("tensor", t_bc)
            n_bc = sem.bump("bc")

            fin_on_vector = (gain is None)

            def v_apply(v):
                v.wait_ge(sem.sem("bc"), n_bc)
                if gain is not None:
                    v.wait_ge(sem.sem("aff"), c_aff)
                if rt_ready_wait is not None:
                    rt_ready_wait(v)
                last = None
                for c in range(DC):
                    v.tensor_mul(rt[:, c, :], xin[:, c, :], self.ps_ab[:, 0:TT])
                    last = v.tensor_sub(rt[:, c, :], rt[:, c, :], self.ps_ab[:, TT:2 * TT])
                    if fin_on_vector and store_bf is not None:
                        last = v.tensor_copy(store_bf(c), rt[:, c, :])
                last.then_inc(sem.sem("app"), 1)
            em.on("vector", v_apply)
            n_app = sem.bump("app")
            if gain is None:
                return (sem.sem("app"), n_app)

            if gain is not None:
                def s_aff(s):
                    s.wait_ge(sem.sem("app"), n_app)
                    last = None
                    for c in range(DC):
                        last = s.activation(rt[:, c, :], rt[:, c, :], AF.Identity,
                                            bias=bias[:, c:c + 1], scale=gain[:, c:c + 1])
                    last.then_inc(sem.sem("aff"), 1)
                em.on("scalar", s_aff)
                n_aff = sem.bump("aff")
                if store_bf is None:
                    return (sem.sem("aff"), n_aff)
                def v_bf(v):
                    v.wait_ge(sem.sem("aff"), n_aff)
                    last = None
                    for c in range(DC):
                        last = v.tensor_copy(store_bf(c), rt[:, c, :])
                    last.then_inc(sem.sem("bf"), 1)
                em.on("vector", v_bf)
                n_bf = sem.bump("bf")
                return (sem.sem("bf"), n_bf)

    def ln_phase(tag, src_dram, dst_dram, xbf_sb, gain=None, bias=None):
        pes = ExitStack()
        ph = Sems(nc, es, tag)
        em = Emitter()
        ln = LNHelper(tag, pes)
        Sl = lambda n, sh, dt: pes.enter_context(nc.sbuf_tensor(n, sh, dt))
        xin = [Sl(f"{tag}_xin{j}", [128, DC, TT], F32) for j in range(2)]
        rt = [Sl(f"{tag}_rt{j}", [128, DC, TT], F32) for j in range(2)] \
            if dst_dram is not None else xin
        tiles = [(s, tt) for s in range(NSEQ) for tt in range(NTT)]
        done_handles = []
        for i, (s, tt) in enumerate(tiles):
            j = i % 2
            def dma_in(e, s=s, tt=tt, i=i, j=j, done_handles=done_handles):
                if i >= 2:
                    h = done_handles[i - 2]
                    e.wait_ge(h[0], h[1])
                e.dma_start(xin[j][:, :, :], dtile(src_dram, s, tt * TT, (tt + 1) * TT)
                            ).then_inc(ph.sem("ld"), 16)
            em.on("sync", dma_in)
            n_ld = ph.bump("ld", 16)

            def xin_wait(eng, n_ld=n_ld):
                eng.wait_ge(ph.sem("ld"), n_ld)

            c_st_r = ph.val("st")

            def rt_wait(eng, i=i, c_st_r=c_st_r):
                if i >= 2 and dst_dram is not None:
                    eng.wait_ge(ph.sem("st"), c_st_r - 16)

            store_bf = None
            if xbf_sb is not None:
                def store_bf(c, s=s, tt=tt):
                    return xbfA_ap(s, c, tt * TT, (tt + 1) * TT)

            h = ln.emit(em, xin[j][:, :, :], rt[j][:, :, :], store_bf=store_bf,
                        gain=gain, bias=bias, xin_wait=xin_wait, rt_ready_wait=rt_wait)
            done_handles.append(h)

            def dma_out(e, j=j, s=s, tt=tt, h=h):
                e.wait_ge(h[0], h[1])
                if dst_dram is not None:
                    e.dma_start(dtile(dst_dram, s, tt * TT, (tt + 1) * TT), rt[j][:, :, :]
                                ).then_inc(ph.sem("st"), 16)
            em.on("sync", dma_out)
            if dst_dram is not None:
                ph.bump("st", 16)
        def fin(e):
            if dst_dram is not None:
                e.wait_ge(ph.sem("st"), ph.val("st"))
        em.on("sync", fin)
        em.replay(nc)
        pes.close()

    XA = [None, None]
    XB = [None, None]

    def openA(tag):
        st = ExitStack()
        XA[0] = st
        XA[1] = st.enter_context(nc.sbuf_tensor(f"xa_{tag}", [128, NSEQ * DC * T], BF16,
                                                side="left"))

    def closeA():
        XA[0].close(); XA[0] = XA[1] = None

    def openB(tag):
        st = ExitStack()
        XB[0] = st
        XB[1] = st.enter_context(nc.sbuf_tensor(f"xb_{tag}", [128, NSEQ * DC * T], BF16,
                                                side="right"))

    def closeB():
        XB[0].close(); XB[0] = XB[1] = None

    def xbfA_ap(s, c, t0, t1):
        base = (s * DC + c) * T
        return XA[1][:, base + t0: base + t1]

    if not _SKIP_EARLY:
        openA("a1")
        with nc.named_scope("aln"):
            ln_phase("aln", x_in, s_r, "XA")

    def xbfB_ap(s, c, t0, t1):
        base = (s * DC + c) * T
        return XB[1][:, base + t0: base + t1]

    def phase_a_attn():
        pes = ExitStack()
        Sl = lambda n, sh, dt: pes.enter_context(nc.sbuf_tensor(n, sh, dt))
        Pl = lambda n, sh: pes.enter_context(nc.psum_tensor(n, sh, F32))
        ph = Sems(nc, es, "aat")
        em = Emitter()
        NH = 16
        sb_proj = Sl("sb_proj", [128, 16, 256], BF16)

        def proj_load(e):
            e.dma_start(sb_proj[0:64, :, :], projW.rearrange("h d f -> d h f")).then_inc(ph.sem("pl"), 16)
            e.dma_start(sb_proj[64:128, :, :], projW.rearrange("h d f -> d h f")).then_inc(ph.sem("pl"), 16)
        em.on("gpsimd", proj_load)
        n_pl = ph.bump("pl", 32)

        def proj_gate(t, n_pl=n_pl):
            t.wait_ge(ph.sem("pl"), n_pl)
        em.on("tensor", proj_gate)

        wqk = [Sl(f"aat_wqk{j}", [128, DC, 128], BF16) for j in range(2)]
        wv = [Sl(f"aat_wv{j}", [128, DC, 64], BF16) for j in range(2)]
        qkT = Sl("aat_qkT", [128, T], BF16)
        v_aug = Sl("aat_vaug", [128, 16, 65], BF16)
        qfT = Sl("aat_qfT", [128, 2, T], BF16)
        kfN = Sl("aat_kfN", [128, 16, 256], BF16)
        kv_sb = Sl("aat_kv", [128, 2, 65], BF16)
        elu_m = Sl("aat_em", [128, 512], F32)
        elu_r = Sl("aat_er", [128, 512], F32)
        elu_e = Sl("aat_ee", [128, 512], F32)
        zr_t = Sl("aat_zrt", [1, 512], F32)
        zr = Sl("aat_zr", [1, 512], F32)
        zb_sb = Sl("aat_zb", [128, 512], F32)
        ps_qk = [Pl("aat_psqk0", [128, 512])]
        ps_v = Pl("aat_psv", [128, 64])
        ps_f = [Pl(f"aat_psf{j}", [128, 512]) for j in range(2)]
        ps_kv = Pl("aat_pskv", [128, 65])
        ps_at = Pl("aat_psat", [128, 512])
        ps_z = Pl("aat_psz", [1, 512])
        ps_zb = Pl("aat_pszb", [128, 512])

        def vaug_init(v):
            v.memset(v_aug[:, :, :], 0.0)
            for nt in range(16):
                v.memset(v_aug[:, nt, 64:65], 1.0)
            v.nop().then_inc(ph.sem("vinit"), 1)
        em.on("vector", vaug_init)
        n_vinit = ph.bump("vinit")

        base_wf = ph.val("wfree")
        for h in range(NH):
            j = h % 2
            def wload(e, h=h, j=j, base_wf=base_wf):
                if h >= 2:
                    e.wait_ge(ph.sem("wfree"), base_wf + h - 1)
                e.dma_start(wqk[j][:, :, 0:64],
                            wqkvT[:, 64 * h:64 * h + 64].rearrange("(c p) o -> p c o", p=128)
                            ).then_inc(ph.sem("wld"), 16)
                e.dma_start(wqk[j][:, :, 64:128],
                            wqkvT[:, D + 64 * h:D + 64 * h + 64].rearrange("(c p) o -> p c o", p=128)
                            ).then_inc(ph.sem("wld"), 16)
                e.dma_start(wv[j][:, :, :],
                            wqkvT[:, 2 * D + 64 * h:2 * D + 64 * h + 64].rearrange("(c p) o -> p c o", p=128)
                            ).then_inc(ph.sem("wld"), 16)
            em.on("gpsimd", wload)
            n_wld = ph.bump("wld", 48)

            for s in range(NSEQ):
                u = h * NSEQ + s
                c_mmf_g = ph.val("mmf")
                for tt in range(NTT):
                    b = 0
                    c_evq = ph.val("evq")

                    def t_qk(t, h=h, s=s, tt=tt, b=b, n_wld=n_wld, c_evq=c_evq, j=j):
                        t.wait_ge(ph.sem("wld"), n_wld)
                        if c_evq >= 1:
                            t.wait_ge(ph.sem("evq"), c_evq)
                        for c in range(DC):
                            i = t.matmul(ps_qk[b][:, :], wqk[j][:, c, :],
                                         xbfA_ap(s, c, tt * TT, (tt + 1) * TT),
                                         start=(c == 0), stop=(c == DC - 1))
                        i.then_inc(ph.sem("mmq"), 1)
                    em.on("tensor", t_qk)
                    n_mmq = ph.bump("mmq")

                    def v_evq(v, tt=tt, b=b, n_mmq=n_mmq, c_mmf_g=c_mmf_g):
                        v.wait_ge(ph.sem("mmq"), n_mmq)
                        v.wait_ge(ph.sem("mmf"), c_mmf_g)
                        v.tensor_copy(qkT[:, tt * TT:(tt + 1) * TT], ps_qk[b][:, :]
                                      ).then_inc(ph.sem("evq"), 1)
                    em.on("vector", v_evq)
                    ph.bump("evq")
                n_evq_done = ph.val("evq")

                c_mkv_v = ph.val("mkv")
                for nt in range(16):
                    c_evv = ph.val("evv")

                    def t_v(t, s=s, nt=nt, c_evv=c_evv, j=j, u=u):
                        if nt == 0 and u == 0:
                            t.wait_ge(ph.sem("vinit"), n_vinit)
                        if c_evv >= 1:
                            t.wait_ge(ph.sem("evv"), c_evv)
                        for c in range(DC):
                            i = t.matmul(ps_v[:, :], xbfA_ap(s, c, nt * 128, (nt + 1) * 128),
                                         wv[j][:, c, :], start=(c == 0), stop=(c == DC - 1))
                        i.then_inc(ph.sem("mmv"), 1)
                    em.on("tensor", t_v)
                    n_mmv = ph.bump("mmv")

                    def v_evv(v, nt=nt, n_mmv=n_mmv, c_mkv_v=c_mkv_v):
                        v.wait_ge(ph.sem("mmv"), n_mmv)
                        v.wait_ge(ph.sem("mkv"), c_mkv_v)
                        v.tensor_copy(v_aug[:, nt, 0:64], ps_v[:, :]).then_inc(ph.sem("evv"), 1)
                    em.on("vector", v_evv)
                    ph.bump("evv")
                if s == NSEQ - 1:
                    def t_wfree(t, n=ph.val("mmv")):
                        t.nop().then_inc(ph.sem("wfree"), 1)
                    em.on("tensor", t_wfree)
                    ph.bump("wfree")

                def emit_elu(ps, n_mm, out_ap, width, out_guard=None):
                    c_els = ph.val("els")
                    c_elf = ph.val("elf")

                    def v_1(v):
                        v.wait_ge(ph.sem("mmf"), n_mm)
                        v.wait_ge(ph.sem("els"), c_els)
                        v.tensor_single_scalar(elu_m[:, 0:width], ps, 0.0, ALU.min)
                        v.tensor_sub(elu_r[:, 0:width], ps, elu_m[:, 0:width]
                                     ).then_inc(ph.sem("elv1"), 1)
                    em.on("vector", v_1)
                    n1 = ph.bump("elv1")

                    def s_1(sc):
                        sc.wait_ge(ph.sem("elv1"), n1)
                        sc.wait_ge(ph.sem("elf"), c_elf)
                        sc.activation(elu_e[:, 0:width], elu_m[:, 0:width], AF.Exp
                                      ).then_inc(ph.sem("els"), 1)
                    em.on("scalar", s_1)
                    n2 = ph.bump("els")

                    def v_2(v):
                        v.wait_ge(ph.sem("els"), n2)
                        if out_guard is not None and out_guard[1] >= 1:
                            v.wait_ge(out_guard[0], out_guard[1])
                        v.tensor_add(out_ap, elu_e[:, 0:width], elu_r[:, 0:width]
                                     ).then_inc(ph.sem("elf"), 1)
                    em.on("vector", v_2)
                    return ph.bump("elf")

                c_mat_g = ph.val("mat")
                for fc in range(2):
                    for tt in range(NTT):
                        b = tt % 2
                        c_elf = ph.val("elf")

                        def t_qf(t, h=h, fc=fc, tt=tt, b=b, c_elf=c_elf, n_evq_done=n_evq_done):
                            t.wait_ge(ph.sem("evq"), n_evq_done)
                            if c_elf >= 2:
                                t.wait_ge(ph.sem("elf"), c_elf - 1)
                            t.matmul(ps_f[b][:, :], sb_proj[0:64, h, fc * 128:(fc + 1) * 128],
                                     qkT[0:64, tt * TT:(tt + 1) * TT], start=True, stop=True
                                     ).then_inc(ph.sem("mmf"), 1)
                        em.on("tensor", t_qf)
                        n_mmf = ph.bump("mmf")
                        emit_elu(ps_f[b][:, :], n_mmf, qfT[:, fc, tt * TT:(tt + 1) * TT], 512,
                                 out_guard=(ph.sem("mat"), c_mat_g))

                c_mkv_g = ph.val("mkv")
                for nt in range(16):
                    b = nt % 2
                    c_elf = ph.val("elf")

                    def t_kf(t, h=h, nt=nt, b=b, c_elf=c_elf, n_evq_done=n_evq_done):
                        t.wait_ge(ph.sem("evq"), n_evq_done)
                        if c_elf >= 2:
                            t.wait_ge(ph.sem("elf"), c_elf - 1)
                        t.matmul(ps_f[b][:, 0:256], qkT[64:128, nt * 128:(nt + 1) * 128],
                                 sb_proj[64:128, h, :], start=True, stop=True
                                 ).then_inc(ph.sem("mmf"), 1)
                    em.on("tensor", t_kf)
                    n_mmf = ph.bump("mmf")
                    emit_elu(ps_f[b][:, 0:256], n_mmf, kfN[:, nt, :], 256,
                             out_guard=(ph.sem("mkv"), c_mkv_g))
                n_kf_done = ph.val("elf")

                for fc in range(2):
                    c_ekv = ph.val("ekv")

                    def t_kv(t, fc=fc, n_kf=n_kf_done, c_ekv=c_ekv):
                        t.wait_ge(ph.sem("elf"), n_kf)
                        if c_ekv >= 1:
                            t.wait_ge(ph.sem("ekv"), c_ekv)
                        for nt in range(16):
                            i = t.matmul(ps_kv[:, :], kfN[:, nt, fc * 128:(fc + 1) * 128],
                                         v_aug[:, nt, :], start=(nt == 0), stop=(nt == 15))
                        i.then_inc(ph.sem("mkv"), 1)
                    em.on("tensor", t_kv)
                    n_mkv = ph.bump("mkv")

                    def v_ekv(v, fc=fc, n_mkv=n_mkv, c_mat_k=ph.val("mat")):
                        v.wait_ge(ph.sem("mkv"), n_mkv)
                        v.wait_ge(ph.sem("mat"), c_mat_k)
                        v.tensor_copy(kv_sb[:, fc, :], ps_kv[:, :]).then_inc(ph.sem("ekv"), 1)
                    em.on("vector", v_ekv)
                    ph.bump("ekv")
                n_kv_done = ph.val("ekv")

                b0 = (h % 2) * 64
                for tt in range(NTT):
                    c_zf = ph.val("zfin")

                    def t_at(t, tt=tt, n_kv=n_kv_done, c_zf=c_zf, b0=b0):
                        t.wait_ge(ph.sem("ekv"), n_kv)
                        if c_zf >= 1:
                            t.wait_ge(ph.sem("zfin"), c_zf)
                        for fc in range(2):
                            t.matmul(ps_at[b0:b0 + 64, :], kv_sb[:, fc, 0:64],
                                     qfT[:, fc, tt * TT:(tt + 1) * TT],
                                     start=(fc == 0), stop=(fc == 1), tile_position=(0, b0))
                        for fc in range(2):
                            i = t.matmul(ps_z[0:1, :], kv_sb[:, fc, 64:65],
                                         qfT[:, fc, tt * TT:(tt + 1) * TT],
                                         start=(fc == 0), stop=(fc == 1))
                        i.then_inc(ph.sem("mat"), 1)
                    em.on("tensor", t_at)
                    n_mat = ph.bump("mat")

                    def v_z(v, n_mat=n_mat):
                        v.wait_ge(ph.sem("mat"), n_mat)
                        v.tensor_single_scalar(zr_t[0:1, :], ps_z[0:1, :], 1e-8, ALU.add)
                        v.reciprocal(zr[0:1, :], zr_t[0:1, :]).then_inc(ph.sem("vz"), 1)
                    em.on("vector", v_z)
                    n_vz = ph.bump("vz")

                    def t_zb(t, n_vz=n_vz, c_zf=c_zf, b0=b0):
                        t.wait_ge(ph.sem("vz"), n_vz)
                        if c_zf >= 1:
                            t.wait_ge(ph.sem("zfin"), c_zf)
                        t.matmul(ps_zb[b0:b0 + 64, :], ones_row_f[0:1, 0:64], zr[0:1, :],
                                 start=True, stop=True, tile_position=(0, b0)
                                 ).then_inc(ph.sem("mzb"), 1)
                    em.on("tensor", t_zb)
                    n_mzb = ph.bump("mzb")

                    def v_fin(v, h=h, s=s, tt=tt, n_mzb=n_mzb, b0=b0):
                        v.wait_ge(ph.sem("mzb"), n_mzb)
                        v.tensor_copy(zb_sb[b0:b0 + 64, :], ps_zb[b0:b0 + 64, :])
                        c = h // 2
                        dst = XB[1][b0:b0 + 64,
                                   (s * DC + c) * T + tt * TT:(s * DC + c) * T + (tt + 1) * TT]
                        v.tensor_mul(dst, ps_at[b0:b0 + 64, :], zb_sb[b0:b0 + 64, :]
                                     ).then_inc(ph.sem("zfin"), 1)
                    em.on("vector", v_fin)
                    ph.bump("zfin")

        emit_moe_casts(em, range(0, 3))
        em.replay(nc)
        pes.close()

    if not _SKIP_EARLY:
        openB("b1")
        with nc.named_scope("aat"):
            phase_a_attn()
        closeA()

    def phase_a_out():
        pes = ExitStack()
        Sl = lambda n, sh, dt: pes.enter_context(nc.sbuf_tensor(n, sh, dt))
        Pl = lambda n, sh: pes.enter_context(nc.psum_tensor(n, sh, F32))
        ph = Sems(nc, es, "aou")
        em = Emitter()
        lnM = LNHelper("m", pes)
        wout_sb = Sl("aou_wout", [128, DC, D], BF16)
        u1 = [Sl("aou_u1_0", [128, DC, TT], F32)]
        u2 = u1
        ps_u = [Pl(f"aou_ps{j}", [128, 512]) for j in range(2)]

        def wload(e):
            e.dma_start(wout_sb[:, :, :], woutT.rearrange("(c p) o -> p c o", p=128)
                        ).then_inc(ph.sem("wld"), 16)
        em.on("gpsimd", wload)
        n_wld_a = ph.bump("wld", 16)

        tiles = [(s, tt) for s in range(NSEQ) for tt in range(NTT)]
        done_handles = []
        for i, (s, tt) in enumerate(tiles):
            j = 0
            for o in range(DC):
                b = o % 2
                c_ev = ph.val("ev")

                def t_mm(t, s=s, tt=tt, o=o, b=b, c_ev=c_ev, i=i, n_wld_a=n_wld_a):
                    t.wait_ge(ph.sem("wld"), n_wld_a)
                    if c_ev >= 2:
                        t.wait_ge(ph.sem("ev"), c_ev - 1)
                    for c in range(DC):
                        k = t.matmul(ps_u[b][:, :], wout_sb[:, c, o * 128:(o + 1) * 128],
                                     xbfB_ap(s, c, tt * TT, (tt + 1) * TT),
                                     start=(c == 0), stop=(c == DC - 1))
                    k.then_inc(ph.sem("mm"), 1)
                em.on("tensor", t_mm)
                n_mm = ph.bump("mm")

                def s_ev(sc, o=o, b=b, j=j, n_mm=n_mm, i=i, done_handles=done_handles,
                         c_st=ph.val("st")):
                    sc.wait_ge(ph.sem("mm"), n_mm)
                    if o == 0 and i >= 1:
                        h0 = done_handles[i - 1]
                        sc.wait_ge(h0[0], h0[1])
                        sc.wait_ge(ph.sem("st"), c_st)
                    sc.activation(u1[j][:, o, :], ps_u[b][:, :], AF.Identity,
                                  bias=sb_boutA[:, o:o + 1]).then_inc(ph.sem("ev"), 1)
                em.on("scalar", s_ev)
                ph.bump("ev")
            n_ev = ph.val("ev")

            def dma_racc(e, s=s, tt=tt, j=j, n_ev=n_ev):
                e.wait_ge(ph.sem("ev"), n_ev)
                e.dma_start(u1[j][:, :, :], dtile(s_r, s, tt * TT, (tt + 1) * TT),
                            accum_op=ALU.add).then_inc(ph.sem("acc"), 16)
            em.on("gpsimd", dma_racc)
            n_acc = ph.bump("acc", 16)

            def xin_wait(eng, n_acc=n_acc):
                eng.wait_ge(ph.sem("acc"), n_acc)

            def rt_wait(eng, i=i, c_st=ph.val("st")):
                if i >= 1:
                    eng.wait_ge(ph.sem("st"), c_st)

            hdl = lnM.emit(em, u1[j][:, :, :], u2[j][:, :, :], store_bf=None,
                           gain=sb_gA, bias=sb_bA, xin_wait=xin_wait, rt_ready_wait=rt_wait)
            done_handles.append(hdl)

            def dma_out(e, j=j, s=s, tt=tt, hdl=hdl):
                e.wait_ge(hdl[0], hdl[1])
                e.dma_start(dtile(s_u2, s, tt * TT, (tt + 1) * TT), u2[j][:, :, :]
                            ).then_inc(ph.sem("st"), 16)
            em.on("sync", dma_out)
            ph.bump("st", 16)

        def fin(e):
            e.wait_ge(ph.sem("st"), ph.val("st"))
        em.on("sync", fin)
        emit_moe_casts(em, range(3, 6))
        em.replay(nc)
        pes.close()

    if not _SKIP_EARLY:
        with nc.named_scope("aou"):
            phase_a_out()
        closeB()

    def phase_a_ffn():
        pes = ExitStack()
        Sl = lambda n, sh, dt: pes.enter_context(nc.sbuf_tensor(n, sh, dt))
        Pl = lambda n, sh: pes.enter_context(nc.psum_tensor(n, sh, F32))
        ph = Sems(nc, es, "aff")
        em = Emitter()
        midsb = Sl("aff_mid", [128, 32, TT], BF16)
        u2bf = Sl("aff_u2b", [128, DC, TT], BF16)
        u3 = Sl("aff_u3", [128, DC, TT], F32)
        ps_mid = [Pl(f"aff_pm{j}", [128, 512]) for j in range(2)]
        ps_x2 = [Pl(f"aff_px{j}", [128, 512]) for j in range(2)]
        lnF = LNHelper("f", pes,
                       ext_xb=(midsb[:, 0:DC, :].rearrange("p a b -> p (a b)"),
                               lambda c: midsb[:, c, :]),
                       ext_xsq=(midsb[:, DC:2 * DC, :].rearrange("p a b -> p (a b)"),
                                lambda c: midsb[:, DC + c, :]))

        def w1v(c, f):
            return XA[1][:, c * 4096 + f * 128: c * 4096 + (f + 1) * 128]

        def w2v(fr, o):
            return XB[1][:, fr * 1024 + o * 128: fr * 1024 + (o + 1) * 128]

        def wload(e):
            e.dma_start(XA[1][:, :].rearrange("p (c f) -> p c f", f=4096),
                        w1T.rearrange("(c p) f -> p c f", p=128)).then_inc(ph.sem("wld"), 16)
            e.dma_start(XB[1][:, :].rearrange("p (c o) -> p c o", o=1024),
                        w2T.rearrange("(c p) o -> p c o", p=128)).then_inc(ph.sem("wld"), 16)
        em.on("gpsimd", wload)
        n_wld_f = ph.bump("wld", 32)

        tiles = [(s, tt) for s in range(NSEQ) for tt in range(NTT)]
        done_handles = []
        for i, (s, tt) in enumerate(tiles):
            c_mid_prev = ph.val("mmid")

            def dma_u2(e, s=s, tt=tt, c_mid_prev=c_mid_prev):
                if c_mid_prev:
                    e.wait_ge(ph.sem("mmid"), c_mid_prev)
                e.dma_start(u2bf[:, :, :], dtile(s_u2, s, tt * TT, (tt + 1) * TT)
                            ).then_inc(ph.sem("ldu"), 16)
            em.on("gpsimd", dma_u2)
            n_ldu = ph.bump("ldu", 16)

            c_stat_g = ln_sems.val("stat")
            for f in range(32):
                b = f % 2
                c_ev = ph.val("evm")

                def t_mid(t, f=f, b=b, c_ev=c_ev, n_ldu=n_ldu, n_wld_f=n_wld_f):
                    t.wait_ge(ph.sem("wld"), n_wld_f)
                    t.wait_ge(ph.sem("ldu"), n_ldu)
                    if c_ev >= 2:
                        t.wait_ge(ph.sem("evm"), c_ev - 1)
                    for c in range(DC):
                        k = t.matmul(ps_mid[b][:, :], w1v(c, f), u2bf[:, c, :],
                                     start=(c == 0), stop=(c == DC - 1))
                    k.then_inc(ph.sem("mmid"), 1)
                em.on("tensor", t_mid)
                n_mm = ph.bump("mmid")

                def s_gelu(sc, f=f, b=b, n_mm=n_mm, c_stat_g=c_stat_g, i=i):
                    sc.wait_ge(ph.sem("mmid"), n_mm)
                    if f < 2 * DC and i > 0:
                        sc.wait_ge(ln_sems.sem("stat"), c_stat_g)
                    sc.activation(midsb[:, f, :], ps_mid[b][:, :], AF.Gelu,
                                  bias=sb_b1A[:, f:f + 1]).then_inc(ph.sem("evm"), 1)
                em.on("scalar", s_gelu)
                ph.bump("evm")
            n_evm = ph.val("evm")

            for o in range(DC):
                b = o % 2
                c_ev = ph.val("evx")

                def t_x2(t, o=o, b=b, c_ev=c_ev, n_evm=n_evm):
                    t.wait_ge(ph.sem("evm"), n_evm)
                    if c_ev >= 2:
                        t.wait_ge(ph.sem("evx"), c_ev - 1)
                    for fr in range(32):
                        k = t.matmul(ps_x2[b][:, :], w2v(fr, o), midsb[:, fr, :],
                                     start=(fr == 0), stop=(fr == 31))
                    k.then_inc(ph.sem("mx2"), 1)
                em.on("tensor", t_x2)
                n_mx = ph.bump("mx2")

                def s_evx(sc, o=o, b=b, n_mx=n_mx, i=i, done_handles=done_handles,
                          c_st=ph.val("st")):
                    sc.wait_ge(ph.sem("mx2"), n_mx)
                    if o == 0 and i >= 1:
                        h0 = done_handles[i - 1]
                        sc.wait_ge(h0[0], h0[1])
                        sc.wait_ge(ph.sem("st"), c_st)
                    sc.activation(u3[:, o, :], ps_x2[b][:, :], AF.Identity,
                                  bias=sb_b2A[:, o:o + 1]).then_inc(ph.sem("evx"), 1)
                em.on("scalar", s_evx)
                ph.bump("evx")
            n_evx = ph.val("evx")

            def dma_uacc(e, s=s, tt=tt, n_evx=n_evx):
                e.wait_ge(ph.sem("evx"), n_evx)
                e.dma_start(u3[:, :, :], dtile(s_u2, s, tt * TT, (tt + 1) * TT),
                            accum_op=ALU.add).then_inc(ph.sem("acc"), 16)
            em.on("gpsimd", dma_uacc)
            n_acc = ph.bump("acc", 16)

            def xin_wait(eng, n_acc=n_acc):
                eng.wait_ge(ph.sem("acc"), n_acc)

            hdl = lnF.emit(em, u3[:, :, :], u3[:, :, :], store_bf=None,
                           gain=sb_gA, bias=sb_bA, xin_wait=xin_wait)
            done_handles.append(hdl)

            def dma_xacc(e, s=s, tt=tt, hdl=hdl):
                e.wait_ge(hdl[0], hdl[1])
                e.dma_start(u3[:, :, :],
                            x_in[s, :, :, tt * TT:(tt + 1) * TT].rearrange("c p t -> p c t"),
                            accum_op=ALU.add).then_inc(ph.sem("acc2"), 16)
            em.on("gpsimd", dma_xacc)
            n_acc2 = ph.bump("acc2", 16)

            def dma_st(e, s=s, tt=tt, n_acc2=n_acc2):
                e.wait_ge(ph.sem("acc2"), n_acc2)
                e.dma_start(dtile(s_xA, s, tt * TT, (tt + 1) * TT), u3[:, :, :]
                            ).then_inc(ph.sem("st"), 16)
            em.on("sync", dma_st)
            ph.bump("st", 16)

        def fin(e):
            e.wait_ge(ph.sem("st"), ph.val("st"))
        em.on("sync", fin)
        emit_moe_casts(em, range(6, 8))
        em.replay(nc)
        pes.close()

    if not _SKIP_EARLY:
        openA("a2"); openB("b2")
        with nc.named_scope("aff"):
            phase_a_ffn()
        closeA(); closeB()

    def phase_f_ln(tag, src_x):
        pes = ExitStack()
        Sl = lambda n, sh, dt: pes.enter_context(nc.sbuf_tensor(n, sh, dt))
        ph = Sems(nc, es, tag)
        em = Emitter()
        ln = LNHelper(tag, pes)
        xin = Sl(f"{tag}_xin", [128, DC, TT], F32)
        r2b = Sl(f"{tag}_r2", [128, DC, TT], F32)
        tiles = [(s, tt) for s in range(NSEQ) for tt in range(NTT)]
        h_prev = None
        for i, (s, tt) in enumerate(tiles):
            def dma_in(e, s=s, tt=tt, i=i, h_prev=h_prev):
                if h_prev is not None:
                    e.wait_ge(h_prev[0], h_prev[1])
                e.dma_start(xin[:, :, :], dtile(src_x, s, tt * TT, (tt + 1) * TT)
                            ).then_inc(ph.sem("ld"), 16)
            em.on("sync", dma_in)
            n_ld = ph.bump("ld", 16)

            def xin_wait(eng, n_ld=n_ld):
                eng.wait_ge(ph.sem("ld"), n_ld)

            def rt_wait(eng, i=i, h_prev=h_prev, c_st=ph.val("st")):
                if i >= 1:
                    eng.wait_ge(ph.sem("st"), c_st)
                if h_prev is not None:
                    eng.wait_ge(h_prev[0], h_prev[1])

            h1 = ln.emit(em, xin[:, :, :], r2b[:, :, :], store_bf=None,
                         xin_wait=xin_wait, rt_ready_wait=rt_wait)

            def dma_r2(e, s=s, tt=tt, h1=h1):
                e.wait_ge(h1[0], h1[1])
                e.dma_start(dtile(s_ffn_in, s, tt * TT, (tt + 1) * TT), r2b[:, :, :]
                            ).then_inc(ph.sem("st"), 16)
            em.on("sync", dma_r2)
            n_st = ph.bump("st", 16)

            def r2_wait(eng, n_st=n_st):
                eng.wait_ge(ph.sem("st"), n_st)

            def store_bf(c, s=s, tt=tt):
                return xbfA_ap(s, c, tt * TT, (tt + 1) * TT)

            def xin2_wait(eng, h1=h1):
                eng.wait_ge(h1[0], h1[1])

            h2 = ln.emit(em, r2b[:, :, :], r2b[:, :, :], store_bf=store_bf,
                         gain=sb_ln1g, bias=sb_ln1b, xin_wait=xin2_wait,
                         rt_ready_wait=r2_wait)
            h_prev = h2
        def fin(e, h_prev=h_prev):
            e.wait_ge(h_prev[0], h_prev[1])
        em.on("sync", fin)
        em.replay(nc)
        pes.close()

    def phase_f_att(tag):
        pes = ExitStack()
        Sl = lambda n, sh, dt: pes.enter_context(nc.sbuf_tensor(n, sh, dt))
        Pl = lambda n, sh: pes.enter_context(nc.psum_tensor(n, sh, F32))
        ph = Sems(nc, es, tag)
        em = Emitter()
        wq_p = [Sl(f"{tag}_wq{j}", [128, DC, 128], BF16) for j in range(2)]
        wk_p = [Sl(f"{tag}_wk{j}", [128, DC, 128], BF16) for j in range(2)]
        wv_p = [Sl(f"{tag}_wv{j}", [128, DC, 128], BF16) for j in range(2)]
        qTp = Sl(f"{tag}_qT", [128, T], BF16)
        kTp = Sl(f"{tag}_kT", [128, T], BF16)
        vN = Sl(f"{tag}_vN", [128, 16, 128], BF16)
        expT = [Sl(f"{tag}_ex{j}", [128, 512], BF16) for j in range(3)]
        zrec = Sl(f"{tag}_zr", [1, 512], F32)
        zb_sb = Sl(f"{tag}_zb", [128, 512], F32)
        ot_f = Sl(f"{tag}_otf", [128, 512], F32)
        ps_p = [Pl(f"{tag}_pp{j}", [128, 512]) for j in range(3)]
        ps_oT = [Pl(f"{tag}_po{j}", [128, 512]) for j in range(2)]
        ps_z = Pl(f"{tag}_pz", [1, 512])
        ps_zb = Pl(f"{tag}_pzb", [128, 512])

        base_wf = ph.val("wfree")
        for hp in range(_F_ATT_HP):
            j = hp % 2

            def wload(e, hp=hp, j=j, base_wf=base_wf):
                if hp >= 2:
                    e.wait_ge(ph.sem("wfree"), base_wf + hp - 1)
                for W, dst in ((Wq, wq_p[j]), (Wk, wk_p[j]), (Wv, wv_p[j])):
                    e.dma_start(dst[:, :, :],
                                W[:, 128 * hp:128 * hp + 128].rearrange("(c p) o -> p c o", p=128)
                                ).then_inc(ph.sem("wld"), 16)
            em.on("gpsimd", wload)
            n_wld = ph.bump("wld", 48)

            for s in range(NSEQ):
                for which, wbuf, bias_sb, dst in ((
                        (0, wq_p[j], sb_bqF, qTp), (1, wk_p[j], sb_bkF, kTp))
                        if _F_ATT_PART >= 1 else ()):
                    for tt in range(NTT):
                        b = tt % 2
                        c_ev = ph.val("evp")
                        c_cons = ph.val("mst")
                        c_expg = ph.val("exp")

                        def t_p(t, s=s, tt=tt, b=b, wbuf=wbuf, c_ev=c_ev, n_wld=n_wld,
                                c_expg=c_expg):
                            t.wait_ge(ph.sem("wld"), n_wld)
                            t.wait_ge(ph.sem("exp"), c_expg)
                            if c_ev >= 2:
                                t.wait_ge(ph.sem("evp"), c_ev - 1)
                            for c in range(DC):
                                k = t.matmul(ps_p[b][:, :], wbuf[:, c, :],
                                             xbfA_ap(s, c, tt * TT, (tt + 1) * TT),
                                             start=(c == 0), stop=(c == DC - 1))
                            k.then_inc(ph.sem("mp"), 1)
                        em.on("tensor", t_p)
                        n_mp = ph.bump("mp")

                        def s_evp(sc, tt=tt, b=b, hp=hp, bias_sb=bias_sb, dst=dst,
                                  n_mp=n_mp, c_cons=c_cons):
                            sc.wait_ge(ph.sem("mp"), n_mp)
                            sc.wait_ge(ph.sem("mst"), c_cons)
                            sc.activation(dst[:, tt * TT:(tt + 1) * TT], ps_p[b][:, :],
                                          AF.Identity, bias=bias_sb[:, hp:hp + 1]
                                          ).then_inc(ph.sem("evp"), 1)
                        em.on("scalar", s_evp)
                        ph.bump("evp")

                n_evp_done = ph.val("evp")
                for nt in range(16 if _F_ATT_PART >= 2 else 0):
                    b = nt % 2
                    c_ev = ph.val("evv")
                    c_cons = ph.val("mot")

                    def t_v(t, s=s, nt=nt, b=b, j=j, c_ev=c_ev, n_wld=n_wld,
                            n_evp_done=n_evp_done):
                        t.wait_ge(ph.sem("wld"), n_wld)
                        t.wait_ge(ph.sem("evp"), n_evp_done)
                        if c_ev >= 2:
                            t.wait_ge(ph.sem("evv"), c_ev - 1)
                        for c in range(DC):
                            k = t.matmul(ps_p[b][:, 0:128], xbfA_ap(s, c, nt * 128, (nt + 1) * 128),
                                         wv_p[j][:, c, :], start=(c == 0), stop=(c == DC - 1))
                        k.then_inc(ph.sem("mp"), 1)
                    em.on("tensor", t_v)
                    n_mp = ph.bump("mp")

                    def v_evv(v, nt=nt, b=b, n_mp=n_mp, c_cons=c_cons):
                        v.wait_ge(ph.sem("mp"), n_mp)
                        v.wait_ge(ph.sem("mot"), c_cons)
                        v.tensor_copy(vN[:, nt, :], ps_p[b][:, 0:128]).then_inc(ph.sem("evv"), 1)
                    em.on("vector", v_evv)
                    ph.bump("evv")
                n_proj = (ph.val("evp"), ph.val("evv"))

                if s == NSEQ - 1:
                    def t_wfree(t):
                        t.nop().then_inc(ph.sem("wfree"), 1)
                    em.on("tensor", t_wfree)
                    ph.bump("wfree")

                for hh in range(2 if _F_ATT_PART >= 3 else 0):
                    h = 2 * hp + hh
                    b0 = hh * 64
                    for qc in range(NTT):
                        bo = qc % 2
                        c_fin = ph.val("fin")
                        c_inv0 = ph.val("inv")

                        NKT = 4 * qc + 4
                        ot_q = []
                        for kt in range(NKT + 1):
                            if kt < NKT:
                                b = kt % 3
                                c_exp = ph.val("exp")

                                def t_st(t, s=s, hh=hh, qc=qc, kt=kt, b=b, b0=b0,
                                         n_proj=n_proj, c_exp=c_exp):
                                    t.wait_ge(ph.sem("evp"), n_proj[0])
                                    t.wait_ge(ph.sem("evv"), n_proj[1])
                                    if c_exp >= 3:
                                        t.wait_ge(ph.sem("exp"), c_exp - 2)
                                    t.matmul(ps_p[b][:, :],
                                             kTp[b0:b0 + 64, kt * 128:(kt + 1) * 128],
                                             qTp[b0:b0 + 64, qc * TT:(qc + 1) * TT],
                                             start=True, stop=True).then_inc(ph.sem("mst"), 1)
                                em.on("tensor", t_st)
                                n_mst = ph.bump("mst")

                                def s_exp(sc, b=b, kt=kt, n_mst=n_mst,
                                          c_mot=max(0, ph.val("mot") - 1)):
                                    sc.wait_ge(ph.sem("mst"), n_mst)
                                    sc.wait_ge(ph.sem("mot"), c_mot)
                                    sc.activation(expT[b][:, :], ps_p[b][:, :], AF.Exp,
                                                  scale=0.125).then_inc(ph.sem("exp"), 1)
                                em.on("scalar", s_exp)
                                n_exp = ph.bump("exp")

                                need_mask = (kt >= 4 * qc)
                                if need_mask:
                                    def v_mask(v, b=b, m=kt - 4 * qc, n_exp=n_exp):
                                        v.wait_ge(ph.sem("exp"), n_exp)
                                        v.tensor_mul(expT[b][:, :], expT[b][:, :],
                                                     sb_m512[m][:, :]).then_inc(ph.sem("msk"), 1)
                                    em.on("vector", v_mask)
                                    n_gate = ("msk", ph.bump("msk"))
                                else:
                                    n_gate = ("exp", n_exp)

                                def t_ot(t, s=s, kt=kt, qc=qc, b=b, b0=b0, bo=bo, hh=hh,
                                         n_gate=n_gate, c_fin=c_fin, c_inv0=c_inv0,
                                         NKT=NKT):
                                    t.wait_ge(ph.sem(n_gate[0]), n_gate[1])
                                    if kt == 0 and c_fin >= 2:
                                        t.wait_ge(ph.sem("fin"), c_fin - 1)
                                    if kt == 0 and c_inv0 >= 1:
                                        t.wait_ge(ph.sem("inv"), c_inv0)
                                    t.matmul(ps_oT[bo][b0:b0 + 64, :], vN[:, kt, b0:b0 + 64],
                                             expT[b][:, :], start=(kt == 0),
                                             stop=(kt == NKT - 1), tile_position=(0, b0))
                                    t.matmul(ps_z[0:1, :], ones_col_b[:, 0:1], expT[b][:, :],
                                             start=(kt == 0), stop=(kt == NKT - 1)
                                             ).then_inc(ph.sem("mot"), 1)
                                ot_q.append(t_ot)
                                ph.bump("mot")
                            if kt >= 1:
                                em.on("tensor", ot_q[kt - 1])
                        n_mot = ph.val("mot")

                        if _F_ATT_PART < 4:
                            continue

                        def v_inv(v, n_mot=n_mot):
                            v.wait_ge(ph.sem("mot"), n_mot)
                            v.reciprocal(zrec[0:1, :], ps_z[0:1, :]).then_inc(ph.sem("inv"), 1)
                        em.on("vector", v_inv)
                        n_inv = ph.bump("inv")

                        def t_zb(t, b0=b0, n_inv=n_inv, c_fin=c_fin):
                            t.wait_ge(ph.sem("inv"), n_inv)
                            if c_fin >= 1:
                                t.wait_ge(ph.sem("fin"), c_fin)
                            t.matmul(ps_zb[b0:b0 + 64, :], ones_row_f[0:1, 0:64], zrec[0:1, :],
                                     start=True, stop=True, tile_position=(0, b0)
                                     ).then_inc(ph.sem("mzb"), 1)
                        em.on("tensor", t_zb)
                        n_mzb = ph.bump("mzb")

                        def v_fin(v, h=h, s=s, qc=qc, bo=bo, b0=b0, hp=hp, n_mzb=n_mzb):
                            v.wait_ge(ph.sem("mzb"), n_mzb)
                            v.tensor_copy(zb_sb[b0:b0 + 64, :], ps_zb[b0:b0 + 64, :])
                            v.tensor_mul(ot_f[b0:b0 + 64, :], ps_oT[bo][b0:b0 + 64, :],
                                         zb_sb[b0:b0 + 64, :])
                            c = h // 2
                            dst = XB[1][b0:b0 + 64,
                                       (s * DC + c) * T + qc * TT:(s * DC + c) * T + (qc + 1) * TT]
                            v.tensor_scalar_add(dst, ot_f[b0:b0 + 64, :],
                                                sb_bvF[b0:b0 + 64, hp:hp + 1]
                                                ).then_inc(ph.sem("fin"), 1)
                        em.on("vector", v_fin)
                        ph.bump("fin")

        em.replay(nc)
        pes.close()

    def phase_f_out(tag, wnat_sb):
        pes = ExitStack()
        Sl = lambda n, sh, dt: pes.enter_context(nc.sbuf_tensor(n, sh, dt))
        Pl = lambda n, sh: pes.enter_context(nc.psum_tensor(n, sh, F32))
        ph = Sems(nc, es, tag)
        em = Emitter()
        lnO = LNHelper(tag + "l", pes)
        wo_sb = Sl(f"{tag}_wo", [128, DC, D], BF16)
        u1 = Sl(f"{tag}_u1", [128, DC, TT], F32)
        ps_u = [Pl(f"{tag}_ps{jj}", [128, 512]) for jj in range(2)]
        ps_lg = Pl(f"{tag}_plg", [128, 8])
        lg = Sl(f"{tag}_lg", [128, 8], F32)
        lgt = Sl(f"{tag}_lgt", [128, 8], F32)
        lgu = Sl(f"{tag}_lgu", [128, 8], F32)
        lgs = Sl(f"{tag}_lgs", [128, 6], F32)

        def wload(e):
            e.dma_start(wo_sb[:, :, :], Wo.rearrange("(c p) o -> p c o", p=128)
                        ).then_inc(ph.sem("wld"), 16)
        em.on("gpsimd", wload)
        n_wld_o = ph.bump("wld", 16)

        tiles = [(s, tt) for s in range(NSEQ) for tt in range(NTT)]
        done_handles = []
        for i, (s, tt) in enumerate(tiles):
            for o in range(DC):
                b = o % 2
                c_ev = ph.val("ev")

                def t_mm(t, s=s, tt=tt, o=o, b=b, c_ev=c_ev, n_wld_o=n_wld_o):
                    t.wait_ge(ph.sem("wld"), n_wld_o)
                    if c_ev >= 2:
                        t.wait_ge(ph.sem("ev"), c_ev - 1)
                    for c in range(DC):
                        k = t.matmul(ps_u[b][:, :], wo_sb[:, c, o * 128:(o + 1) * 128],
                                     xbfB_ap(s, c, tt * TT, (tt + 1) * TT),
                                     start=(c == 0), stop=(c == DC - 1))
                    k.then_inc(ph.sem("mm"), 1)
                em.on("tensor", t_mm)
                n_mm = ph.bump("mm")

                def s_ev(sc, o=o, b=b, n_mm=n_mm, i=i, done_handles=done_handles,
                         c_sta=ph.val("sta")):
                    sc.wait_ge(ph.sem("mm"), n_mm)
                    if o == 0 and i >= 1:
                        h0 = done_handles[i - 1]
                        sc.wait_ge(h0[0], h0[1])
                        sc.wait_ge(ph.sem("sta"), c_sta)
                    sc.activation(u1[:, o, :], ps_u[b][:, :], AF.Identity,
                                  bias=sb_boF[:, o:o + 1]).then_inc(ph.sem("ev"), 1)
                em.on("scalar", s_ev)
                ph.bump("ev")
            n_ev = ph.val("ev")

            def dma_racc(e, s=s, tt=tt, n_ev=n_ev):
                e.wait_ge(ph.sem("ev"), n_ev)
                e.dma_start(u1[:, :, :], dtile(s_ffn_in, s, tt * TT, (tt + 1) * TT),
                            accum_op=ALU.add).then_inc(ph.sem("acc"), 16)
            em.on("gpsimd", dma_racc)
            n_acc = ph.bump("acc", 16)

            def dma_sta(e, s=s, tt=tt, n_acc=n_acc):
                e.wait_ge(ph.sem("acc"), n_acc)
                e.dma_start(dtile(s_a, s, tt * TT, (tt + 1) * TT), u1[:, :, :]
                            ).then_inc(ph.sem("sta"), 16)
            em.on("sync", dma_sta)
            n_sta = ph.bump("sta", 16)

            def xin_wait(eng, n_acc=n_acc):
                eng.wait_ge(ph.sem("acc"), n_acc)

            def rt_wait(eng, n_sta=n_sta):
                eng.wait_ge(ph.sem("sta"), n_sta)

            def store_bf(c, s=s, tt=tt):
                return xbfA_ap(s, c, tt * TT, (tt + 1) * TT)

            hdl = lnO.emit(em, u1[:, :, :], u1[:, :, :], store_bf=store_bf,
                           gain=sb_ln2g, bias=sb_ln2b, xin_wait=xin_wait,
                           rt_ready_wait=rt_wait)
            done_handles.append(hdl)

            for st4 in range(4):
                gi = (s * NTT + tt) * 4 + st4
                t0 = tt * TT + st4 * 128

                def t_lg(t, s=s, t0=t0, hdl=hdl, c_rtv=ph.val("rtv")):
                    t.wait_ge(hdl[0], hdl[1])
                    if c_rtv >= 1:
                        t.wait_ge(ph.sem("rtv"), c_rtv)
                    for c in range(DC):
                        t.matmul(ps_lg[:, :], xbfA_ap(s, c, t0, t0 + 128),
                                 sb_Wr[:, c, :], start=(c == 0), stop=False)
                    k = t.matmul(ps_lg[:, :], ones_row_b[0:1, :], sb_br_b[0:1, :],
                                 start=False, stop=True)
                    k.then_inc(ph.sem("mlg"), 1)
                em.on("tensor", t_lg)
                n_mlg = ph.bump("mlg")

                def v_r1(v, n_mlg=n_mlg, c_r2=ph.val("r2"), c_sg=ph.val("sg")):
                    v.wait_ge(ph.sem("mlg"), n_mlg)
                    v.wait_ge(ph.sem("sg"), c_sg)
                    v.wait_ge(ph.sem("r2"), c_r2)
                    v.tensor_copy(lg[:, :], ps_lg[:, :])
                    v.reduce_max(lgs[:, 0:1], ps_lg[:, :], axis=AX.X).then_inc(ph.sem("r1"), 1)
                em.on("vector", v_r1)
                n_r1 = ph.bump("r1")

                def g_r2(gp, n_r1=n_r1):
                    gp.wait_ge(ph.sem("r1"), n_r1)
                    gp.tensor_scalar(lgt[:, :], lg[:, :], lgs[:, 0:1], None, ALU.is_ge
                                     ).then_inc(ph.sem("r2"), 1)
                em.on("gpsimd", g_r2)
                n_r2 = ph.bump("r2")

                def v_r2b(v, n_r2=n_r2):
                    v.wait_ge(ph.sem("r2"), n_r2)
                    v.scalar_tensor_tensor(lgu[:, :], lgt[:, :], -1e30, lg[:, :],
                                           ALU.mult, ALU.add).then_inc(ph.sem("r2b"), 1)
                em.on("vector", v_r2b)
                n_r2b = ph.bump("r2b")

                def v_r3(v, n_r2b=n_r2b):
                    v.wait_ge(ph.sem("r2b"), n_r2b)
                    v.reduce_sum(lgs[:, 2:3], lgt[:, :], axis=AX.X)
                    v.reduce_max(lgs[:, 1:2], lgu[:, :], axis=AX.X).then_inc(ph.sem("r3"), 1)
                em.on("vector", v_r3)
                n_r3 = ph.bump("r3")

                def g_r4(gp, n_r3=n_r3):
                    gp.wait_ge(ph.sem("r3"), n_r3)
                    gp.tensor_scalar(lgs[:, 3:4], lgs[:, 2:3], 2.0, None, ALU.is_ge)
                    gp.tensor_sub(lgs[:, 4:5], lgs[:, 0:1], lgs[:, 1:2]
                                  ).then_inc(ph.sem("r4"), 1)
                em.on("gpsimd", g_r4)
                n_r4 = ph.bump("r4")

                def v_r5(v, n_r4=n_r4):
                    v.wait_ge(ph.sem("r4"), n_r4)
                    v.scalar_tensor_tensor(lgs[:, 5:6], lgs[:, 3:4], lgs[:, 4:5],
                                           lgs[:, 1:2], ALU.mult, ALU.add).then_inc(ph.sem("r5"), 1)
                em.on("vector", v_r5)
                n_r5 = ph.bump("r5")

                def g_r6(gp, n_r5=n_r5):
                    gp.wait_ge(ph.sem("r5"), n_r5)
                    gp.tensor_scalar(lgt[:, :], lg[:, :], lgs[:, 5:6], None, ALU.is_ge)
                    gp.tensor_single_scalar(lgs[:, 0:1], lgs[:, 0:1], -1.0, ALU.mult
                                            ).then_inc(ph.sem("r6"), 1)
                em.on("gpsimd", g_r6)
                n_r6 = ph.bump("r6")

                def s_exp(sc, n_r6=n_r6):
                    sc.wait_ge(ph.sem("r6"), n_r6)
                    sc.activation(lg[:, :], lg[:, :], AF.Exp, bias=lgs[:, 0:1]
                                  ).then_inc(ph.sem("sg"), 1)
                em.on("scalar", s_exp)
                n_sg = ph.bump("sg")

                def v_r7(v, n_sg=n_sg):
                    v.wait_ge(ph.sem("sg"), n_sg)
                    v.tensor_mul(lg[:, :], lg[:, :], lgt[:, :])
                    v.tensor_copy(lgu[:, 1:2], lgs[:, 0:1])
                    v.reduce_sum(lgs[:, 4:5], lg[:, :], axis=AX.X)
                    v.tensor_copy(lgu[:, 2:3], lgs[:, 0:1])
                    v.reciprocal(lgu[:, 0:1], lgs[:, 4:5]).then_inc(ph.sem("r8"), 1)
                em.on("vector", v_r7)
                n_r8 = ph.bump("r8")

                def g_r9(gp, gi=gi, n_r8=n_r8):
                    gp.wait_ge(ph.sem("r8"), n_r8)
                    gp.tensor_single_scalar(wnat_sb[:, gi, :], lg[:, :], lgu[:, 0:1],
                                            ALU.mult).then_inc(ph.sem("rtv"), 1)
                em.on("gpsimd", g_r9)
                ph.bump("rtv")

        def fin(e):
            e.wait_ge(ph.sem("sta"), ph.val("sta"))
        em.on("sync", fin)

        def fin_v(v):
            v.wait_ge(ph.sem("rtv"), ph.val("rtv"))
        em.on("vector", fin_v)
        em.replay(nc)
        pes.close()

    def dump_xbfB(dst):
        ph3 = Sems(nc, es, "dmpB")
        em3 = Emitter()
        def cv(v):
            pass
        base_n3 = ph3.val("st")

        def go(e, base_n3=base_n3):
            n = base_n3
            for s in range(NSEQ):
                for c in range(DC):
                    e.dma_start(dst[s, c, :, :], XB[1][:, (s * DC + c) * T:(s * DC + c + 1) * T]
                                ).then_inc(ph3.sem("st"), 16)
                    n += 16
            e.wait_ge(ph3.sem("st"), n)
        em3.on("gpsimd", go)
        for _ in range(NSEQ * DC):
            ph3.bump("st", 16)
        em3.replay(nc)


    def phase_f_moe(tag, wnat_sb, src_xprev, dst_x):
        pes = ExitStack()
        Sl = lambda n, sh, dt: pes.enter_context(nc.sbuf_tensor(n, sh, dt))
        Pl = lambda n, sh: pes.enter_context(nc.psum_tensor(n, sh, F32))
        ph = Sems(nc, es, tag)
        em = Emitter()
        moe = Sl(f"{tag}_moe", [128, DC, 1024], F32)
        w1b = [Sl(f"{tag}_w1{j}", [128, DC, 512], BF16) for j in range(2)]
        w2b = [Sl(f"{tag}_w2{j}", [128, DC, 256], BF16) for j in range(2)]
        w_eb = Sl(f"{tag}_web", [128, 1024], F32)
        w_row = Sl(f"{tag}_wrow", [1, 1024], F32)
        eo_f = Sl(f"{tag}_eof", [128, 1024], F32)
        ps_mid = [Pl(f"{tag}_pm{j}", [128, 1024]) for j in range(2)]
        ps_w = Pl(f"{tag}_pw", [128, 1024])

        def midv(f):
            return XB[1][:, f * 1024:(f + 1) * 1024]

        def gate0(t):
            t.wait_ge(gsem.sem("mcast"), gsem.val("mcast"))
        em.on("tensor", gate0)

        for g in range(4):
            s, half = g // 2, g % 2
            t0 = half * 1024

            for e in range(8):
                for pc in range(8):
                    j = (e * 8 + pc) % 2
                    c_mm = ph.val("mmid")

                    def dma_w1(eng, e=e, pc=pc, j=j, c_mm=c_mm):
                        if c_mm >= 8:
                            eng.wait_ge(ph.sem("mmid"), c_mm - 4)
                        eng.dma_start(w1b[j][:, :, :],
                                      ew1b[e, :, pc * 512:(pc + 1) * 512]
                                      .rearrange("(c p) f -> p c f", p=128)
                                      ).then_inc(ph.sem("lw1"), 16)
                    em.on("sync", dma_w1)
                    n_lw1 = ph.bump("lw1", 16)

                    for fi in range(4):
                        f = pc * 4 + fi
                        b = f % 2
                        c_ev = ph.val("evm")

                        def t_mid(t, s=s, t0=t0, f=f, fi=fi, b=b, j=j, c_ev=c_ev,
                                  n_lw1=n_lw1, c_evo_rel=ph.val("evo")):
                            t.wait_ge(ph.sem("lw1"), n_lw1)
                            t.wait_ge(ph.sem("evo"), c_evo_rel)
                            if c_ev >= 2:
                                t.wait_ge(ph.sem("evm"), c_ev - 1)
                            for half2 in range(2):
                                for c in range(DC):
                                    k = t.matmul(
                                        ps_mid[b][:, half2 * 512:(half2 + 1) * 512],
                                        w1b[j][:, c, fi * 128:(fi + 1) * 128],
                                        xbfA_ap(s, c, t0 + half2 * 512, t0 + (half2 + 1) * 512),
                                        start=(c == 0), stop=(c == DC - 1))
                            k.then_inc(ph.sem("mmid"), 1)
                        em.on("tensor", t_mid)
                        n_mm = ph.bump("mmid")

                        def s_gelu(sc, e=e, f=f, b=b, n_mm=n_mm, c_eo=ph.val("meo")):
                            sc.wait_ge(ph.sem("mmid"), n_mm)
                            sc.wait_ge(ph.sem("meo"), c_eo)
                            sc.activation(midv(f), ps_mid[b][:, :], AF.Gelu,
                                          bias=sb_eb1[:, e * 32 + f:e * 32 + f + 1]
                                          ).then_inc(ph.sem("evm"), 1)
                        em.on("scalar", s_gelu)
                        ph.bump("evm")
                n_evm = ph.val("evm")

                c_gv = ph.val("gv")

                def t_gate(t, s=s, half=half, e=e, c_gv=c_gv):
                    if c_gv >= 1:
                        t.wait_ge(ph.sem("gv"), c_gv)
                    for st4 in range(8):
                        gi = s * 16 + half * 8 + st4
                        k = t.transpose(ps_w[0:1, st4 * 128:st4 * 128 + 128],
                                        wnat_sb[:, gi, e:e + 1], sb_identF[:, :])
                    k.then_inc(ph.sem("gt"), 1)
                em.on("tensor", t_gate)
                n_gt = ph.bump("gt")

                def v_grow(v, n_gt=n_gt):
                    v.wait_ge(ph.sem("gt"), n_gt)
                    v.tensor_copy(w_row[0:1, :], ps_w[0:1, :]).then_inc(ph.sem("gr"), 1)
                em.on("vector", v_grow)
                n_gr = ph.bump("gr")

                def t_gbc(t, n_gr=n_gr):
                    t.wait_ge(ph.sem("gr"), n_gr)
                    t.matmul(ps_w[:, 0:512], ones_row_f[0:1, :], w_row[0:1, 0:512],
                             start=True, stop=True)
                    t.matmul(ps_w[:, 512:1024], ones_row_f[0:1, :], w_row[0:1, 512:1024],
                             start=True, stop=True).then_inc(ph.sem("gb"), 1)
                em.on("tensor", t_gbc)
                n_gb = ph.bump("gb")

                def v_geb(v, n_gb=n_gb, c_mo=ph.val("mo")):
                    v.wait_ge(ph.sem("gb"), n_gb)
                    v.wait_ge(ph.sem("mo"), c_mo)
                    v.tensor_copy(w_eb[:, :], ps_w[:, :]).then_inc(ph.sem("gv"), 1)
                em.on("vector", v_geb)
                n_gv = ph.bump("gv")

                for oq in range(4):
                    for rc in range(4):
                        j2 = (oq * 4 + rc) % 2
                        c_meo = ph.val("meo")

                        def dma_w2(eng, e=e, oq=oq, rc=rc, j2=j2, c_meo=c_meo):
                            if c_meo >= 4:
                                eng.wait_ge(ph.sem("meo"), c_meo - 2)
                            eng.dma_start(w2b[j2][:, :, :],
                                          ew2b[e, rc * 1024:(rc + 1) * 1024,
                                               oq * 256:(oq + 1) * 256]
                                          .rearrange("(c p) o -> p c o", p=128)
                                          ).then_inc(ph.sem("lw2"), 16)
                        em.on("sync", dma_w2)
                        n_lw2 = ph.bump("lw2", 16)

                        for oi in range(2):
                            def t_eo(t, oi=oi, rc=rc, j2=j2, n_lw2=n_lw2, n_evm=n_evm,
                                     c_evo=ph.val("evo"), c_gelu_done=n_evm):
                                t.wait_ge(ph.sem("lw2"), n_lw2)
                                t.wait_ge(ph.sem("evm"), n_evm)
                                if rc == 0 and c_evo >= 2:
                                    t.wait_ge(ph.sem("evo"), c_evo)
                                for fc in range(DC):
                                    f = rc * 8 + fc
                                    for half2 in range(2):
                                        k = t.matmul(
                                            ps_mid[oi][:, half2 * 512:(half2 + 1) * 512],
                                            w2b[j2][:, fc, oi * 128:(oi + 1) * 128],
                                            midv(f)[:, half2 * 512:(half2 + 1) * 512],
                                            start=(f == 0), stop=(f == 31))
                                k.then_inc(ph.sem("meo"), 1)
                            em.on("tensor", t_eo)
                            ph.bump("meo")
                    n_meo = ph.val("meo")

                    for oi in range(2):
                        ot = oq * 2 + oi

                        def s_evo(sc, e=e, ot=ot, oi=oi, n_meo=n_meo,
                                  c_mo=ph.val("mo")):
                            sc.wait_ge(ph.sem("meo"), n_meo)
                            sc.wait_ge(ph.sem("mo"), c_mo)
                            sc.activation(eo_f[:, :], ps_mid[oi][:, :], AF.Identity,
                                          bias=sb_eb2[:, e * 8 + ot:e * 8 + ot + 1]
                                          ).then_inc(ph.sem("evo"), 1)
                        em.on("scalar", s_evo)
                        n_evo = ph.bump("evo")

                        def v_moe(v, e=e, ot=ot, n_evo=n_evo, n_gv=n_gv, g=g,
                                  c_stm=ph.val("stx")):
                            v.wait_ge(ph.sem("evo"), n_evo)
                            v.wait_ge(ph.sem("gv"), n_gv)
                            if e == 0 and c_stm:
                                v.wait_ge(ph.sem("stx"), c_stm)
                            if e == 0:
                                v.tensor_mul(moe[:, ot, :], eo_f[:, :], w_eb[:, :]
                                             ).then_inc(ph.sem("mo"), 1)
                            else:
                                v.tensor_mul(eo_f[:, :], eo_f[:, :], w_eb[:, :])
                                v.tensor_add(moe[:, ot, :], moe[:, ot, :], eo_f[:, :]
                                             ).then_inc(ph.sem("mo"), 1)
                        em.on("vector", v_moe)
                        ph.bump("mo")
            n_mo_g = ph.val("mo")

            def dma_acc_a(eng, s=s, t0=t0, n_mo_g=n_mo_g):
                eng.wait_ge(ph.sem("mo"), n_mo_g)
                eng.dma_start(moe[:, :, :], dtile(s_a, s, t0, t0 + 1024),
                              accum_op=ALU.add).then_inc(ph.sem("ac1"), 16)
            em.on("gpsimd", dma_acc_a)
            n_ac1 = ph.bump("ac1", 16)

            def dma_acc_x(eng, s=s, t0=t0, n_ac1=n_ac1):
                eng.wait_ge(ph.sem("ac1"), n_ac1)
                eng.dma_start(moe[:, :, :], dtile(src_xprev, s, t0, t0 + 1024),
                              accum_op=ALU.add).then_inc(ph.sem("ac2"), 16)
            em.on("gpsimd", dma_acc_x)
            n_ac2 = ph.bump("ac2", 16)

            def dma_stx(eng, s=s, t0=t0, n_ac2=n_ac2):
                eng.wait_ge(ph.sem("ac2"), n_ac2)
                eng.dma_start(dtile(dst_x, s, t0, t0 + 1024), moe[:, :, :]
                              ).then_inc(ph.sem("stx"), 16)
            em.on("sync", dma_stx)
            ph.bump("stx", 16)

        def fin(e):
            e.wait_ge(ph.sem("stx"), ph.val("stx"))
        em.on("sync", fin)
        em.replay(nc)
        pes.close()
    wnatF = S("wnatF", [128, NSEQ * 16, 8], F32)
    if not _SKIP_EARLY:
        openA("a3")
        with nc.named_scope("f1l"):
            phase_f_ln("f1l", s_xA)
    if not _SKIP_EARLY and _PHASES >= 2:
        openB("b3")
        with nc.named_scope("f1a"):
            phase_f_att("f1a")
        if _DEBUG_TAP == "xbfB":
            dump_xbfB(s_xC)
    if not _SKIP_EARLY and _PHASES >= 3:
        with nc.named_scope("f1o"):
            phase_f_out("f1o", wnatF)
        if _DEBUG_TAP == "wnat":
            ph4 = Sems(nc, es, "dmpw")
            em4 = Emitter()
            base_n4 = ph4.val("st")

            def gow(e, base_n4=base_n4):
                e.dma_start(s_xC[0, 0, :, 0:256], wnatF[:, :, :].rearrange("p g e -> p (g e)")
                            ).then_inc(ph4.sem("st"), 16)
                e.wait_ge(ph4.sem("st"), base_n4 + 16)
            em4.on("sync", gow)
            ph4.bump("st", 16)
            em4.replay(nc)
    if not _SKIP_EARLY and _PHASES >= 4:
        with nc.named_scope("f1m"):
            phase_f_moe("f1m", wnatF, s_xA, s_xB)
        closeA(); closeB()
    if _PHASES >= 5:
        openA("a4")
        with nc.named_scope("bln"):
            ln_phase("bln", s_xB, None, "XA")
        _build_b = True
    else:
        _build_b = False

    def phase_b_main():
        pes = ExitStack()
        Sl = lambda n, sh, dt: pes.enter_context(nc.sbuf_tensor(n, sh, dt))
        Pl = lambda n, sh, dt=F32: pes.enter_context(nc.psum_tensor(n, sh, dt))
        ph = Sems(nc, es, "bm")
        em = Emitter()
        NH = 8
        BL = 16
        wqkv = [Sl(f"bm_w{j}", [128, DC, 128], BF16) for j in range(6)]
        bwm_sb = Sl("bm_bwm", [128, DC, 16], BF16)
        f32w = Sl("bm_f32w", [128, T], F32)
        vf32 = Sl("bm_vf32", [128, T], F32)
        xp = Sl("bm_xp", [128, T + 3], F32)
        cvw = Sl("bm_cvw", [128, T], F32)
        b16s = Sl("bm_b16s", [128, T], BF16)
        vb = Sl("bm_vb", [128, T], BF16)
        qn = Sl("bm_qn", [128, T], BF16)
        kn = Sl("bm_kn", [128, T], BF16)
        kbfm = Sl("bm_kbfm", [128, T], BF16)
        vkb = Sl("bm_vkb", [128, BL, 256], BF16)
        ktm = Sl("bm_ktm", [128, BL, 128], BF16)
        utm = Sl("bm_utm", [128, BL, 128], BF16)
        wTf = Sl("bm_wT", [128, BL, 128], BF16)
        aiT = Sl("bm_aiT", [128, BL, 128], BF16)
        odel = Sl("bm_odel", [128, T], F32)
        gbb = Sl("bm_gb", [128, T], F32)
        bcb = Sl("bm_bcb", [128, T], F32)
        bmn = Sl("bm_bmn", [128, BL, 16], F32)
        bsg = Sl("bm_bsg", [128, BL, 8], F32)
        rowA = Sl("bm_rowA", [1, T], F32)
        rowB = Sl("bm_rowB", [1, T], F32)
        Ssb = Sl("bm_S", [128, 128], BF16)
        Sf = Sl("bm_Sf", [128, 128], F32)
        tsb = Sl("bm_tsb", [128, 256], BF16)
        uis = Sl("bm_uis", [128, 128], BF16)
        Xp_ = [Sl(f"bm_X{j}", [128, 128], BF16) for j in range(2)]
        Yp_ = [Sl(f"bm_Y{j}", [128, 128], BF16) for j in range(6)]
        wtmp = Sl("bm_wtmp", [128, 128], BF16)
        psA = Pl("bm_psA", [128, 512])
        psB = Pl("bm_psB", [128, 512])
        psC = Pl("bm_psC", [128, 256])
        psD = Pl("bm_psD", [128, 128])
        psE = Pl("bm_psE", [128, 128], BF16)
        psF = Pl("bm_psF", [128, 128])
        psG = Pl("bm_psG", [1, 512])

        def pg(name, allow=0):
            return (name, ph.val(name) - allow)

        def pg_wait(t, w):
            if w[1] > 0:
                t.wait_ge(ph.sem(w[0]), w[1])

        def init(v):
            v.memset(xp[:, 0:3], 0.0)
        em.on("vector", init)

        def bwm_load(e):
            e.dma_start(bwm_sb[:, :, :], bwmix.rearrange("(c p) o -> p c o", p=128)
                        ).then_inc(ph.sem("bl"), 16)
        em.on("gpsimd", bwm_load)
        n_bl = ph.bump("bl", 16)

        units = [(s, h) for s in range(NSEQ) for h in range(NH)]
        for ui_, (s, h) in enumerate(units):
            if _B_PART < 1 or ui_ >= _B_UNITS:
                continue
            j3 = ui_ % 2

            def wload(e, s=s, h=h, j3=j3, c_pr=ph.val("prj")):
                if c_pr >= 24:
                    e.wait_ge(ph.sem("prj"), c_pr - 12)
                for wi, W in enumerate((qwB, kwB, vwB)):
                    e.dma_start(wqkv[3 * j3 + wi][:, :, :],
                                W[:, 128 * h:128 * h + 128].rearrange("(c p) o -> p c o", p=128)
                                ).then_inc(ph.sem("wl"), 16)
            em.on("gpsimd", wload)
            n_wl = ph.bump("wl", 48)

            if h == 0:
                for bt in range(BL):
                    def t_bm(t, s=s, bt=bt, c_uie=ph.val("uie"), n_bl=n_bl,
                             w_dd=pg("dd")):
                        t.wait_ge(ph.sem("bl"), n_bl)
                        t.wait_ge(ph.sem("uie"), c_uie)
                        pg_wait(t, w_dd)
                        for c in range(DC):
                            k = t.matmul(psD[:, 0:16],
                                         xbfA_ap(s, c, bt * 128, (bt + 1) * 128),
                                         bwm_sb[:, c, :], start=(c == 0), stop=(c == DC - 1))
                        k.then_inc(ph.sem("bmm"), 1)
                    em.on("tensor", t_bm)
                    n_bmm = ph.bump("bmm")

                    def v_bme(v, bt=bt, n_bmm=n_bmm):
                        v.wait_ge(ph.sem("bmm"), n_bmm)
                        v.tensor_copy(bmn[:, bt, :], psD[:, 0:16]).then_inc(ph.sem("dd"), 1)
                    em.on("vector", v_bme)
                    ph.bump("dd")
                n_dd = ph.val("dd")

                def s_bsg(sc, n_dd=n_dd):
                    sc.wait_ge(ph.sem("dd"), n_dd)
                    sc.activation(bsg[:, :, :], bmn[:, :, 0:8], AF.Sigmoid
                                  ).then_inc(ph.sem("bsg"), 1)
                em.on("scalar", s_bsg)
                ph.bump("bsg")
            n_bsg_u = ph.val("bsg")

            def proj_conv(widx, outf32, outbf):
                for tt in range(NTT):
                    ps = psA if tt % 2 == 0 else psB

                    def t_pj(t, s=s, tt=tt, ps=ps, widx=widx, n_wl=n_wl,
                             w_ab=pg("ab", 1)):
                        t.wait_ge(ph.sem("wl"), n_wl)
                        pg_wait(t, w_ab)
                        for c in range(DC):
                            k = t.matmul(ps[:, :], wqkv[widx][:, c, :],
                                         xbfA_ap(s, c, tt * TT, (tt + 1) * TT),
                                         start=(c == 0), stop=(c == DC - 1))
                        k.then_inc(ph.sem("prj"), 1)
                    em.on("tensor", t_pj)
                    n_pj = ph.bump("prj")

                    def v_pje(v, tt=tt, ps=ps, n_pj=n_pj, c_cv=ph.val("cv4")):
                        v.wait_ge(ph.sem("prj"), n_pj)
                        v.wait_ge(ph.sem("cv4"), c_cv)
                        v.tensor_copy(xp[:, 3 + tt * TT:3 + (tt + 1) * TT], ps[:, :]
                                      ).then_inc(ph.sem("ab"), 1)
                    em.on("vector", v_pje)
                    ph.bump("ab")
                n_ab = ph.val("ab")

                cw = {0: sb_convq, 1: sb_convk, 2: sb_convv}[widx % 3]

                def v_c0(v, n_ab=n_ab, h=h, cw=cw, c_sl=ph.val("sl")):
                    v.wait_ge(ph.sem("ab"), n_ab)
                    v.wait_ge(ph.sem("sl"), c_sl)
                    v.tensor_single_scalar(cvw[:, :], xp[:, 0:T], cw[:, h, 0:1], ALU.mult
                                           ).then_inc(ph.sem("cv1"), 1)
                em.on("vector", v_c0)
                n_c1 = ph.bump("cv1")

                def v_c123(v, h=h, cw=cw, n_c1=n_c1):
                    v.wait_ge(ph.sem("cv1"), n_c1)
                    v.scalar_tensor_tensor(cvw[:, :], xp[:, 1:T + 1], cw[:, h, 1:2],
                                           cvw[:, :], ALU.mult, ALU.add)
                    v.scalar_tensor_tensor(cvw[:, :], xp[:, 2:T + 2], cw[:, h, 2:3],
                                           cvw[:, :], ALU.mult, ALU.add)
                    v.scalar_tensor_tensor(cvw[:, :], xp[:, 3:T + 3], cw[:, h, 3:4],
                                           cvw[:, :], ALU.mult, ALU.add
                                           ).then_inc(ph.sem("cv4"), 1)
                em.on("vector", v_c123)
                n_c4 = ph.bump("cv4")

                def s_silu(sc, outf32=outf32, outbf=outbf, n_c4=n_c4):
                    sc.wait_ge(ph.sem("cv4"), n_c4)
                    last = sc.activation(outf32, cvw[:, :], AF.Silu)
                    if outbf is not None:
                        last = sc.activation(outbf, cvw[:, :], AF.Silu)
                    last.then_inc(ph.sem("sl"), 1)
                em.on("scalar", s_silu)
                return ph.bump("sl")

            def emit_l2(srcf, dstb, n_sl):
                def s_sq(sc, n_sl=n_sl, srcf=srcf, c_gm=ph.val("l2m")):
                    sc.wait_ge(ph.sem("sl"), n_sl)
                    sc.wait_ge(ph.sem("l2m"), c_gm)
                    sc.activation(b16s[:, :], srcf, AF.Square).then_inc(ph.sem("l2s"), 1)
                em.on("scalar", s_sq)
                n_sq = ph.bump("l2s")

                for ct in range(NTT):
                    def t_sum(t, ct=ct, n_sq=n_sq, w_gg=pg("gg")):
                        t.wait_ge(ph.sem("l2s"), n_sq)
                        pg_wait(t, w_gg)
                        t.matmul(psG[0:1, :], ones_col_b[:, 0:1],
                                 b16s[:, ct * TT:(ct + 1) * TT], start=True, stop=True
                                 ).then_inc(ph.sem("l2m"), 1)
                    em.on("tensor", t_sum)
                    n_sm = ph.bump("l2m")

                    def v_row(v, ct=ct, n_sm=n_sm, c_rs=ph.val("l2r")):
                        v.wait_ge(ph.sem("l2m"), n_sm)
                        if ct == 0:
                            v.wait_ge(ph.sem("l2r"), c_rs)
                        v.tensor_copy(rowA[0:1, ct * TT:(ct + 1) * TT], psG[0:1, :]
                                      ).then_inc(ph.sem("gg"), 1)
                    em.on("vector", v_row)
                    ph.bump("gg")
                n_gg = ph.val("gg")

                def s_rstd(sc, n_gg=n_gg, c_bc=ph.val("l2c")):
                    sc.wait_ge(ph.sem("gg"), n_gg)
                    sc.wait_ge(ph.sem("l2c"), c_bc)
                    sc.activation(rowB[0:1, :], rowA[0:1, :], AF.Ln,
                                  bias=constf(1e-12)[0:1, 0:1])
                    sc.activation(rowB[0:1, :], rowB[0:1, :], AF.Exp, scale=-0.5
                                  ).then_inc(ph.sem("l2r"), 1)
                em.on("scalar", s_rstd)
                n_rs = ph.bump("l2r")

                for ct in range(NTT):
                    ps = psA if ct % 2 == 0 else psB

                    def t_bc(t, ct=ct, ps=ps, n_rs=n_rs, w_ab=pg("ab", 1)):
                        t.wait_ge(ph.sem("l2r"), n_rs)
                        pg_wait(t, w_ab)
                        t.matmul(ps[:, :], ones_row_f[0:1, :], rowB[0:1, ct * TT:(ct + 1) * TT],
                                 start=True, stop=True).then_inc(ph.sem("l2c"), 1)
                    em.on("tensor", t_bc)
                    n_bc = ph.bump("l2c")

                    def v_mul(v, ct=ct, ps=ps, n_bc=n_bc, srcf=srcf, dstb=dstb):
                        v.wait_ge(ph.sem("l2c"), n_bc)
                        v.tensor_mul(dstb[:, ct * TT:(ct + 1) * TT],
                                     srcf[:, ct * TT:(ct + 1) * TT], ps[:, :]
                                     ).then_inc(ph.sem("ab"), 1)
                    em.on("vector", v_mul)
                    ph.bump("ab")
                return ph.val("ab")

            n_v = proj_conv(3 * j3 + 2, vf32[:, :], vb[:, :])
            proj_conv(3 * j3 + 0, f32w[:, :], None)
            n_qn = emit_l2(f32w[:, :], qn[:, :], ph.val("sl"))
            proj_conv(3 * j3 + 1, f32w[:, :], None)
            n_kn = emit_l2(f32w[:, :], kn[:, :], ph.val("sl"))
            if _B_PART < 2:
                continue

            c_l2r_b = ph.val("l2r")
            for grp in range(4):
                def t_brow(t, s=s, h=h, grp=grp, n_bsg_u=n_bsg_u, c_l2r_b=c_l2r_b,
                           w_gg=pg("gg")):
                    t.wait_ge(ph.sem("bsg"), n_bsg_u)
                    t.wait_ge(ph.sem("l2r"), c_l2r_b)
                    pg_wait(t, w_gg)
                    for q4 in range(4):
                        bt = grp * 4 + q4
                        k = t.transpose(psG[0:1, q4 * 128:q4 * 128 + 128],
                                        bsg[:, bt, h:h + 1], sb_identF[:, :])
                    k.then_inc(ph.sem("btr"), 1)
                em.on("tensor", t_brow)
                n_btr = ph.bump("btr")

                def v_brow(v, grp=grp, n_btr=n_btr):
                    v.wait_ge(ph.sem("btr"), n_btr)
                    v.tensor_copy(rowA[0:1, grp * 512:(grp + 1) * 512], psG[0:1, :]
                                  ).then_inc(ph.sem("gg"), 1)
                em.on("vector", v_brow)
                ph.bump("gg")
            n_brow = ph.val("gg")

            for ct in range(NTT):
                ps = psA if ct % 2 == 0 else psB

                def t_bbc(t, ct=ct, ps=ps, n_brow=n_brow, w_ab=pg("ab", 1)):
                    t.wait_ge(ph.sem("gg"), n_brow)
                    pg_wait(t, w_ab)
                    t.matmul(ps[:, :], ones_row_f[0:1, :], rowA[0:1, ct * TT:(ct + 1) * TT],
                             start=True, stop=True).then_inc(ph.sem("bbc"), 1)
                em.on("tensor", t_bbc)
                n_bbc = ph.bump("bbc")

                def v_kbf(v, ct=ct, ps=ps, n_bbc=n_bbc, n_kn=n_kn):
                    v.wait_ge(ph.sem("bbc"), n_bbc)
                    v.tensor_mul(kbfm[:, ct * TT:(ct + 1) * TT],
                                 kn[:, ct * TT:(ct + 1) * TT], ps[:, :]
                                 ).then_inc(ph.sem("ab"), 1)
                em.on("vector", v_kbf)
                ph.bump("ab")
            n_kbf = ph.val("ab")

            for bl in range(BL):
                def t_tr(t, bl=bl, n_kn=n_kn, n_v=n_v, w_ee=pg("ee")):
                    t.wait_ge(ph.sem("ab"), n_kn)
                    t.wait_ge(ph.sem("sl"), n_v)
                    pg_wait(t, w_ee)
                    t.transpose(psE[:, :], kn[:, bl * 128:(bl + 1) * 128], sb_identB[:, :]
                                ).then_inc(ph.sem("ktr"), 1)
                em.on("tensor", t_tr)
                n_ktr = ph.bump("ktr")

                def v_ktr(v, bl=bl, n_ktr=n_ktr, c_kb=ph.val("kbm")):
                    v.wait_ge(ph.sem("ktr"), n_ktr)
                    v.wait_ge(ph.sem("kbm"), c_kb)
                    v.tensor_copy(ktm[:, bl, :], psE[:, :]).then_inc(ph.sem("ee"), 1)
                em.on("vector", v_ktr)
                n_ee = ph.bump("ee")

                def g_kbm(gp, s=s, h=h, bl=bl, n_ee=n_ee):
                    gp.wait_ge(ph.sem("ee"), n_ee)
                    gp.tensor_single_scalar(vkb[:, bl, 128:256], ktm[:, bl, :],
                                            bsg[:, bl, h:h + 1], ALU.mult
                                            ).then_inc(ph.sem("kbm"), 1)
                em.on("gpsimd", g_kbm)
                ph.bump("kbm")

                def t_tr2(t, bl=bl, w_ee=pg("ee"), n_v=n_v):
                    t.wait_ge(ph.sem("sl"), n_v)
                    pg_wait(t, w_ee)
                    t.transpose(psE[:, :], vb[:, bl * 128:(bl + 1) * 128], sb_identB[:, :]
                                ).then_inc(ph.sem("vtr"), 1)
                em.on("tensor", t_tr2)
                n_vtr = ph.bump("vtr")

                def v_vtr(v, bl=bl, n_vtr=n_vtr, c_kb=ph.val("kbm")):
                    v.wait_ge(ph.sem("vtr"), n_vtr)
                    v.wait_ge(ph.sem("kbm"), c_kb)
                    v.tensor_copy(wtmp[:, :], psE[:, :]).then_inc(ph.sem("ee"), 1)
                em.on("vector", v_vtr)
                n_ee2 = ph.bump("ee")

                def g_vbm(gp, s=s, h=h, bl=bl, n_ee2=n_ee2):
                    gp.wait_ge(ph.sem("ee"), n_ee2)
                    gp.tensor_single_scalar(vkb[:, bl, 0:128], wtmp[:, :],
                                            bsg[:, bl, h:h + 1], ALU.mult
                                            ).then_inc(ph.sem("kbm"), 1)
                em.on("gpsimd", g_vbm)
                ph.bump("kbm")
            n_tm_done = ph.val("kbm")

            if _B_PART < 3:
                continue
            for bl in range(BL):
                def t_A(t, bl=bl, n_kbf=n_kbf, c_eapA=ph.val("eap")):
                    t.wait_ge(ph.sem("ab"), n_kbf)
                    t.wait_ge(ph.sem("eap"), c_eapA)
                    t.matmul(psC[:, 0:128], kbfm[:, bl * 128:(bl + 1) * 128],
                             kn[:, bl * 128:(bl + 1) * 128], start=True, stop=True)
                    t.matmul(psC[:, 128:256], kn[:, bl * 128:(bl + 1) * 128],
                             kbfm[:, bl * 128:(bl + 1) * 128], start=True, stop=True
                             ).then_inc(ph.sem("mA"), 1)
                em.on("tensor", t_A)
                n_mA = ph.bump("mA")

                def v_A(v, n_mA=n_mA, c_sq=ph.val("msq"), c_ap2=ph.val("map2")):
                    v.wait_ge(ph.sem("mA"), n_mA)
                    v.wait_ge(ph.sem("msq"), c_sq)
                    v.wait_ge(ph.sem("map2"), c_ap2)
                    v.tensor_mul(Xp_[0][:, :], psC[:, 0:128], sb_maskA[:, :])
                    v.tensor_mul(Yp_[0][:, :], psC[:, 128:256], sb_maskAT[:, :]
                                 ).then_inc(ph.sem("cc"), 1)
                em.on("vector", v_A)
                n_cc = ph.bump("cc")

                def t_ai(t, bl=bl, n_kn=n_kn, w_dd=pg("dd"), c_uie=ph.val("uie")):
                    t.wait_ge(ph.sem("ab"), n_kn)
                    t.wait_ge(ph.sem("uie"), c_uie)
                    pg_wait(t, w_dd)
                    t.matmul(psD[:, :], kn[:, bl * 128:(bl + 1) * 128],
                             qn[:, bl * 128:(bl + 1) * 128], start=True, stop=True
                             ).then_inc(ph.sem("mai"), 1)
                em.on("tensor", t_ai)
                n_mai = ph.bump("mai")

                def v_ai(v, bl=bl, n_mai=n_mai):
                    v.wait_ge(ph.sem("mai"), n_mai)
                    v.tensor_mul(aiT[:, bl, :], psD[:, :], sb_maskTIT[:, :]
                                 ).then_inc(ph.sem("dd"), 1)
                em.on("vector", v_ai)
                ph.bump("dd")

                for lev in range(5):
                    def t_sq(t, lev=lev, n_cc=n_cc, c_esq=ph.val("esq")):
                        if lev == 0:
                            t.wait_ge(ph.sem("cc"), n_cc)
                        else:
                            t.wait_ge(ph.sem("esq"), c_esq)
                        xs = Xp_[lev % 2]
                        ys = Yp_[lev]
                        t.matmul(psC[:, 0:128], ys[:, :], xs[:, :], start=True, stop=True)
                        t.matmul(psC[:, 128:256], xs[:, :], ys[:, :], start=True, stop=True
                                 ).then_inc(ph.sem("msq"), 1)
                    em.on("tensor", t_sq)
                    n_msq = ph.bump("msq")

                    def v_esq(v, lev=lev, n_msq=n_msq):
                        v.wait_ge(ph.sem("msq"), n_msq)
                        v.tensor_copy(Xp_[(lev + 1) % 2][:, :], psC[:, 0:128])
                        v.tensor_copy(Yp_[lev + 1][:, :], psC[:, 128:256]
                                      ).then_inc(ph.sem("esq"), 1)
                    em.on("vector", v_esq)
                    ph.bump("esq")
                n_esq = ph.val("esq")

                def v_eap0(v, bl=bl, n_tm=n_tm_done, c_ap=ph.val("map2")):
                    v.wait_ge(ph.sem("kbm"), n_tm)
                    v.wait_ge(ph.sem("map2"), c_ap)
                    v.tensor_copy(tsb[:, :], vkb[:, bl, :]).then_inc(ph.sem("eap"), 1)
                em.on("vector", v_eap0)
                n_eap = ph.bump("eap")

                for lev in range(6):
                    ysrc = Yp_[0] if lev == 0 else Yp_[lev]

                    def t_ap(t, lev=lev, ysrc=ysrc, n_eap=n_eap, n_esq=n_esq,
                             c_eapc=ph.val("eap")):
                        t.wait_ge(ph.sem("eap"), n_eap)
                        t.wait_ge(ph.sem("esq"), n_esq)
                        t.matmul(psC[:, :], ysrc[:, :], tsb[:, :], start=True, stop=True
                                 ).then_inc(ph.sem("map2"), 1)
                    em.on("tensor", t_ap)
                    n_map2 = ph.bump("map2")

                    def v_eap(v, lev=lev, bl=bl, n_map2=n_map2):
                        v.wait_ge(ph.sem("map2"), n_map2)
                        v.tensor_add(tsb[:, :], tsb[:, :], psC[:, :])
                        if lev == 5:
                            v.tensor_copy(utm[:, bl, :], tsb[:, 0:128])
                            v.tensor_copy(wtmp[:, :], tsb[:, 128:256]).then_inc(ph.sem("eap"), 1)
                        else:
                            v.engine_nop().then_inc(ph.sem("eap"), 1)
                    em.on("vector", v_eap)
                    n_eap = ph.bump("eap")

                def t_wT(t, bl=bl, n_eap=n_eap, w_ee=pg("ee")):
                    t.wait_ge(ph.sem("eap"), n_eap)
                    pg_wait(t, w_ee)
                    t.transpose(psE[:, :], wtmp[:, :], sb_identB[:, :]).then_inc(ph.sem("mwt"), 1)
                em.on("tensor", t_wT)
                n_mwt = ph.bump("mwt")

                def v_wT(v, bl=bl, n_mwt=n_mwt):
                    v.wait_ge(ph.sem("mwt"), n_mwt)
                    v.tensor_copy(wTf[:, bl, :], psE[:, :]).then_inc(ph.sem("ee"), 1)
                em.on("vector", v_wT)
                ph.bump("ee")
            n_blocks = (ph.val("ee"), ph.val("dd"), ph.val("eap"))

            if _B_PART < 4:
                continue
            c_ab_scan = ph.val("ab")
            for cc2 in range(2 * BL):
                bl, hh2 = cc2 // 2, cc2 % 2
                b0 = hh2 * 64
                po = psA if (cc2 // 8) % 2 == 0 else psB
                col = (cc2 % 8) * 64

                def t_ui(t, bl=bl, b0=b0, cc2=cc2, n_blocks=n_blocks,
                         c_sev=ph.val("scp"), c_uie=ph.val("uie")):
                    if cc2 == 0:
                        t.wait_ge(ph.sem("ee"), n_blocks[0])
                        t.wait_ge(ph.sem("dd"), n_blocks[1])
                        t.wait_ge(ph.sem("eap"), n_blocks[2])
                        return
                    t.wait_ge(ph.sem("scp"), c_sev)
                    t.wait_ge(ph.sem("uie"), c_uie)
                    t.matmul(psD[b0:b0 + 64, :], wTf[:, bl, b0:b0 + 64], Ssb[:, :],
                             start=True, stop=True, tile_position=(0, b0)
                             ).then_inc(ph.sem("mui"), 1)
                em.on("tensor", t_ui)
                n_mui = ph.bump("mui") if cc2 > 0 else ph.val("mui")

                def v_ui(v, bl=bl, b0=b0, cc2=cc2, n_mui=n_mui):
                    if cc2 > 0:
                        v.wait_ge(ph.sem("mui"), n_mui)
                        v.tensor_sub(uis[b0:b0 + 64, :], utm[b0:b0 + 64, bl, :],
                                     psD[b0:b0 + 64, :]).then_inc(ph.sem("uie"), 1)
                    else:
                        v.tensor_copy(uis[b0:b0 + 64, :], utm[b0:b0 + 64, bl, :]
                                      ).then_inc(ph.sem("uie"), 1)
                em.on("vector", v_ui)
                n_uie = ph.bump("uie")

                def t_sc(t, bl=bl, b0=b0, cc2=cc2, po=po, col=col, n_uie=n_uie,
                         w_oev=pg("oev", 1), n_kbf=n_kbf, c_ab_scan=c_ab_scan):
                    t.wait_ge(ph.sem("uie"), n_uie)
                    if cc2 == 0:
                        t.wait_ge(ph.sem("ab"), c_ab_scan)
                    if col == 0:
                        pg_wait(t, w_oev)
                    if cc2 > 0:
                        t.matmul(po[:, col:col + 64], Ssb[:, :],
                                 qn[:, cc2 * 64:(cc2 + 1) * 64], start=True, stop=False)
                    t.matmul(po[:, col:col + 64], uis[b0:b0 + 64, :],
                             aiT[b0:b0 + 64, bl, b0:b0 + 64], start=(cc2 == 0), stop=True)
                    t.matmul(psF[:, :], ktm[b0:b0 + 64, bl, :], uis[b0:b0 + 64, :],
                             start=True, stop=True).then_inc(ph.sem("msc"), 1)
                em.on("tensor", t_sc)
                n_msc = ph.bump("msc")

                def v_sev(v, cc2=cc2, n_msc=n_msc):
                    v.wait_ge(ph.sem("msc"), n_msc)
                    if cc2 == 0:
                        v.tensor_copy(Sf[:, :], psF[:, :]).then_inc(ph.sem("sev"), 1)
                    else:
                        v.tensor_add(Sf[:, :], Sf[:, :], psF[:, :]).then_inc(ph.sem("sev"), 1)
                em.on("vector", v_sev)
                n_sev = ph.bump("sev")

                def s_scp(sc, n_sev=n_sev):
                    sc.wait_ge(ph.sem("sev"), n_sev)
                    sc.activation(Ssb[:, :], Sf[:, :], AF.Identity).then_inc(ph.sem("scp"), 1)
                em.on("scalar", s_scp)
                ph.bump("scp")

                if cc2 % 8 == 7:
                    def v_oev(v, cc2=cc2, po=po, n_msc=ph.val("msc"),
                              c_ost=ph.val("ost")):
                        v.wait_ge(ph.sem("msc"), n_msc)
                        v.wait_ge(ph.sem("ost"), c_ost)
                        v.tensor_copy(odel[:, (cc2 - 7) * 64:(cc2 + 1) * 64], po[:, :]
                                      ).then_inc(ph.sem("oev"), 1)
                    em.on("vector", v_oev)
                    ph.bump("oev")
            n_oev = ph.val("oev")

            if _B_PART < 5:
                continue
            c_bbc_g = ph.val("bbc")
            for grp in range(4):
                def t_grow(t, s=s, h=h, grp=grp, c_bbc_g=c_bbc_g, w_gg=pg("gg")):
                    t.wait_ge(ph.sem("bbc"), c_bbc_g)
                    pg_wait(t, w_gg)
                    for q4 in range(4):
                        bt = grp * 4 + q4
                        k = t.transpose(psG[0:1, q4 * 128:q4 * 128 + 128],
                                        bmn[:, bt, 8 + h:9 + h], sb_identF[:, :])
                    k.then_inc(ph.sem("gtr"), 1)
                em.on("tensor", t_grow)
                n_gtr = ph.bump("gtr")

                def v_grow(v, grp=grp, n_gtr=n_gtr):
                    v.wait_ge(ph.sem("gtr"), n_gtr)
                    v.tensor_copy(rowA[0:1, grp * 512:(grp + 1) * 512], psG[0:1, :]
                                  ).then_inc(ph.sem("gg"), 1)
                em.on("vector", v_grow)
                ph.bump("gg")
            n_gr2 = ph.val("gg")

            def s_gsig(sc, h=h, n_gr2=n_gr2):
                sc.wait_ge(ph.sem("gg"), n_gr2)
                sc.activation(rowB[0:1, :], rowA[0:1, :], AF.Sigmoid,
                              bias=sb_mixsum[0:1, h:h + 1]).then_inc(ph.sem("gsg"), 1)
            em.on("scalar", s_gsig)
            n_gsg = ph.bump("gsg")

            for ct in range(NTT):
                ps = psA if ct % 2 == 0 else psB

                def t_gbc(t, ct=ct, ps=ps, n_gsg=n_gsg, n_oev=n_oev,
                          w_ab=pg("ab", 1)):
                    t.wait_ge(ph.sem("gsg"), n_gsg)
                    t.wait_ge(ph.sem("oev"), n_oev)
                    pg_wait(t, w_ab)
                    t.matmul(ps[:, :], ones_row_f[0:1, :], rowB[0:1, ct * TT:(ct + 1) * TT],
                             start=True, stop=True).then_inc(ph.sem("gbm"), 1)
                em.on("tensor", t_gbc)
                n_gbm = ph.bump("gbm")

                def v_gbe(v, ct=ct, ps=ps, n_gbm=n_gbm):
                    v.wait_ge(ph.sem("gbm"), n_gbm)
                    v.tensor_copy(gbb[:, ct * TT:(ct + 1) * TT], ps[:, :]
                                  ).then_inc(ph.sem("ab"), 1)
                em.on("vector", v_gbe)
                ph.bump("ab")
            n_gb = ph.val("ab")

            def v_og1(v, n_gb=n_gb, n_v=n_v):
                v.wait_ge(ph.sem("ab"), n_gb)
                v.tensor_sub(odel[:, :], odel[:, :], vf32[:, :]).then_inc(ph.sem("og1"), 1)
            em.on("vector", v_og1)
            n_og1 = ph.bump("og1")

            def g_og2(gp, n_og1=n_og1):
                gp.wait_ge(ph.sem("og1"), n_og1)
                gp.tensor_mul(odel[:, :], odel[:, :], gbb[:, :]).then_inc(ph.sem("og2"), 1)
            em.on("gpsimd", g_og2)
            n_og2 = ph.bump("og2")

            def v_og3(v, n_og2=n_og2):
                v.wait_ge(ph.sem("og2"), n_og2)
                v.tensor_add(odel[:, :], odel[:, :], vf32[:, :]).then_inc(ph.sem("og3"), 1)
            em.on("vector", v_og3)
            n_og3 = ph.bump("og3")

            def s_osq(sc, n_og3=n_og3, c_gm=ph.val("l2m")):
                sc.wait_ge(ph.sem("og3"), n_og3)
                sc.wait_ge(ph.sem("l2m"), c_gm)
                sc.activation(b16s[:, :], odel[:, :], AF.Square).then_inc(ph.sem("l2s"), 1)
            em.on("scalar", s_osq)
            n_osq = ph.bump("l2s")

            for ct in range(NTT):
                def t_osum(t, ct=ct, n_osq=n_osq, w_gg=pg("gg")):
                    t.wait_ge(ph.sem("l2s"), n_osq)
                    pg_wait(t, w_gg)
                    t.matmul(psG[0:1, :], ones_col_b[:, 0:1],
                             b16s[:, ct * TT:(ct + 1) * TT], start=True, stop=True
                             ).then_inc(ph.sem("l2m"), 1)
                em.on("tensor", t_osum)
                n_osm = ph.bump("l2m")

                def v_orow(v, ct=ct, n_osm=n_osm, c_rs=ph.val("orr")):
                    v.wait_ge(ph.sem("l2m"), n_osm)
                    if ct == 0:
                        v.wait_ge(ph.sem("orr"), c_rs)
                    v.tensor_copy(rowA[0:1, ct * TT:(ct + 1) * TT], psG[0:1, :]
                                  ).then_inc(ph.sem("gg"), 1)
                em.on("vector", v_orow)
                ph.bump("gg")
            n_ogg = ph.val("gg")

            def s_orstd(sc, n_ogg=n_ogg):
                sc.wait_ge(ph.sem("gg"), n_ogg)
                sc.activation(rowB[0:1, :], rowA[0:1, :], AF.Ln, scale=1.0 / 128,
                              bias=constf(EPS_LN)[0:1, 0:1])
                sc.activation(rowB[0:1, :], rowB[0:1, :], AF.Exp, scale=-0.5
                              ).then_inc(ph.sem("orr"), 1)
            em.on("scalar", s_orstd)
            n_orr = ph.bump("orr")

            for ct in range(NTT):
                ps = psA if ct % 2 == 0 else psB

                def t_obc(t, ct=ct, ps=ps, n_orr=n_orr, c_of=ph.val("ofin"),
                          w_ab=pg("ab", 1)):
                    t.wait_ge(ph.sem("orr"), n_orr)
                    t.wait_ge(ph.sem("ofin"), c_of)
                    pg_wait(t, w_ab)
                    t.matmul(ps[:, :], ones_row_f[0:1, :], rowB[0:1, ct * TT:(ct + 1) * TT],
                             start=True, stop=True).then_inc(ph.sem("obm"), 1)
                em.on("tensor", t_obc)
                n_obm = ph.bump("obm")

                def v_onm(v, ct=ct, ps=ps, n_obm=n_obm):
                    v.wait_ge(ph.sem("obm"), n_obm)
                    v.tensor_mul(bcb[:, ct * TT:(ct + 1) * TT],
                                 odel[:, ct * TT:(ct + 1) * TT], ps[:, :]
                                 ).then_inc(ph.sem("onm"), 1)
                em.on("vector", v_onm)
                ph.bump("onm")
            n_onm = ph.val("onm")

            def g_ofin(gp, s=s, h=h, n_onm=n_onm):
                gp.wait_ge(ph.sem("onm"), n_onm)
                gp.tensor_single_scalar(odel[:, :], bcb[:, :], sb_onorm[:, 0:1],
                                        ALU.mult).then_inc(ph.sem("ofin"), 1)
            em.on("gpsimd", g_ofin)
            n_ofin = ph.bump("ofin")

            def dma_on(e, s=s, h=h, n_ofin=n_ofin):
                e.wait_ge(ph.sem("ofin"), n_ofin)
                e.dma_start(s_r[s, h, :, :], odel[:, :]).then_inc(ph.sem("ost"), 16)
            em.on("sync", dma_on)
            ph.bump("ost", 16)

        def fin_s(e):
            e.wait_ge(ph.sem("ost"), ph.val("ost"))
        em.on("sync", fin_s)
        em.replay(nc)
        pes.close()
        return

    if _build_b:
        with nc.named_scope("bm"):
            phase_b_main()

    def phase_b_out():
        pes = ExitStack()
        Sl = lambda n, sh, dt: pes.enter_context(nc.sbuf_tensor(n, sh, dt))
        Pl = lambda n, sh: pes.enter_context(nc.psum_tensor(n, sh, F32))
        ph = Sems(nc, es, "bo")
        em = Emitter()
        ow_sb = Sl("bo_ow", [128, DC, D], BF16)
        u1 = Sl("bo_u1", [128, DC, TT], F32)
        onb = [Sl(f"bo_on{j}", [128, DC, TT], BF16) for j in range(2)]
        ps_u = [Pl(f"bo_ps{j}", [128, 512]) for j in range(2)]

        def wload(e):
            e.dma_start(ow_sb[:, :, :], owB.rearrange("(c p) o -> p c o", p=128)
                        ).then_inc(ph.sem("wld"), 16)
        em.on("gpsimd", wload)
        n_wld_b = ph.bump("wld", 16)

        tiles = [(s, tt) for s in range(NSEQ) for tt in range(NTT)]
        for i, (s, tt) in enumerate(tiles):
            jb = i % 2

            def dma_on(e, s=s, tt=tt, i=i, jb=jb, c_mm=ph.val("mm")):
                if i >= 2:
                    e.wait_ge(ph.sem("mm"), c_mm - DC)
                e.dma_start(onb[jb][:, :, :], dtile(s_r, s, tt * TT, (tt + 1) * TT)
                            ).then_inc(ph.sem("onl"), 16)
            em.on("gpsimd", dma_on)
            n_onl = ph.bump("onl", 16)

            for o in range(DC):
                b = o % 2

                def t_mm(t, s=s, tt=tt, o=o, b=b, jb=jb, n_onl=n_onl, c_ev=ph.val("ev"),
                         n_wld_b=n_wld_b):
                    t.wait_ge(ph.sem("wld"), n_wld_b)
                    t.wait_ge(ph.sem("onl"), n_onl)
                    if c_ev >= 2:
                        t.wait_ge(ph.sem("ev"), c_ev - 1)
                    for c in range(DC):
                        k = t.matmul(ps_u[b][:, :], ow_sb[:, c, o * 128:(o + 1) * 128],
                                     onb[jb][:, c, :], start=(c == 0), stop=(c == DC - 1))
                    k.then_inc(ph.sem("mm"), 1)
                em.on("tensor", t_mm)
                n_mm = ph.bump("mm")

                def v_ev(v, o=o, b=b, n_mm=n_mm, i=i, c_acc=ph.val("acc"),
                         c_st=ph.val("st")):
                    v.wait_ge(ph.sem("mm"), n_mm)
                    if o == 0 and i >= 1:
                        v.wait_ge(ph.sem("acc"), c_acc)
                        v.wait_ge(ph.sem("st"), c_st)
                    v.tensor_copy(u1[:, o, :], ps_u[b][:, :]).then_inc(ph.sem("ev"), 1)
                em.on("vector", v_ev)
                ph.bump("ev")
            n_ev = ph.val("ev")

            def dma_acc(e, s=s, tt=tt, n_ev=n_ev):
                e.wait_ge(ph.sem("ev"), n_ev)
                e.dma_start(u1[:, :, :], dtile(s_xB, s, tt * TT, (tt + 1) * TT),
                            accum_op=ALU.add).then_inc(ph.sem("acc"), 16)
            em.on("gpsimd", dma_acc)
            n_acc = ph.bump("acc", 16)

            def dma_st(e, s=s, tt=tt, n_acc=n_acc):
                e.wait_ge(ph.sem("acc"), n_acc)
                e.dma_start(dtile(s_xC, s, tt * TT, (tt + 1) * TT), u1[:, :, :]
                            ).then_inc(ph.sem("st"), 16)
            em.on("sync", dma_st)
            ph.bump("st", 16)

        def fin(e):
            e.wait_ge(ph.sem("st"), ph.val("st"))
        em.on("sync", fin)
        em.replay(nc)
        pes.close()

    if _build_b:
        with nc.named_scope("bo"):
            phase_b_out()
        closeA()

    if _PHASES >= 6:
        openA("a5")
        with nc.named_scope("f2l"):
            phase_f_ln("f2l", s_xC)
        openB("b5")
        with nc.named_scope("f2a"):
            phase_f_att("f2a")
        with nc.named_scope("f2o"):
            phase_f_out("f2o", wnatF)
        with nc.named_scope("f2m"):
            phase_f_moe("f2m", wnatF, s_xC, y_out)
        closeA(); closeB()


    _dram_by_name = {"s_r": s_r, "s_u2": s_u2, "s_xA": s_xA, "s_ffn_in": s_ffn_in,
                     "s_a": s_a, "s_xB": s_xB, "s_xC": s_xC, "xbfB": s_xC, "wnat": s_xC}

    def emit_copy_dram_to_y(src):
        pes2 = ExitStack()
        ph2 = Sems(nc, es, "fin")
        em2 = Emitter()
        buf = pes2.enter_context(nc.sbuf_tensor("finbuf", [128, DC, TT], F32))
        tiles = [(s, tt) for s in range(NSEQ) for tt in range(NTT)]
        base_n = ph2.val("st")

        def go(e, base_n=base_n):
            n = base_n
            for i, (s, tt) in enumerate(tiles):
                e.dma_start(buf[:, :, :], dtile(src, s, tt * TT, (tt + 1) * TT)
                            ).then_inc(ph2.sem("st"), 16)
                n += 16
                e.wait_ge(ph2.sem("st"), n)
                e.dma_start(dtile(y_out, s, tt * TT, (tt + 1) * TT), buf[:, :, :]
                            ).then_inc(ph2.sem("st"), 16)
                n += 16
            e.wait_ge(ph2.sem("st"), n)
        em2.on("sync", go)
        for _ in range(2 * len(tiles)):
            ph2.bump("st", 16)
        em2.replay(nc)
        pes2.close()

    if _DEBUG_TAP is not None:
        emit_copy_dram_to_y(_dram_by_name[_DEBUG_TAP])
        return nc

    if _PHASES < 6:
        emit_copy_dram_to_y(s_r)
    return nc



def _host_inputs(inputs):
    f32 = lambda a: np.ascontiguousarray(np.asarray(a), dtype=np.float32)
    shared = {}
    shared["wqkvT"] = f32(np.asarray(inputs["L0_attn_a_w_qkv"]).T)
    shared["woutT"] = f32(np.asarray(inputs["L0_attn_a_w_out"]).T)
    shared["w1T"] = f32(np.asarray(inputs["L0_attn_a_w1"]).T)
    shared["w2T"] = f32(np.asarray(inputs["L0_attn_a_w2"]).T)
    shared["projW"] = f32(inputs["L0_attn_a_proj"])
    vec2d = lambda v, c: f32(np.asarray(v).reshape(c, 128).T)
    shared["gammaA"] = vec2d(inputs["L0_attn_a_gamma"], DC)
    shared["betaA"] = vec2d(inputs["L0_attn_a_beta"], DC)
    shared["b_outA"] = vec2d(inputs["L0_attn_a_b_out"], DC)
    shared["b1A"] = vec2d(inputs["L0_attn_a_b1"], 32)
    shared["b2A"] = vec2d(inputs["L0_attn_a_b2"], DC)
    shared["Wq"] = f32(inputs["L0_ffn_Wq"]); shared["Wk"] = f32(inputs["L0_ffn_Wk"])
    shared["Wv"] = f32(inputs["L0_ffn_Wv"]); shared["Wo"] = f32(inputs["L0_ffn_Wo"])
    shared["bqF"] = vec2d(inputs["L0_ffn_bq"], DC); shared["bkF"] = vec2d(inputs["L0_ffn_bk"], DC)
    shared["bvF"] = vec2d(inputs["L0_ffn_bv"], DC); shared["boF"] = vec2d(inputs["L0_ffn_bo"], DC)
    shared["ln1g"] = vec2d(inputs["L0_ffn_ln1_g"], DC); shared["ln1b"] = vec2d(inputs["L0_ffn_ln1_b"], DC)
    shared["ln2g"] = vec2d(inputs["L0_ffn_ln2_g"], DC); shared["ln2b"] = vec2d(inputs["L0_ffn_ln2_b"], DC)
    shared["WrD"] = f32(inputs["L0_ffn_Wr"])
    shared["brD"] = f32(np.asarray(inputs["L0_ffn_br"]).reshape(1, 8))
    shared["ew1"] = f32(inputs["L0_ffn_e_w1"])
    shared["ew2"] = f32(inputs["L0_ffn_e_w2"])
    shared["eb1"] = f32(np.asarray(inputs["L0_ffn_e_b1"]).reshape(8, 32, 128).transpose(2, 0, 1).reshape(128, 256))
    shared["eb2"] = f32(np.asarray(inputs["L0_ffn_e_b2"]).reshape(8, 8, 128).transpose(2, 0, 1).reshape(128, 64))
    shared["qwB"] = f32(inputs["L0_attn_b_q_w"]); shared["kwB"] = f32(inputs["L0_attn_b_k_w"])
    shared["vwB"] = f32(inputs["L0_attn_b_v_w"]); shared["owB"] = f32(inputs["L0_attn_b_o_w"])
    shared["bwmix"] = f32(np.concatenate([np.asarray(inputs["L0_attn_b_b_w"]),
                                          np.asarray(inputs["L0_attn_b_mix_w"])], axis=1))
    cw = lambda v: f32(np.asarray(v).reshape(DC, 128, 4).transpose(1, 0, 2))
    shared["convq"] = cw(inputs["L0_attn_b_conv_q_w"])
    shared["convk"] = cw(inputs["L0_attn_b_conv_k_w"])
    shared["convv"] = cw(inputs["L0_attn_b_conv_v_w"])
    shared["mixb_in"] = f32(np.asarray(inputs["L0_attn_b_mix_b"]).reshape(1, 8))
    shared["mixbias_in"] = f32(np.asarray(inputs["L0_attn_b_mix_bias"]).reshape(1, 8))
    shared["onormW"] = f32(np.asarray(inputs["L0_attn_b_o_norm_w"]).reshape(128, 1))

    bf = ml_dtypes.bfloat16
    shared["identB"] = np.eye(128, dtype=bf)
    shared["identF"] = np.eye(128, dtype=np.float32)
    kq = (np.arange(128)[None, :] >= np.arange(128)[:, None])
    shared["mask_kq"] = kq.astype(bf)
    blk = (np.arange(128)[:, None] // 64) == (np.arange(128)[None, :] // 64)
    strict_low = blk & (np.arange(128)[None, :] < np.arange(128)[:, None])
    shared["maskA_neg"] = (-1.0 * strict_low).astype(bf)
    shared["maskAT_neg"] = (-1.0 * strict_low.T).astype(bf)
    shared["maskTriIncT"] = (blk & (np.arange(128)[:, None] <= np.arange(128)[None, :])).astype(bf)
    for m in range(4):
        shared[f"m512_{m}"] = (np.arange(512)[None, :] >= (m * 128 + np.arange(128)[:, None])).astype(bf)

    x = np.asarray(inputs["L0_attn_a_x"], dtype=np.float32)
    in_maps = []
    for c in range(NCORE):
        m = dict(shared)
        xs = np.ascontiguousarray(x[2 * c:2 * c + 2].transpose(0, 2, 1))
        m["x"] = xs.reshape(NSEQ, DC, 128, T)
        in_maps.append(m)
    return in_maps


def _assemble(res):
    outs = []
    for c in range(NCORE):
        y = res.results[c]["y"].reshape(NSEQ, D, T)
        outs.append(y.transpose(0, 2, 1))
    return np.ascontiguousarray(np.concatenate(outs, axis=0), dtype=np.float32)


def kernel(**inputs):
    in_maps = _host_inputs(inputs)
    nc = build_program()
    res = run_bass_kernel_spmd(nc, in_maps, core_ids=list(range(NCORE)))
    return _assemble(res)


# version 06 (final)
# speedup vs baseline: 1.1409x; 1.0583x over previous
from contextlib import ExitStack

import numpy as np
import ml_dtypes

import concourse.bass as bass
import concourse.mybir as mybir
from concourse.bass_utils import run_bass_kernel_spmd

F32 = mybir.dt.float32
BF16 = mybir.dt.bfloat16
AF = mybir.ActivationFunctionType
ALU = mybir.AluOpType
AX = mybir.AxisListType

NCORE = 8
NSEQ = 2
T = 2048
D = 1024
DC = D // 128
TT = 512
NTT = T // TT
EPS_LN = 1e-5

_DEBUG_TAP = None
_PHASES = 99
_F_ATT_HP = 8
_F_ATT_PART = 99
_B_PART = 99
_B_UNITS = 99
_B_ILV = True
_SKIP_EARLY = False


class Emitter:
    ENGINES = ("sync", "gpsimd", "tensor", "vector", "scalar")

    def __init__(self):
        self.ops = {e: [] for e in self.ENGINES}

    def on(self, eng, fn):
        self.ops[eng].append(fn)

    def replay(self, nc):
        with nc.Block() as blk:
            for eng in self.ENGINES:
                if not self.ops[eng]:
                    continue
                def run(e, _ops=tuple(self.ops[eng])):
                    for f in _ops:
                        f(e)
                getattr(blk, eng)(run)


class Sems:

    _REG = None

    def __init__(self, nc, es, prefix):
        if Sems._REG is None or Sems._REG[0] is not nc:
            Sems._REG = (nc, es, {}, {})
        object.__setattr__(self, "_d", None)

    ALIAS = {}
    for _grp in ("og1 og2 og3|og", "r1 r2 r2b r3 r4 r5 r6 r8 mlg sg|rt",
                 "cv1 cv2 cv3 cv4|cv", "l2s l2m l2r l2c|l2x", "btr gtr|tr3",
                 "bmm mA mai|mai", "gt gr gb gv|gat", "map map2|map",
                 "mui msc|msn", "ktr vtr mwt kbm|trm", "gbm obm bbc|bcm",
                 "gsg orr|sgr", "onm ofin|onf", "elv1 els|elx", "inv mzb|izb",
                 "mst|mp", "mmq mmv mmf|mma", "zfin vz|zf", "mat mkv|mk",
                 "sev uie|su",
                 "ld lw1 lw2 cl|dls",
                 "ldu wl bl owl onl cw pl|dlg",
                 "cl2|cl2", "ac1 ac2 acc2|acc", "sta stx|st", "evo|ev",
                 "vinit brow sq cast bc var app aff bf stat rstd|mi1",
                 "bsg bme|mi2"):
        _names, _to = _grp.split("|")
        for _n in _names.split():
            ALIAS[_n] = _to

    def sem(self, name):
        name = Sems.ALIAS.get(name, name)
        nc, es, sems, cnt = Sems._REG
        if name not in sems:
            sems[name] = es.enter_context(nc.semaphore(f"s_{name}"))
            cnt[name] = 0
        return sems[name]

    def bump(self, name, k=1):
        name = Sems.ALIAS.get(name, name)
        self.sem(name)
        Sems._REG[3][name] += k
        return Sems._REG[3][name]

    def val(self, name):
        name = Sems.ALIAS.get(name, name)
        self.sem(name)
        return Sems._REG[3][name]


def build_program():
    nc = bass.Bass("TRN2", target_bir_lowering=False, detect_race_conditions=False)
    es = ExitStack()

    def S(name, shape, dtype):
        return es.enter_context(nc.sbuf_tensor(name, shape, dtype))

    def P(name, shape, dtype=F32):
        return es.enter_context(nc.psum_tensor(name, shape, dtype))

    DI = lambda name, shape, dtype=F32: nc.dram_tensor(name, shape, dtype, kind="ExternalInput")
    DS = lambda name, shape, dtype=F32: nc.dram_tensor(name, shape, dtype, kind="Internal")

    x_in = DI("x", [NSEQ, DC, 128, T])
    y_out = nc.dram_tensor("y", [NSEQ, DC, 128, T], F32, kind="ExternalOutput")

    wqkvT = DI("wqkvT", [D, 3 * D])
    woutT = DI("woutT", [D, D])
    w1T = DI("w1T", [D, 4 * D])
    w2T = DI("w2T", [4 * D, D])
    projW = DI("projW", [16, 64, 256])
    gammaA = DI("gammaA", [128, DC]); betaA = DI("betaA", [128, DC])
    b_outA = DI("b_outA", [128, DC]); b1A = DI("b1A", [128, 32]); b2A = DI("b2A", [128, DC])

    Wq = DI("Wq", [D, D]); Wk = DI("Wk", [D, D]); Wv = DI("Wv", [D, D]); Wo = DI("Wo", [D, D])
    bqF = DI("bqF", [128, DC]); bkF = DI("bkF", [128, DC]); bvF = DI("bvF", [128, DC]); boF = DI("boF", [128, DC])
    ln1g = DI("ln1g", [128, DC]); ln1b = DI("ln1b", [128, DC])
    ln2g = DI("ln2g", [128, DC]); ln2b = DI("ln2b", [128, DC])
    WrD = DI("WrD", [D, 8]); brD = DI("brD", [1, 8])
    ew1 = DI("ew1", [8, D, 4 * D]); ew2 = DI("ew2", [8, 4 * D, D])
    eb1 = DI("eb1", [128, 8 * 32]); eb2 = DI("eb2", [128, 8 * 8])

    qwB = DI("qwB", [D, D]); kwB = DI("kwB", [D, D]); vwB = DI("vwB", [D, D]); owB = DI("owB", [D, D])
    bwmix = DI("bwmix", [D, 16])
    convq = DI("convq", [128, DC, 4]); convk = DI("convk", [128, DC, 4]); convv = DI("convv", [128, DC, 4])
    mixb_in = DI("mixb_in", [1, 8]); mixbias_in = DI("mixbias_in", [1, 8])
    onormW = DI("onormW", [128, 1])

    identB = DI("identB", [128, 128], BF16)
    identF = DI("identF", [128, 128])
    mask_kq = DI("mask_kq", [128, 128], BF16)
    maskA_neg = DI("maskA_neg", [128, 128], BF16)
    maskAT_neg = DI("maskAT_neg", [128, 128], BF16)
    maskTriIncT = DI("maskTriIncT", [128, 128], BF16)
    m512 = [DI(f"m512_{m}", [128, 512], BF16) for m in range(4)]

    SHP = [NSEQ, DC, 128, T]
    s_r = DS("s_r", SHP)
    s_u2 = DS("s_u2", SHP)
    s_xA = DS("s_xA", SHP)
    s_ffn_in = DS("s_ffn_in", SHP)
    s_a = DS("s_a", SHP)
    s_xB = DS("s_xB", SHP)
    s_xC = DS("s_xC", SHP)
    ew1b = nc.dram_tensor("ew1b", [8, D, 4 * D], BF16, kind="Internal")
    ew2b = nc.dram_tensor("ew2b", [8, 4 * D, D], BF16, kind="Internal")

    def dtile(dr, s, t0, t1):
        return dr[s, :, :, t0:t1].rearrange("c p t -> p c t")

    ones_col_b = S("ones_col_b", [128, 1], BF16)
    ones_row_b = S("ones_row_b", [1, 128], BF16)
    ones_row_f = S("ones_row_f", [1, 128], F32)
    sb_identB = S("sb_identB", [128, 128], BF16)
    sb_identF = S("sb_identF", [128, 128], F32)
    sb_mask_kq = S("sb_mask_kq", [128, 128], BF16)
    sb_maskA = S("sb_maskA", [128, 128], BF16)
    sb_maskAT = S("sb_maskAT", [128, 128], BF16)
    sb_maskTIT = S("sb_maskTIT", [128, 128], BF16)
    sb_m512 = [S(f"sb_m512_{m}", [128, 512], BF16) for m in range(4)]

    sb_gA = S("sb_gA", [128, DC], F32); sb_bA = S("sb_bA", [128, DC], F32)
    sb_boutA = S("sb_boutA", [128, DC], F32)
    sb_b1A = S("sb_b1A", [128, 32], F32); sb_b2A = S("sb_b2A", [128, DC], F32)
    sb_bqF = S("sb_bqF", [128, DC], F32); sb_bkF = S("sb_bkF", [128, DC], F32)
    sb_bvF = S("sb_bvF", [128, DC], F32); sb_boF = S("sb_boF", [128, DC], F32)
    sb_ln1g = S("sb_ln1g", [128, DC], F32); sb_ln1b = S("sb_ln1b", [128, DC], F32)
    sb_ln2g = S("sb_ln2g", [128, DC], F32); sb_ln2b = S("sb_ln2b", [128, DC], F32)
    sb_Wr = S("sb_Wr", [128, DC, 8], BF16)
    sb_br = S("sb_br", [1, 8], F32)
    sb_br_b = S("sb_br_b", [1, 8], BF16)
    sb_eb1 = S("sb_eb1", [128, 8 * 32], F32)
    sb_eb2 = S("sb_eb2", [128, 8 * 8], F32)
    sb_convq = S("sb_convq", [128, DC, 4], F32)
    sb_convk = S("sb_convk", [128, DC, 4], F32)
    sb_convv = S("sb_convv", [128, DC, 4], F32)
    sb_mix8a = S("sb_mix8a", [1, 8], F32)
    sb_mixsum = S("sb_mixsum", [1, 8], F32)
    sb_onorm = S("sb_onorm", [128, 1], F32)

    gsem = Sems(nc, es, "g")

    const_pool = {}

    def constf(val):
        if val not in const_pool:
            const_pool[val] = S(f"constf_{len(const_pool)}", [128, 1], F32)
        return const_pool[val]

    for _v in (1e-5, 1e-12, 1e-8):
        constf(_v)

    em = Emitter()

    def _const_loads(e):
        loads = [
            (sb_identB[:, :], identB[:, :]), (sb_identF[:, :], identF[:, :]),
            (sb_mask_kq[:, :], mask_kq[:, :]),
            (sb_maskA[:, :], maskA_neg[:, :]), (sb_maskAT[:, :], maskAT_neg[:, :]),
            (sb_maskTIT[:, :], maskTriIncT[:, :]),
            (sb_m512[0][:, :], m512[0][:, :]), (sb_m512[1][:, :], m512[1][:, :]),
            (sb_m512[2][:, :], m512[2][:, :]), (sb_m512[3][:, :], m512[3][:, :]),
            (sb_gA[:, :], gammaA[:, :]), (sb_bA[:, :], betaA[:, :]),
            (sb_boutA[:, :], b_outA[:, :]), (sb_b1A[:, :], b1A[:, :]), (sb_b2A[:, :], b2A[:, :]),
            (sb_bqF[:, :], bqF[:, :]), (sb_bkF[:, :], bkF[:, :]),
            (sb_bvF[:, :], bvF[:, :]), (sb_boF[:, :], boF[:, :]),
            (sb_ln1g[:, :], ln1g[:, :]), (sb_ln1b[:, :], ln1b[:, :]),
            (sb_ln2g[:, :], ln2g[:, :]), (sb_ln2b[:, :], ln2b[:, :]),
            (sb_br[:, :], brD[:, :]),
            (sb_eb1[:, :], eb1[:, :]), (sb_eb2[:, :], eb2[:, :]),
            (sb_convq[:, :, :], convq[:, :, :]), (sb_convk[:, :, :], convk[:, :, :]),
            (sb_convv[:, :, :], convv[:, :, :]),
            (sb_onorm[:, :], onormW[:, :]),
            (sb_mix8a[:, :], mixb_in[:, :]),
            (sb_mixsum[:, :], mixbias_in[:, :]),
        ]
        for dst, src in loads:
            e.dma_start(dst, src).then_inc(gsem.sem("cl"), 16)
            gsem.bump("cl", 16)

    def _cast_loads(e):
        e.dma_start(sb_Wr[:, :, :], WrD.rearrange("(c p) o -> p c o", p=128)).then_inc(gsem.sem("cw"), 16)
        gsem.bump("cw", 16)

    def _const_init(v):
        v.wait_ge(gsem.sem("cl"), gsem.val("cl"))
        v.wait_ge(gsem.sem("cw"), gsem.val("cw"))
        v.memset(ones_col_b[:, :], 1.0)
        v.memset(ones_row_b[:, :], 1.0)
        v.memset(ones_row_f[:, :], 1.0)
        v.tensor_add(sb_mixsum[:, :], sb_mixsum[:, :], sb_mix8a[:, :])
        v.tensor_copy(sb_br_b[:, :], sb_br[:, :])
        for _v, _t in const_pool.items():
            v.memset(_t[:, :], _v)

    em.on("sync", _const_loads)
    em.on("gpsimd", _cast_loads)
    em.on("vector", _const_init)
    em.replay(nc)

    def emit_moe_casts(em, eids):
        def go(e, eids=tuple(eids)):
            for ei in eids:
                e.dma_start(ew1b[ei, :, :], ew1[ei, :, :]).then_inc(gsem.sem("mcast"), 16)
                e.dma_start(ew2b[ei, :, :], ew2[ei, :, :]).then_inc(gsem.sem("mcast"), 16)
                gsem.bump("mcast", 32)
        em.on("gpsimd", go)

    ln_sems = Sems(nc, es, "LN")

    class LNHelper:

        def __init__(self, tag, stack, ext_xb=None, ext_xsq=None):
            Sl = lambda n, sh, dt: stack.enter_context(nc.sbuf_tensor(n, sh, dt))
            Pl = lambda n, sh: stack.enter_context(nc.psum_tensor(n, sh, F32))
            self.ps_stats = Pl(f"lnps_{tag}", [1, 2 * TT])
            self.ps_ab = Pl(f"lnab_{tag}", [128, 2 * TT])
            if ext_xb is None:
                t = Sl(f"lnxb_{tag}", [128, DC * TT], BF16)
                ext_xb = (t[:, :], lambda c, t=t: t[:, c * TT:(c + 1) * TT])
            if ext_xsq is None:
                t2 = Sl(f"lnxq_{tag}", [128, DC * TT], BF16)
                ext_xsq = (t2[:, :], lambda c, t2=t2: t2[:, c * TT:(c + 1) * TT])
            self.xb_full, self.xb_c = ext_xb
            self.xsq_full, self.xsq_c = ext_xsq
            self.rows = Sl(f"lnrw_{tag}", [1, 2 * TT], F32)
            self.mu = Sl(f"lnmu_{tag}", [1, TT], F32)
            self.var = Sl(f"lnvr_{tag}", [1, TT], F32)
            self.sem = ln_sems

        def emit(self, em, xin, rt, store_bf=None, gain=None, bias=None,
                 rt_ready_wait=None, xin_wait=None):
            sem = self.sem
            c_stat = sem.val("stat")
            c_var = sem.val("var")
            c_rstd = sem.val("rstd")
            c_app = sem.val("app")
            c_aff = sem.val("aff")

            def v_cast(v):
                if xin_wait is not None:
                    xin_wait(v)
                v.wait_ge(sem.sem("stat"), c_stat)
                v.tensor_copy(self.xb_full, xin.rearrange("p c t -> p (c t)")
                              ).then_inc(sem.sem("cast"), 1)
            em.on("vector", v_cast)
            n_cast = sem.bump("cast")

            def s_sq(s):
                if xin_wait is not None:
                    xin_wait(s)
                s.wait_ge(sem.sem("stat"), c_stat)
                s.square(self.xsq_full, xin.rearrange("p c t -> p (c t)")
                         ).then_inc(sem.sem("sq"), 1)
            em.on("scalar", s_sq)
            n_sq = sem.bump("sq")

            def t_stats(t):
                t.wait_ge(sem.sem("cast"), n_cast)
                t.wait_ge(sem.sem("sq"), n_sq)
                t.wait_ge(sem.sem("var"), c_var)
                for c in range(DC):
                    t.matmul(self.ps_stats[0:1, 0:TT], ones_col_b[:, 0:1],
                             self.xb_c(c), start=(c == 0), stop=(c == DC - 1))
                for c in range(DC):
                    i = t.matmul(self.ps_stats[0:1, TT:2 * TT], ones_col_b[:, 0:1],
                                 self.xsq_c(c), start=(c == 0), stop=(c == DC - 1))
                i.then_inc(sem.sem("stat"), 1)
            em.on("tensor", t_stats)
            n_stat = sem.bump("stat")

            def v_rows(v):
                v.wait_ge(sem.sem("stat"), n_stat)
                v.wait_ge(sem.sem("rstd"), c_rstd)
                v.tensor_single_scalar(self.mu[0:1, :], self.ps_stats[0:1, 0:TT], 1.0 / D, ALU.mult)
                v.tensor_single_scalar(self.var[0:1, :], self.ps_stats[0:1, TT:2 * TT], 1.0 / D, ALU.mult)
                v.tensor_mul(self.rows[0:1, 0:TT], self.mu[0:1, :], self.mu[0:1, :])
                v.tensor_sub(self.var[0:1, :], self.var[0:1, :], self.rows[0:1, 0:TT]
                             ).then_inc(sem.sem("var"), 1)
            em.on("vector", v_rows)
            n_var = sem.bump("var")

            def s_rstd(s):
                s.wait_ge(sem.sem("var"), n_var)
                s.activation(self.rows[0:1, 0:TT], self.var[0:1, :], AF.Ln, bias=constf(EPS_LN)[0:1, 0:1])
                s.activation(self.rows[0:1, 0:TT], self.rows[0:1, 0:TT], AF.Exp, scale=-0.5
                             ).then_inc(sem.sem("rstd"), 1)
            em.on("scalar", s_rstd)
            n_rstd = sem.bump("rstd")

            def v_brow(v):
                v.wait_ge(sem.sem("rstd"), n_rstd)
                v.tensor_mul(self.rows[0:1, TT:2 * TT], self.mu[0:1, :], self.rows[0:1, 0:TT]
                             ).then_inc(sem.sem("brow"), 1)
            em.on("vector", v_brow)
            n_brow = sem.bump("brow")

            def t_bc(t):
                t.wait_ge(sem.sem("brow"), n_brow)
                t.wait_ge(sem.sem("app"), c_app)
                t.matmul(self.ps_ab[:, 0:TT], ones_row_f[0:1, :], self.rows[0:1, 0:TT],
                         start=True, stop=True)
                t.matmul(self.ps_ab[:, TT:2 * TT], ones_row_f[0:1, :], self.rows[0:1, TT:2 * TT],
                         start=True, stop=True).then_inc(sem.sem("bc"), 1)
            em.on("tensor", t_bc)
            n_bc = sem.bump("bc")

            fin_on_vector = (gain is None)

            def v_apply(v):
                v.wait_ge(sem.sem("bc"), n_bc)
                if gain is not None:
                    v.wait_ge(sem.sem("aff"), c_aff)
                if rt_ready_wait is not None:
                    rt_ready_wait(v)
                last = None
                for c in range(DC):
                    v.tensor_mul(rt[:, c, :], xin[:, c, :], self.ps_ab[:, 0:TT])
                    last = v.tensor_sub(rt[:, c, :], rt[:, c, :], self.ps_ab[:, TT:2 * TT])
                    if fin_on_vector and store_bf is not None:
                        last = v.tensor_copy(store_bf(c), rt[:, c, :])
                last.then_inc(sem.sem("app"), 1)
            em.on("vector", v_apply)
            n_app = sem.bump("app")
            if gain is None:
                return (sem.sem("app"), n_app)

            if gain is not None:
                def s_aff(s):
                    s.wait_ge(sem.sem("app"), n_app)
                    last = None
                    for c in range(DC):
                        last = s.activation(rt[:, c, :], rt[:, c, :], AF.Identity,
                                            bias=bias[:, c:c + 1], scale=gain[:, c:c + 1])
                    last.then_inc(sem.sem("aff"), 1)
                em.on("scalar", s_aff)
                n_aff = sem.bump("aff")
                if store_bf is None:
                    return (sem.sem("aff"), n_aff)
                def v_bf(v):
                    v.wait_ge(sem.sem("aff"), n_aff)
                    last = None
                    for c in range(DC):
                        last = v.tensor_copy(store_bf(c), rt[:, c, :])
                    last.then_inc(sem.sem("bf"), 1)
                em.on("vector", v_bf)
                n_bf = sem.bump("bf")
                return (sem.sem("bf"), n_bf)

    def ln_phase(tag, src_dram, dst_dram, xbf_sb, gain=None, bias=None):
        pes = ExitStack()
        ph = Sems(nc, es, tag)
        em = Emitter()
        ln = LNHelper(tag, pes)
        Sl = lambda n, sh, dt: pes.enter_context(nc.sbuf_tensor(n, sh, dt))
        xin = [Sl(f"{tag}_xin{j}", [128, DC, TT], F32) for j in range(2)]
        rt = [Sl(f"{tag}_rt{j}", [128, DC, TT], F32) for j in range(2)] \
            if dst_dram is not None else xin
        tiles = [(s, tt) for s in range(NSEQ) for tt in range(NTT)]
        done_handles = []
        for i, (s, tt) in enumerate(tiles):
            j = i % 2
            def dma_in(e, s=s, tt=tt, i=i, j=j, done_handles=done_handles):
                if i >= 2:
                    h = done_handles[i - 2]
                    e.wait_ge(h[0], h[1])
                e.dma_start(xin[j][:, :, :], dtile(src_dram, s, tt * TT, (tt + 1) * TT)
                            ).then_inc(ph.sem("ld"), 16)
            em.on("sync", dma_in)
            n_ld = ph.bump("ld", 16)

            def xin_wait(eng, n_ld=n_ld):
                eng.wait_ge(ph.sem("ld"), n_ld)

            c_st_r = ph.val("st")

            def rt_wait(eng, i=i, c_st_r=c_st_r):
                if i >= 2 and dst_dram is not None:
                    eng.wait_ge(ph.sem("st"), c_st_r - 16)

            store_bf = None
            if xbf_sb is not None:
                def store_bf(c, s=s, tt=tt):
                    return xbfA_ap(s, c, tt * TT, (tt + 1) * TT)

            h = ln.emit(em, xin[j][:, :, :], rt[j][:, :, :], store_bf=store_bf,
                        gain=gain, bias=bias, xin_wait=xin_wait, rt_ready_wait=rt_wait)
            done_handles.append(h)

            def dma_out(e, j=j, s=s, tt=tt, h=h):
                e.wait_ge(h[0], h[1])
                if dst_dram is not None:
                    e.dma_start(dtile(dst_dram, s, tt * TT, (tt + 1) * TT), rt[j][:, :, :]
                                ).then_inc(ph.sem("st"), 16)
            em.on("sync", dma_out)
            if dst_dram is not None:
                ph.bump("st", 16)
        def fin(e):
            if dst_dram is not None:
                e.wait_ge(ph.sem("st"), ph.val("st"))
        em.on("sync", fin)
        em.replay(nc)
        pes.close()

    XA = [None, None]
    XB = [None, None]

    def openA(tag):
        st = ExitStack()
        XA[0] = st
        XA[1] = st.enter_context(nc.sbuf_tensor(f"xa_{tag}", [128, NSEQ * DC * T], BF16,
                                                side="left"))

    def closeA():
        XA[0].close(); XA[0] = XA[1] = None

    def openB(tag):
        st = ExitStack()
        XB[0] = st
        XB[1] = st.enter_context(nc.sbuf_tensor(f"xb_{tag}", [128, NSEQ * DC * T], BF16,
                                                side="right"))

    def closeB():
        XB[0].close(); XB[0] = XB[1] = None

    def xbfA_ap(s, c, t0, t1):
        base = (s * DC + c) * T
        return XA[1][:, base + t0: base + t1]

    if not _SKIP_EARLY:
        openA("a1")
        with nc.named_scope("aln"):
            ln_phase("aln", x_in, s_r, "XA")

    def xbfB_ap(s, c, t0, t1):
        base = (s * DC + c) * T
        return XB[1][:, base + t0: base + t1]

    def phase_a_attn():
        pes = ExitStack()
        Sl = lambda n, sh, dt: pes.enter_context(nc.sbuf_tensor(n, sh, dt))
        Pl = lambda n, sh: pes.enter_context(nc.psum_tensor(n, sh, F32))
        ph = Sems(nc, es, "aat")
        em = Emitter()
        NH = 16
        sb_proj = Sl("sb_proj", [128, 16, 256], BF16)

        def proj_load(e):
            e.dma_start(sb_proj[0:64, :, :], projW.rearrange("h d f -> d h f")).then_inc(ph.sem("pl"), 16)
            e.dma_start(sb_proj[64:128, :, :], projW.rearrange("h d f -> d h f")).then_inc(ph.sem("pl"), 16)
        em.on("gpsimd", proj_load)
        n_pl = ph.bump("pl", 32)

        def proj_gate(t, n_pl=n_pl):
            t.wait_ge(ph.sem("pl"), n_pl)
        em.on("tensor", proj_gate)

        wqk = [Sl(f"aat_wqk{j}", [128, DC, 128], BF16) for j in range(2)]
        wv = [Sl(f"aat_wv{j}", [128, DC, 64], BF16) for j in range(2)]
        qkT = Sl("aat_qkT", [128, T], BF16)
        v_aug = Sl("aat_vaug", [128, 16, 65], BF16)
        qfT = Sl("aat_qfT", [128, 2, T], BF16)
        kfN = Sl("aat_kfN", [128, 16, 256], BF16)
        kv_sb = Sl("aat_kv", [128, 2, 65], BF16)
        elu_m = Sl("aat_em", [128, 512], F32)
        elu_r = Sl("aat_er", [128, 512], F32)
        elu_e = Sl("aat_ee", [128, 512], F32)
        zr_t = Sl("aat_zrt", [1, 512], F32)
        zr = Sl("aat_zr", [1, 512], F32)
        zb_sb = Sl("aat_zb", [128, 512], F32)
        ps_qk = [Pl("aat_psqk0", [128, 512])]
        ps_v = Pl("aat_psv", [128, 64])
        ps_f = [Pl(f"aat_psf{j}", [128, 512]) for j in range(2)]
        ps_kv = Pl("aat_pskv", [128, 65])
        ps_at = Pl("aat_psat", [128, 512])
        ps_z = Pl("aat_psz", [1, 512])
        ps_zb = Pl("aat_pszb", [128, 512])

        def vaug_init(v):
            v.memset(v_aug[:, :, :], 0.0)
            for nt in range(16):
                v.memset(v_aug[:, nt, 64:65], 1.0)
            v.nop().then_inc(ph.sem("vinit"), 1)
        em.on("vector", vaug_init)
        n_vinit = ph.bump("vinit")

        base_wf = ph.val("wfree")
        for h in range(NH):
            j = h % 2
            def wload(e, h=h, j=j, base_wf=base_wf):
                if h >= 2:
                    e.wait_ge(ph.sem("wfree"), base_wf + h - 1)
                e.dma_start(wqk[j][:, :, 0:64],
                            wqkvT[:, 64 * h:64 * h + 64].rearrange("(c p) o -> p c o", p=128)
                            ).then_inc(ph.sem("wld"), 16)
                e.dma_start(wqk[j][:, :, 64:128],
                            wqkvT[:, D + 64 * h:D + 64 * h + 64].rearrange("(c p) o -> p c o", p=128)
                            ).then_inc(ph.sem("wld"), 16)
                e.dma_start(wv[j][:, :, :],
                            wqkvT[:, 2 * D + 64 * h:2 * D + 64 * h + 64].rearrange("(c p) o -> p c o", p=128)
                            ).then_inc(ph.sem("wld"), 16)
            em.on("gpsimd", wload)
            n_wld = ph.bump("wld", 48)

            for s in range(NSEQ):
                u = h * NSEQ + s
                c_mmf_g = ph.val("mmf")
                for tt in range(NTT):
                    b = 0
                    c_evq = ph.val("evq")

                    def t_qk(t, h=h, s=s, tt=tt, b=b, n_wld=n_wld, c_evq=c_evq, j=j):
                        t.wait_ge(ph.sem("wld"), n_wld)
                        if c_evq >= 1:
                            t.wait_ge(ph.sem("evq"), c_evq)
                        for c in range(DC):
                            i = t.matmul(ps_qk[b][:, :], wqk[j][:, c, :],
                                         xbfA_ap(s, c, tt * TT, (tt + 1) * TT),
                                         start=(c == 0), stop=(c == DC - 1))
                        i.then_inc(ph.sem("mmq"), 1)
                    em.on("tensor", t_qk)
                    n_mmq = ph.bump("mmq")

                    def v_evq(v, tt=tt, b=b, n_mmq=n_mmq, c_mmf_g=c_mmf_g):
                        v.wait_ge(ph.sem("mmq"), n_mmq)
                        v.wait_ge(ph.sem("mmf"), c_mmf_g)
                        v.tensor_copy(qkT[:, tt * TT:(tt + 1) * TT], ps_qk[b][:, :]
                                      ).then_inc(ph.sem("evq"), 1)
                    em.on("vector", v_evq)
                    ph.bump("evq")
                n_evq_done = ph.val("evq")

                c_mkv_v = ph.val("mkv")
                for nt in range(16):
                    c_evv = ph.val("evv")

                    def t_v(t, s=s, nt=nt, c_evv=c_evv, j=j, u=u):
                        if nt == 0 and u == 0:
                            t.wait_ge(ph.sem("vinit"), n_vinit)
                        if c_evv >= 1:
                            t.wait_ge(ph.sem("evv"), c_evv)
                        for c in range(DC):
                            i = t.matmul(ps_v[:, :], xbfA_ap(s, c, nt * 128, (nt + 1) * 128),
                                         wv[j][:, c, :], start=(c == 0), stop=(c == DC - 1))
                        i.then_inc(ph.sem("mmv"), 1)
                    em.on("tensor", t_v)
                    n_mmv = ph.bump("mmv")

                    def v_evv(v, nt=nt, n_mmv=n_mmv, c_mkv_v=c_mkv_v):
                        v.wait_ge(ph.sem("mmv"), n_mmv)
                        v.wait_ge(ph.sem("mkv"), c_mkv_v)
                        v.tensor_copy(v_aug[:, nt, 0:64], ps_v[:, :]).then_inc(ph.sem("evv"), 1)
                    em.on("vector", v_evv)
                    ph.bump("evv")
                if s == NSEQ - 1:
                    def t_wfree(t, n=ph.val("mmv")):
                        t.nop().then_inc(ph.sem("wfree"), 1)
                    em.on("tensor", t_wfree)
                    ph.bump("wfree")

                def emit_elu(ps, n_mm, out_ap, width, out_guard=None):
                    c_els = ph.val("els")
                    c_elf = ph.val("elf")

                    def v_1(v):
                        v.wait_ge(ph.sem("mmf"), n_mm)
                        v.wait_ge(ph.sem("els"), c_els)
                        v.tensor_single_scalar(elu_m[:, 0:width], ps, 0.0, ALU.min)
                        v.tensor_sub(elu_r[:, 0:width], ps, elu_m[:, 0:width]
                                     ).then_inc(ph.sem("elv1"), 1)
                    em.on("vector", v_1)
                    n1 = ph.bump("elv1")

                    def s_1(sc):
                        sc.wait_ge(ph.sem("elv1"), n1)
                        sc.wait_ge(ph.sem("elf"), c_elf)
                        sc.activation(elu_e[:, 0:width], elu_m[:, 0:width], AF.Exp
                                      ).then_inc(ph.sem("els"), 1)
                    em.on("scalar", s_1)
                    n2 = ph.bump("els")

                    def v_2(v):
                        v.wait_ge(ph.sem("els"), n2)
                        if out_guard is not None and out_guard[1] >= 1:
                            v.wait_ge(out_guard[0], out_guard[1])
                        v.tensor_add(out_ap, elu_e[:, 0:width], elu_r[:, 0:width]
                                     ).then_inc(ph.sem("elf"), 1)
                    em.on("vector", v_2)
                    return ph.bump("elf")

                c_mat_g = ph.val("mat")
                for fc in range(2):
                    for tt in range(NTT):
                        b = tt % 2
                        c_elf = ph.val("elf")

                        def t_qf(t, h=h, fc=fc, tt=tt, b=b, c_elf=c_elf, n_evq_done=n_evq_done):
                            t.wait_ge(ph.sem("evq"), n_evq_done)
                            if c_elf >= 2:
                                t.wait_ge(ph.sem("elf"), c_elf - 1)
                            t.matmul(ps_f[b][:, :], sb_proj[0:64, h, fc * 128:(fc + 1) * 128],
                                     qkT[0:64, tt * TT:(tt + 1) * TT], start=True, stop=True
                                     ).then_inc(ph.sem("mmf"), 1)
                        em.on("tensor", t_qf)
                        n_mmf = ph.bump("mmf")
                        emit_elu(ps_f[b][:, :], n_mmf, qfT[:, fc, tt * TT:(tt + 1) * TT], 512,
                                 out_guard=(ph.sem("mat"), c_mat_g))

                c_mkv_g = ph.val("mkv")
                for nt in range(16):
                    b = nt % 2
                    c_elf = ph.val("elf")

                    def t_kf(t, h=h, nt=nt, b=b, c_elf=c_elf, n_evq_done=n_evq_done):
                        t.wait_ge(ph.sem("evq"), n_evq_done)
                        if c_elf >= 2:
                            t.wait_ge(ph.sem("elf"), c_elf - 1)
                        t.matmul(ps_f[b][:, 0:256], qkT[64:128, nt * 128:(nt + 1) * 128],
                                 sb_proj[64:128, h, :], start=True, stop=True
                                 ).then_inc(ph.sem("mmf"), 1)
                    em.on("tensor", t_kf)
                    n_mmf = ph.bump("mmf")
                    emit_elu(ps_f[b][:, 0:256], n_mmf, kfN[:, nt, :], 256,
                             out_guard=(ph.sem("mkv"), c_mkv_g))
                n_kf_done = ph.val("elf")

                for fc in range(2):
                    c_ekv = ph.val("ekv")

                    def t_kv(t, fc=fc, n_kf=n_kf_done, c_ekv=c_ekv):
                        t.wait_ge(ph.sem("elf"), n_kf)
                        if c_ekv >= 1:
                            t.wait_ge(ph.sem("ekv"), c_ekv)
                        for nt in range(16):
                            i = t.matmul(ps_kv[:, :], kfN[:, nt, fc * 128:(fc + 1) * 128],
                                         v_aug[:, nt, :], start=(nt == 0), stop=(nt == 15))
                        i.then_inc(ph.sem("mkv"), 1)
                    em.on("tensor", t_kv)
                    n_mkv = ph.bump("mkv")

                    def v_ekv(v, fc=fc, n_mkv=n_mkv, c_mat_k=ph.val("mat")):
                        v.wait_ge(ph.sem("mkv"), n_mkv)
                        v.wait_ge(ph.sem("mat"), c_mat_k)
                        v.tensor_copy(kv_sb[:, fc, :], ps_kv[:, :]).then_inc(ph.sem("ekv"), 1)
                    em.on("vector", v_ekv)
                    ph.bump("ekv")
                n_kv_done = ph.val("ekv")

                b0 = (h % 2) * 64
                for tt in range(NTT):
                    c_zf = ph.val("zfin")

                    def t_at(t, tt=tt, n_kv=n_kv_done, c_zf=c_zf, b0=b0):
                        t.wait_ge(ph.sem("ekv"), n_kv)
                        if c_zf >= 1:
                            t.wait_ge(ph.sem("zfin"), c_zf)
                        for fc in range(2):
                            t.matmul(ps_at[b0:b0 + 64, :], kv_sb[:, fc, 0:64],
                                     qfT[:, fc, tt * TT:(tt + 1) * TT],
                                     start=(fc == 0), stop=(fc == 1), tile_position=(0, b0))
                        for fc in range(2):
                            i = t.matmul(ps_z[0:1, :], kv_sb[:, fc, 64:65],
                                         qfT[:, fc, tt * TT:(tt + 1) * TT],
                                         start=(fc == 0), stop=(fc == 1))
                        i.then_inc(ph.sem("mat"), 1)
                    em.on("tensor", t_at)
                    n_mat = ph.bump("mat")

                    def v_z(v, n_mat=n_mat):
                        v.wait_ge(ph.sem("mat"), n_mat)
                        v.tensor_single_scalar(zr_t[0:1, :], ps_z[0:1, :], 1e-8, ALU.add)
                        v.reciprocal(zr[0:1, :], zr_t[0:1, :]).then_inc(ph.sem("vz"), 1)
                    em.on("vector", v_z)
                    n_vz = ph.bump("vz")

                    def t_zb(t, n_vz=n_vz, c_zf=c_zf, b0=b0):
                        t.wait_ge(ph.sem("vz"), n_vz)
                        if c_zf >= 1:
                            t.wait_ge(ph.sem("zfin"), c_zf)
                        t.matmul(ps_zb[b0:b0 + 64, :], ones_row_f[0:1, 0:64], zr[0:1, :],
                                 start=True, stop=True, tile_position=(0, b0)
                                 ).then_inc(ph.sem("mzb"), 1)
                    em.on("tensor", t_zb)
                    n_mzb = ph.bump("mzb")

                    def v_fin(v, h=h, s=s, tt=tt, n_mzb=n_mzb, b0=b0):
                        v.wait_ge(ph.sem("mzb"), n_mzb)
                        v.tensor_copy(zb_sb[b0:b0 + 64, :], ps_zb[b0:b0 + 64, :])
                        c = h // 2
                        dst = XB[1][b0:b0 + 64,
                                   (s * DC + c) * T + tt * TT:(s * DC + c) * T + (tt + 1) * TT]
                        v.tensor_mul(dst, ps_at[b0:b0 + 64, :], zb_sb[b0:b0 + 64, :]
                                     ).then_inc(ph.sem("zfin"), 1)
                    em.on("vector", v_fin)
                    ph.bump("zfin")

        emit_moe_casts(em, range(0, 3))
        em.replay(nc)
        pes.close()

    if not _SKIP_EARLY:
        openB("b1")
        with nc.named_scope("aat"):
            phase_a_attn()
        closeA()

    def phase_a_out():
        pes = ExitStack()
        Sl = lambda n, sh, dt: pes.enter_context(nc.sbuf_tensor(n, sh, dt))
        Pl = lambda n, sh: pes.enter_context(nc.psum_tensor(n, sh, F32))
        ph = Sems(nc, es, "aou")
        em = Emitter()
        lnM = LNHelper("m", pes)
        wout_sb = Sl("aou_wout", [128, DC, D], BF16)
        u1 = [Sl("aou_u1_0", [128, DC, TT], F32)]
        u2 = u1
        ps_u = [Pl(f"aou_ps{j}", [128, 512]) for j in range(2)]

        def wload(e):
            e.dma_start(wout_sb[:, :, :], woutT.rearrange("(c p) o -> p c o", p=128)
                        ).then_inc(ph.sem("wld"), 16)
        em.on("gpsimd", wload)
        n_wld_a = ph.bump("wld", 16)

        tiles = [(s, tt) for s in range(NSEQ) for tt in range(NTT)]
        done_handles = []
        for i, (s, tt) in enumerate(tiles):
            j = 0
            for o in range(DC):
                b = o % 2
                c_ev = ph.val("ev")

                def t_mm(t, s=s, tt=tt, o=o, b=b, c_ev=c_ev, i=i, n_wld_a=n_wld_a):
                    t.wait_ge(ph.sem("wld"), n_wld_a)
                    if c_ev >= 2:
                        t.wait_ge(ph.sem("ev"), c_ev - 1)
                    for c in range(DC):
                        k = t.matmul(ps_u[b][:, :], wout_sb[:, c, o * 128:(o + 1) * 128],
                                     xbfB_ap(s, c, tt * TT, (tt + 1) * TT),
                                     start=(c == 0), stop=(c == DC - 1))
                    k.then_inc(ph.sem("mm"), 1)
                em.on("tensor", t_mm)
                n_mm = ph.bump("mm")

                def s_ev(sc, o=o, b=b, j=j, n_mm=n_mm, i=i, done_handles=done_handles,
                         c_st=ph.val("st")):
                    sc.wait_ge(ph.sem("mm"), n_mm)
                    if o == 0 and i >= 1:
                        h0 = done_handles[i - 1]
                        sc.wait_ge(h0[0], h0[1])
                        sc.wait_ge(ph.sem("st"), c_st)
                    sc.activation(u1[j][:, o, :], ps_u[b][:, :], AF.Identity,
                                  bias=sb_boutA[:, o:o + 1]).then_inc(ph.sem("ev"), 1)
                em.on("scalar", s_ev)
                ph.bump("ev")
            n_ev = ph.val("ev")

            def dma_racc(e, s=s, tt=tt, j=j, n_ev=n_ev):
                e.wait_ge(ph.sem("ev"), n_ev)
                e.dma_start(u1[j][:, :, :], dtile(s_r, s, tt * TT, (tt + 1) * TT),
                            accum_op=ALU.add).then_inc(ph.sem("acc"), 16)
            em.on("gpsimd", dma_racc)
            n_acc = ph.bump("acc", 16)

            def xin_wait(eng, n_acc=n_acc):
                eng.wait_ge(ph.sem("acc"), n_acc)

            def rt_wait(eng, i=i, c_st=ph.val("st")):
                if i >= 1:
                    eng.wait_ge(ph.sem("st"), c_st)

            hdl = lnM.emit(em, u1[j][:, :, :], u2[j][:, :, :], store_bf=None,
                           gain=sb_gA, bias=sb_bA, xin_wait=xin_wait, rt_ready_wait=rt_wait)
            done_handles.append(hdl)

            def dma_out(e, j=j, s=s, tt=tt, hdl=hdl):
                e.wait_ge(hdl[0], hdl[1])
                e.dma_start(dtile(s_u2, s, tt * TT, (tt + 1) * TT), u2[j][:, :, :]
                            ).then_inc(ph.sem("st"), 16)
            em.on("sync", dma_out)
            ph.bump("st", 16)

        def fin(e):
            e.wait_ge(ph.sem("st"), ph.val("st"))
        em.on("sync", fin)
        emit_moe_casts(em, range(3, 6))
        em.replay(nc)
        pes.close()

    if not _SKIP_EARLY:
        with nc.named_scope("aou"):
            phase_a_out()
        closeB()

    def phase_a_ffn():
        pes = ExitStack()
        Sl = lambda n, sh, dt: pes.enter_context(nc.sbuf_tensor(n, sh, dt))
        Pl = lambda n, sh: pes.enter_context(nc.psum_tensor(n, sh, F32))
        ph = Sems(nc, es, "aff")
        em = Emitter()
        midsb = Sl("aff_mid", [128, 32, TT], BF16)
        u2bf = Sl("aff_u2b", [128, DC, TT], BF16)
        u3 = Sl("aff_u3", [128, DC, TT], F32)
        ps_mid = [Pl(f"aff_pm{j}", [128, 512]) for j in range(2)]
        ps_x2 = [Pl(f"aff_px{j}", [128, 512]) for j in range(2)]
        lnF = LNHelper("f", pes,
                       ext_xb=(midsb[:, 0:DC, :].rearrange("p a b -> p (a b)"),
                               lambda c: midsb[:, c, :]),
                       ext_xsq=(midsb[:, DC:2 * DC, :].rearrange("p a b -> p (a b)"),
                                lambda c: midsb[:, DC + c, :]))

        def w1v(c, f):
            return XA[1][:, c * 4096 + f * 128: c * 4096 + (f + 1) * 128]

        def w2v(fr, o):
            return XB[1][:, fr * 1024 + o * 128: fr * 1024 + (o + 1) * 128]

        def wload(e):
            e.dma_start(XA[1][:, :].rearrange("p (c f) -> p c f", f=4096),
                        w1T.rearrange("(c p) f -> p c f", p=128)).then_inc(ph.sem("wld"), 16)
            e.dma_start(XB[1][:, :].rearrange("p (c o) -> p c o", o=1024),
                        w2T.rearrange("(c p) o -> p c o", p=128)).then_inc(ph.sem("wld"), 16)
        em.on("gpsimd", wload)
        n_wld_f = ph.bump("wld", 32)

        tiles = [(s, tt) for s in range(NSEQ) for tt in range(NTT)]
        done_handles = []
        for i, (s, tt) in enumerate(tiles):
            c_mid_prev = ph.val("mmid")

            def dma_u2(e, s=s, tt=tt, c_mid_prev=c_mid_prev):
                if c_mid_prev:
                    e.wait_ge(ph.sem("mmid"), c_mid_prev)
                e.dma_start(u2bf[:, :, :], dtile(s_u2, s, tt * TT, (tt + 1) * TT)
                            ).then_inc(ph.sem("ldu"), 16)
            em.on("gpsimd", dma_u2)
            n_ldu = ph.bump("ldu", 16)

            c_stat_g = ln_sems.val("stat")
            for f in range(32):
                b = f % 2
                c_ev = ph.val("evm")

                def t_mid(t, f=f, b=b, c_ev=c_ev, n_ldu=n_ldu, n_wld_f=n_wld_f):
                    t.wait_ge(ph.sem("wld"), n_wld_f)
                    t.wait_ge(ph.sem("ldu"), n_ldu)
                    if c_ev >= 2:
                        t.wait_ge(ph.sem("evm"), c_ev - 1)
                    for c in range(DC):
                        k = t.matmul(ps_mid[b][:, :], w1v(c, f), u2bf[:, c, :],
                                     start=(c == 0), stop=(c == DC - 1))
                    k.then_inc(ph.sem("mmid"), 1)
                em.on("tensor", t_mid)
                n_mm = ph.bump("mmid")

                def s_gelu(sc, f=f, b=b, n_mm=n_mm, c_stat_g=c_stat_g, i=i):
                    sc.wait_ge(ph.sem("mmid"), n_mm)
                    if f < 2 * DC and i > 0:
                        sc.wait_ge(ln_sems.sem("stat"), c_stat_g)
                    sc.activation(midsb[:, f, :], ps_mid[b][:, :], AF.Gelu,
                                  bias=sb_b1A[:, f:f + 1]).then_inc(ph.sem("evm"), 1)
                em.on("scalar", s_gelu)
                ph.bump("evm")
            n_evm = ph.val("evm")

            for o in range(DC):
                b = o % 2
                c_ev = ph.val("evx")

                def t_x2(t, o=o, b=b, c_ev=c_ev, n_evm=n_evm):
                    t.wait_ge(ph.sem("evm"), n_evm)
                    if c_ev >= 2:
                        t.wait_ge(ph.sem("evx"), c_ev - 1)
                    for fr in range(32):
                        k = t.matmul(ps_x2[b][:, :], w2v(fr, o), midsb[:, fr, :],
                                     start=(fr == 0), stop=(fr == 31))
                    k.then_inc(ph.sem("mx2"), 1)
                em.on("tensor", t_x2)
                n_mx = ph.bump("mx2")

                def s_evx(sc, o=o, b=b, n_mx=n_mx, i=i, done_handles=done_handles,
                          c_st=ph.val("st")):
                    sc.wait_ge(ph.sem("mx2"), n_mx)
                    if o == 0 and i >= 1:
                        h0 = done_handles[i - 1]
                        sc.wait_ge(h0[0], h0[1])
                        sc.wait_ge(ph.sem("st"), c_st)
                    sc.activation(u3[:, o, :], ps_x2[b][:, :], AF.Identity,
                                  bias=sb_b2A[:, o:o + 1]).then_inc(ph.sem("evx"), 1)
                em.on("scalar", s_evx)
                ph.bump("evx")
            n_evx = ph.val("evx")

            def dma_uacc(e, s=s, tt=tt, n_evx=n_evx):
                e.wait_ge(ph.sem("evx"), n_evx)
                e.dma_start(u3[:, :, :], dtile(s_u2, s, tt * TT, (tt + 1) * TT),
                            accum_op=ALU.add).then_inc(ph.sem("acc"), 16)
            em.on("gpsimd", dma_uacc)
            n_acc = ph.bump("acc", 16)

            def xin_wait(eng, n_acc=n_acc):
                eng.wait_ge(ph.sem("acc"), n_acc)

            hdl = lnF.emit(em, u3[:, :, :], u3[:, :, :], store_bf=None,
                           gain=sb_gA, bias=sb_bA, xin_wait=xin_wait)
            done_handles.append(hdl)

            def dma_xacc(e, s=s, tt=tt, hdl=hdl):
                e.wait_ge(hdl[0], hdl[1])
                e.dma_start(u3[:, :, :],
                            x_in[s, :, :, tt * TT:(tt + 1) * TT].rearrange("c p t -> p c t"),
                            accum_op=ALU.add).then_inc(ph.sem("acc2"), 16)
            em.on("gpsimd", dma_xacc)
            n_acc2 = ph.bump("acc2", 16)

            def dma_st(e, s=s, tt=tt, n_acc2=n_acc2):
                e.wait_ge(ph.sem("acc2"), n_acc2)
                e.dma_start(dtile(s_xA, s, tt * TT, (tt + 1) * TT), u3[:, :, :]
                            ).then_inc(ph.sem("st"), 16)
            em.on("sync", dma_st)
            ph.bump("st", 16)

        def fin(e):
            e.wait_ge(ph.sem("st"), ph.val("st"))
        em.on("sync", fin)
        emit_moe_casts(em, range(6, 8))
        em.replay(nc)
        pes.close()

    if not _SKIP_EARLY:
        openA("a2"); openB("b2")
        with nc.named_scope("aff"):
            phase_a_ffn()
        closeA(); closeB()

    def phase_f_ln(tag, src_x):
        pes = ExitStack()
        Sl = lambda n, sh, dt: pes.enter_context(nc.sbuf_tensor(n, sh, dt))
        ph = Sems(nc, es, tag)
        em = Emitter()
        ln = LNHelper(tag, pes)
        xin = Sl(f"{tag}_xin", [128, DC, TT], F32)
        r2b = Sl(f"{tag}_r2", [128, DC, TT], F32)
        tiles = [(s, tt) for s in range(NSEQ) for tt in range(NTT)]
        h_prev = None
        for i, (s, tt) in enumerate(tiles):
            def dma_in(e, s=s, tt=tt, i=i, h_prev=h_prev):
                if h_prev is not None:
                    e.wait_ge(h_prev[0], h_prev[1])
                e.dma_start(xin[:, :, :], dtile(src_x, s, tt * TT, (tt + 1) * TT)
                            ).then_inc(ph.sem("ld"), 16)
            em.on("sync", dma_in)
            n_ld = ph.bump("ld", 16)

            def xin_wait(eng, n_ld=n_ld):
                eng.wait_ge(ph.sem("ld"), n_ld)

            def rt_wait(eng, i=i, h_prev=h_prev, c_st=ph.val("st")):
                if i >= 1:
                    eng.wait_ge(ph.sem("st"), c_st)
                if h_prev is not None:
                    eng.wait_ge(h_prev[0], h_prev[1])

            h1 = ln.emit(em, xin[:, :, :], r2b[:, :, :], store_bf=None,
                         xin_wait=xin_wait, rt_ready_wait=rt_wait)

            def dma_r2(e, s=s, tt=tt, h1=h1):
                e.wait_ge(h1[0], h1[1])
                e.dma_start(dtile(s_ffn_in, s, tt * TT, (tt + 1) * TT), r2b[:, :, :]
                            ).then_inc(ph.sem("st"), 16)
            em.on("sync", dma_r2)
            n_st = ph.bump("st", 16)

            def r2_wait(eng, n_st=n_st):
                eng.wait_ge(ph.sem("st"), n_st)

            def store_bf(c, s=s, tt=tt):
                return xbfA_ap(s, c, tt * TT, (tt + 1) * TT)

            def xin2_wait(eng, h1=h1):
                eng.wait_ge(h1[0], h1[1])

            h2 = ln.emit(em, r2b[:, :, :], r2b[:, :, :], store_bf=store_bf,
                         gain=sb_ln1g, bias=sb_ln1b, xin_wait=xin2_wait,
                         rt_ready_wait=r2_wait)
            h_prev = h2
        def fin(e, h_prev=h_prev):
            e.wait_ge(h_prev[0], h_prev[1])
        em.on("sync", fin)
        em.replay(nc)
        pes.close()

    def phase_f_att(tag):
        pes = ExitStack()
        Sl = lambda n, sh, dt: pes.enter_context(nc.sbuf_tensor(n, sh, dt))
        Pl = lambda n, sh: pes.enter_context(nc.psum_tensor(n, sh, F32))
        ph = Sems(nc, es, tag)
        em = Emitter()
        wq_p = [Sl(f"{tag}_wq{j}", [128, DC, 128], BF16) for j in range(2)]
        wk_p = [Sl(f"{tag}_wk{j}", [128, DC, 128], BF16) for j in range(2)]
        wv_p = [Sl(f"{tag}_wv{j}", [128, DC, 128], BF16) for j in range(2)]
        qTp = Sl(f"{tag}_qT", [128, T], BF16)
        kTp = Sl(f"{tag}_kT", [128, T], BF16)
        vN = Sl(f"{tag}_vN", [128, 16, 128], BF16)
        expT = [Sl(f"{tag}_ex{j}", [128, 512], BF16) for j in range(3)]
        zrec = Sl(f"{tag}_zr", [1, 512], F32)
        zb_sb = Sl(f"{tag}_zb", [128, 512], F32)
        ot_f = Sl(f"{tag}_otf", [128, 512], F32)
        ps_p = [Pl(f"{tag}_pp{j}", [128, 512]) for j in range(3)]
        ps_oT = [Pl(f"{tag}_po{j}", [128, 512]) for j in range(2)]
        ps_z = Pl(f"{tag}_pz", [1, 512])
        ps_zb = Pl(f"{tag}_pzb", [128, 512])

        base_wf = ph.val("wfree")
        for hp in range(_F_ATT_HP):
            j = hp % 2

            def wload(e, hp=hp, j=j, base_wf=base_wf):
                if hp >= 2:
                    e.wait_ge(ph.sem("wfree"), base_wf + hp - 1)
                for W, dst in ((Wq, wq_p[j]), (Wk, wk_p[j]), (Wv, wv_p[j])):
                    e.dma_start(dst[:, :, :],
                                W[:, 128 * hp:128 * hp + 128].rearrange("(c p) o -> p c o", p=128)
                                ).then_inc(ph.sem("wld"), 16)
            em.on("gpsimd", wload)
            n_wld = ph.bump("wld", 48)

            for s in range(NSEQ):
                for which, wbuf, bias_sb, dst in ((
                        (0, wq_p[j], sb_bqF, qTp), (1, wk_p[j], sb_bkF, kTp))
                        if _F_ATT_PART >= 1 else ()):
                    for tt in range(NTT):
                        b = tt % 2
                        c_ev = ph.val("evp")
                        c_cons = ph.val("mst")
                        c_expg = ph.val("exp")

                        def t_p(t, s=s, tt=tt, b=b, wbuf=wbuf, c_ev=c_ev, n_wld=n_wld,
                                c_expg=c_expg):
                            t.wait_ge(ph.sem("wld"), n_wld)
                            t.wait_ge(ph.sem("exp"), c_expg)
                            if c_ev >= 2:
                                t.wait_ge(ph.sem("evp"), c_ev - 1)
                            for c in range(DC):
                                k = t.matmul(ps_p[b][:, :], wbuf[:, c, :],
                                             xbfA_ap(s, c, tt * TT, (tt + 1) * TT),
                                             start=(c == 0), stop=(c == DC - 1))
                            k.then_inc(ph.sem("mp"), 1)
                        em.on("tensor", t_p)
                        n_mp = ph.bump("mp")

                        def s_evp(sc, tt=tt, b=b, hp=hp, bias_sb=bias_sb, dst=dst,
                                  n_mp=n_mp, c_cons=c_cons):
                            sc.wait_ge(ph.sem("mp"), n_mp)
                            sc.wait_ge(ph.sem("mst"), c_cons)
                            sc.activation(dst[:, tt * TT:(tt + 1) * TT], ps_p[b][:, :],
                                          AF.Identity, bias=bias_sb[:, hp:hp + 1]
                                          ).then_inc(ph.sem("evp"), 1)
                        em.on("scalar", s_evp)
                        ph.bump("evp")

                n_evp_done = ph.val("evp")
                for nt in range(16 if _F_ATT_PART >= 2 else 0):
                    b = nt % 2
                    c_ev = ph.val("evv")
                    c_cons = ph.val("mot")

                    def t_v(t, s=s, nt=nt, b=b, j=j, c_ev=c_ev, n_wld=n_wld,
                            n_evp_done=n_evp_done):
                        t.wait_ge(ph.sem("wld"), n_wld)
                        t.wait_ge(ph.sem("evp"), n_evp_done)
                        if c_ev >= 2:
                            t.wait_ge(ph.sem("evv"), c_ev - 1)
                        for c in range(DC):
                            k = t.matmul(ps_p[b][:, 0:128], xbfA_ap(s, c, nt * 128, (nt + 1) * 128),
                                         wv_p[j][:, c, :], start=(c == 0), stop=(c == DC - 1))
                        k.then_inc(ph.sem("mp"), 1)
                    em.on("tensor", t_v)
                    n_mp = ph.bump("mp")

                    def v_evv(v, nt=nt, b=b, n_mp=n_mp, c_cons=c_cons):
                        v.wait_ge(ph.sem("mp"), n_mp)
                        v.wait_ge(ph.sem("mot"), c_cons)
                        v.tensor_copy(vN[:, nt, :], ps_p[b][:, 0:128]).then_inc(ph.sem("evv"), 1)
                    em.on("vector", v_evv)
                    ph.bump("evv")
                n_proj = (ph.val("evp"), ph.val("evv"))

                if s == NSEQ - 1:
                    def t_wfree(t):
                        t.nop().then_inc(ph.sem("wfree"), 1)
                    em.on("tensor", t_wfree)
                    ph.bump("wfree")

                for hh in range(2 if _F_ATT_PART >= 3 else 0):
                    h = 2 * hp + hh
                    b0 = hh * 64
                    for qc in range(NTT):
                        bo = qc % 2
                        c_fin = ph.val("fin")
                        c_inv0 = ph.val("inv")

                        NKT = 4 * qc + 4
                        ot_q = []
                        for kt in range(NKT + 1):
                            if kt < NKT:
                                b = kt % 3
                                c_exp = ph.val("exp")

                                def t_st(t, s=s, hh=hh, qc=qc, kt=kt, b=b, b0=b0,
                                         n_proj=n_proj, c_exp=c_exp):
                                    t.wait_ge(ph.sem("evp"), n_proj[0])
                                    t.wait_ge(ph.sem("evv"), n_proj[1])
                                    if c_exp >= 3:
                                        t.wait_ge(ph.sem("exp"), c_exp - 2)
                                    t.matmul(ps_p[b][:, :],
                                             kTp[b0:b0 + 64, kt * 128:(kt + 1) * 128],
                                             qTp[b0:b0 + 64, qc * TT:(qc + 1) * TT],
                                             start=True, stop=True).then_inc(ph.sem("mst"), 1)
                                em.on("tensor", t_st)
                                n_mst = ph.bump("mst")

                                def s_exp(sc, b=b, kt=kt, n_mst=n_mst,
                                          c_mot=max(0, ph.val("mot") - 1)):
                                    sc.wait_ge(ph.sem("mst"), n_mst)
                                    sc.wait_ge(ph.sem("mot"), c_mot)
                                    sc.activation(expT[b][:, :], ps_p[b][:, :], AF.Exp,
                                                  scale=0.125).then_inc(ph.sem("exp"), 1)
                                em.on("scalar", s_exp)
                                n_exp = ph.bump("exp")

                                need_mask = (kt >= 4 * qc)
                                if need_mask:
                                    def v_mask(v, b=b, m=kt - 4 * qc, n_exp=n_exp):
                                        v.wait_ge(ph.sem("exp"), n_exp)
                                        v.tensor_mul(expT[b][:, :], expT[b][:, :],
                                                     sb_m512[m][:, :]).then_inc(ph.sem("msk"), 1)
                                    em.on("vector", v_mask)
                                    n_gate = ("msk", ph.bump("msk"))
                                else:
                                    n_gate = ("exp", n_exp)

                                def t_ot(t, s=s, kt=kt, qc=qc, b=b, b0=b0, bo=bo, hh=hh,
                                         n_gate=n_gate, c_fin=c_fin, c_inv0=c_inv0,
                                         NKT=NKT):
                                    t.wait_ge(ph.sem(n_gate[0]), n_gate[1])
                                    if kt == 0 and c_fin >= 2:
                                        t.wait_ge(ph.sem("fin"), c_fin - 1)
                                    if kt == 0 and c_inv0 >= 1:
                                        t.wait_ge(ph.sem("inv"), c_inv0)
                                    t.matmul(ps_oT[bo][b0:b0 + 64, :], vN[:, kt, b0:b0 + 64],
                                             expT[b][:, :], start=(kt == 0),
                                             stop=(kt == NKT - 1), tile_position=(0, b0))
                                    t.matmul(ps_z[0:1, :], ones_col_b[:, 0:1], expT[b][:, :],
                                             start=(kt == 0), stop=(kt == NKT - 1)
                                             ).then_inc(ph.sem("mot"), 1)
                                ot_q.append(t_ot)
                                ph.bump("mot")
                            if kt >= 1:
                                em.on("tensor", ot_q[kt - 1])
                        n_mot = ph.val("mot")

                        if _F_ATT_PART < 4:
                            continue

                        def v_inv(v, n_mot=n_mot):
                            v.wait_ge(ph.sem("mot"), n_mot)
                            v.reciprocal(zrec[0:1, :], ps_z[0:1, :]).then_inc(ph.sem("inv"), 1)
                        em.on("vector", v_inv)
                        n_inv = ph.bump("inv")

                        def t_zb(t, b0=b0, n_inv=n_inv, c_fin=c_fin):
                            t.wait_ge(ph.sem("inv"), n_inv)
                            if c_fin >= 1:
                                t.wait_ge(ph.sem("fin"), c_fin)
                            t.matmul(ps_zb[b0:b0 + 64, :], ones_row_f[0:1, 0:64], zrec[0:1, :],
                                     start=True, stop=True, tile_position=(0, b0)
                                     ).then_inc(ph.sem("mzb"), 1)
                        em.on("tensor", t_zb)
                        n_mzb = ph.bump("mzb")

                        def v_fin(v, h=h, s=s, qc=qc, bo=bo, b0=b0, hp=hp, n_mzb=n_mzb):
                            v.wait_ge(ph.sem("mzb"), n_mzb)
                            v.tensor_copy(zb_sb[b0:b0 + 64, :], ps_zb[b0:b0 + 64, :])
                            v.tensor_mul(ot_f[b0:b0 + 64, :], ps_oT[bo][b0:b0 + 64, :],
                                         zb_sb[b0:b0 + 64, :])
                            c = h // 2
                            dst = XB[1][b0:b0 + 64,
                                       (s * DC + c) * T + qc * TT:(s * DC + c) * T + (qc + 1) * TT]
                            v.tensor_scalar_add(dst, ot_f[b0:b0 + 64, :],
                                                sb_bvF[b0:b0 + 64, hp:hp + 1]
                                                ).then_inc(ph.sem("fin"), 1)
                        em.on("vector", v_fin)
                        ph.bump("fin")

        em.replay(nc)
        pes.close()

    def phase_f_out(tag, wnat_sb):
        pes = ExitStack()
        Sl = lambda n, sh, dt: pes.enter_context(nc.sbuf_tensor(n, sh, dt))
        Pl = lambda n, sh: pes.enter_context(nc.psum_tensor(n, sh, F32))
        ph = Sems(nc, es, tag)
        em = Emitter()
        lnO = LNHelper(tag + "l", pes)
        wo_sb = Sl(f"{tag}_wo", [128, DC, D], BF16)
        u1 = Sl(f"{tag}_u1", [128, DC, TT], F32)
        ps_u = [Pl(f"{tag}_ps{jj}", [128, 512]) for jj in range(2)]
        ps_lg = Pl(f"{tag}_plg", [128, 8])
        lg = Sl(f"{tag}_lg", [128, 8], F32)
        lgt = Sl(f"{tag}_lgt", [128, 8], F32)
        lgu = Sl(f"{tag}_lgu", [128, 8], F32)
        lgs = Sl(f"{tag}_lgs", [128, 6], F32)

        def wload(e):
            e.dma_start(wo_sb[:, :, :], Wo.rearrange("(c p) o -> p c o", p=128)
                        ).then_inc(ph.sem("wld"), 16)
        em.on("gpsimd", wload)
        n_wld_o = ph.bump("wld", 16)

        tiles = [(s, tt) for s in range(NSEQ) for tt in range(NTT)]
        done_handles = []
        for i, (s, tt) in enumerate(tiles):
            for o in range(DC):
                b = o % 2
                c_ev = ph.val("ev")

                def t_mm(t, s=s, tt=tt, o=o, b=b, c_ev=c_ev, n_wld_o=n_wld_o):
                    t.wait_ge(ph.sem("wld"), n_wld_o)
                    if c_ev >= 2:
                        t.wait_ge(ph.sem("ev"), c_ev - 1)
                    for c in range(DC):
                        k = t.matmul(ps_u[b][:, :], wo_sb[:, c, o * 128:(o + 1) * 128],
                                     xbfB_ap(s, c, tt * TT, (tt + 1) * TT),
                                     start=(c == 0), stop=(c == DC - 1))
                    k.then_inc(ph.sem("mm"), 1)
                em.on("tensor", t_mm)
                n_mm = ph.bump("mm")

                def s_ev(sc, o=o, b=b, n_mm=n_mm, i=i, done_handles=done_handles,
                         c_sta=ph.val("sta")):
                    sc.wait_ge(ph.sem("mm"), n_mm)
                    if o == 0 and i >= 1:
                        h0 = done_handles[i - 1]
                        sc.wait_ge(h0[0], h0[1])
                        sc.wait_ge(ph.sem("sta"), c_sta)
                    sc.activation(u1[:, o, :], ps_u[b][:, :], AF.Identity,
                                  bias=sb_boF[:, o:o + 1]).then_inc(ph.sem("ev"), 1)
                em.on("scalar", s_ev)
                ph.bump("ev")
            n_ev = ph.val("ev")

            def dma_racc(e, s=s, tt=tt, n_ev=n_ev):
                e.wait_ge(ph.sem("ev"), n_ev)
                e.dma_start(u1[:, :, :], dtile(s_ffn_in, s, tt * TT, (tt + 1) * TT),
                            accum_op=ALU.add).then_inc(ph.sem("acc"), 16)
            em.on("gpsimd", dma_racc)
            n_acc = ph.bump("acc", 16)

            def dma_sta(e, s=s, tt=tt, n_acc=n_acc):
                e.wait_ge(ph.sem("acc"), n_acc)
                e.dma_start(dtile(s_a, s, tt * TT, (tt + 1) * TT), u1[:, :, :]
                            ).then_inc(ph.sem("sta"), 16)
            em.on("sync", dma_sta)
            n_sta = ph.bump("sta", 16)

            def xin_wait(eng, n_acc=n_acc):
                eng.wait_ge(ph.sem("acc"), n_acc)

            def rt_wait(eng, n_sta=n_sta):
                eng.wait_ge(ph.sem("sta"), n_sta)

            def store_bf(c, s=s, tt=tt):
                return xbfA_ap(s, c, tt * TT, (tt + 1) * TT)

            hdl = lnO.emit(em, u1[:, :, :], u1[:, :, :], store_bf=store_bf,
                           gain=sb_ln2g, bias=sb_ln2b, xin_wait=xin_wait,
                           rt_ready_wait=rt_wait)
            done_handles.append(hdl)

            for st4 in range(4):
                gi = (s * NTT + tt) * 4 + st4
                t0 = tt * TT + st4 * 128

                def t_lg(t, s=s, t0=t0, hdl=hdl, c_rtv=ph.val("rtv")):
                    t.wait_ge(hdl[0], hdl[1])
                    if c_rtv >= 1:
                        t.wait_ge(ph.sem("rtv"), c_rtv)
                    for c in range(DC):
                        t.matmul(ps_lg[:, :], xbfA_ap(s, c, t0, t0 + 128),
                                 sb_Wr[:, c, :], start=(c == 0), stop=False)
                    k = t.matmul(ps_lg[:, :], ones_row_b[0:1, :], sb_br_b[0:1, :],
                                 start=False, stop=True)
                    k.then_inc(ph.sem("mlg"), 1)
                em.on("tensor", t_lg)
                n_mlg = ph.bump("mlg")

                def v_r1(v, n_mlg=n_mlg, c_r2=ph.val("r2"), c_sg=ph.val("sg")):
                    v.wait_ge(ph.sem("mlg"), n_mlg)
                    v.wait_ge(ph.sem("sg"), c_sg)
                    v.wait_ge(ph.sem("r2"), c_r2)
                    v.tensor_copy(lg[:, :], ps_lg[:, :])
                    v.reduce_max(lgs[:, 0:1], ps_lg[:, :], axis=AX.X).then_inc(ph.sem("r1"), 1)
                em.on("vector", v_r1)
                n_r1 = ph.bump("r1")

                def g_r2(gp, n_r1=n_r1):
                    gp.wait_ge(ph.sem("r1"), n_r1)
                    gp.tensor_scalar(lgt[:, :], lg[:, :], lgs[:, 0:1], None, ALU.is_ge
                                     ).then_inc(ph.sem("r2"), 1)
                em.on("gpsimd", g_r2)
                n_r2 = ph.bump("r2")

                def v_r2b(v, n_r2=n_r2):
                    v.wait_ge(ph.sem("r2"), n_r2)
                    v.scalar_tensor_tensor(lgu[:, :], lgt[:, :], -1e30, lg[:, :],
                                           ALU.mult, ALU.add).then_inc(ph.sem("r2b"), 1)
                em.on("vector", v_r2b)
                n_r2b = ph.bump("r2b")

                def v_r3(v, n_r2b=n_r2b):
                    v.wait_ge(ph.sem("r2b"), n_r2b)
                    v.reduce_sum(lgs[:, 2:3], lgt[:, :], axis=AX.X)
                    v.reduce_max(lgs[:, 1:2], lgu[:, :], axis=AX.X).then_inc(ph.sem("r3"), 1)
                em.on("vector", v_r3)
                n_r3 = ph.bump("r3")

                def g_r4(gp, n_r3=n_r3):
                    gp.wait_ge(ph.sem("r3"), n_r3)
                    gp.tensor_scalar(lgs[:, 3:4], lgs[:, 2:3], 2.0, None, ALU.is_ge)
                    gp.tensor_sub(lgs[:, 4:5], lgs[:, 0:1], lgs[:, 1:2]
                                  ).then_inc(ph.sem("r4"), 1)
                em.on("gpsimd", g_r4)
                n_r4 = ph.bump("r4")

                def v_r5(v, n_r4=n_r4):
                    v.wait_ge(ph.sem("r4"), n_r4)
                    v.scalar_tensor_tensor(lgs[:, 5:6], lgs[:, 3:4], lgs[:, 4:5],
                                           lgs[:, 1:2], ALU.mult, ALU.add).then_inc(ph.sem("r5"), 1)
                em.on("vector", v_r5)
                n_r5 = ph.bump("r5")

                def g_r6(gp, n_r5=n_r5):
                    gp.wait_ge(ph.sem("r5"), n_r5)
                    gp.tensor_scalar(lgt[:, :], lg[:, :], lgs[:, 5:6], None, ALU.is_ge)
                    gp.tensor_single_scalar(lgs[:, 0:1], lgs[:, 0:1], -1.0, ALU.mult
                                            ).then_inc(ph.sem("r6"), 1)
                em.on("gpsimd", g_r6)
                n_r6 = ph.bump("r6")

                def s_exp(sc, n_r6=n_r6):
                    sc.wait_ge(ph.sem("r6"), n_r6)
                    sc.activation(lg[:, :], lg[:, :], AF.Exp, bias=lgs[:, 0:1]
                                  ).then_inc(ph.sem("sg"), 1)
                em.on("scalar", s_exp)
                n_sg = ph.bump("sg")

                def v_r7(v, n_sg=n_sg):
                    v.wait_ge(ph.sem("sg"), n_sg)
                    v.tensor_mul(lg[:, :], lg[:, :], lgt[:, :])
                    v.tensor_copy(lgu[:, 1:2], lgs[:, 0:1])
                    v.reduce_sum(lgs[:, 4:5], lg[:, :], axis=AX.X)
                    v.tensor_copy(lgu[:, 2:3], lgs[:, 0:1])
                    v.reciprocal(lgu[:, 0:1], lgs[:, 4:5]).then_inc(ph.sem("r8"), 1)
                em.on("vector", v_r7)
                n_r8 = ph.bump("r8")

                def g_r9(gp, gi=gi, n_r8=n_r8):
                    gp.wait_ge(ph.sem("r8"), n_r8)
                    gp.tensor_single_scalar(wnat_sb[:, gi, :], lg[:, :], lgu[:, 0:1],
                                            ALU.mult).then_inc(ph.sem("rtv"), 1)
                em.on("gpsimd", g_r9)
                ph.bump("rtv")

        def fin(e):
            e.wait_ge(ph.sem("sta"), ph.val("sta"))
        em.on("sync", fin)

        def fin_v(v):
            v.wait_ge(ph.sem("rtv"), ph.val("rtv"))
        em.on("vector", fin_v)
        em.replay(nc)
        pes.close()

    def dump_xbfB(dst):
        ph3 = Sems(nc, es, "dmpB")
        em3 = Emitter()
        def cv(v):
            pass
        base_n3 = ph3.val("st")

        def go(e, base_n3=base_n3):
            n = base_n3
            for s in range(NSEQ):
                for c in range(DC):
                    e.dma_start(dst[s, c, :, :], XB[1][:, (s * DC + c) * T:(s * DC + c + 1) * T]
                                ).then_inc(ph3.sem("st"), 16)
                    n += 16
            e.wait_ge(ph3.sem("st"), n)
        em3.on("gpsimd", go)
        for _ in range(NSEQ * DC):
            ph3.bump("st", 16)
        em3.replay(nc)


    def phase_f_moe(tag, wnat_sb, src_xprev, dst_x):
        pes = ExitStack()
        Sl = lambda n, sh, dt: pes.enter_context(nc.sbuf_tensor(n, sh, dt))
        Pl = lambda n, sh: pes.enter_context(nc.psum_tensor(n, sh, F32))
        ph = Sems(nc, es, tag)
        em = Emitter()
        moe = Sl(f"{tag}_moe", [128, DC, 1024], F32)
        w1b = [Sl(f"{tag}_w1{j}", [128, DC, 512], BF16) for j in range(2)]
        w2b = [Sl(f"{tag}_w2{j}", [128, DC, 256], BF16) for j in range(2)]
        w_eb = Sl(f"{tag}_web", [128, 1024], F32)
        w_row = Sl(f"{tag}_wrow", [1, 1024], F32)
        eo_f = Sl(f"{tag}_eof", [128, 1024], F32)
        ps_mid = [Pl(f"{tag}_pm{j}", [128, 1024]) for j in range(2)]
        ps_w = Pl(f"{tag}_pw", [128, 1024])

        def midv(f):
            return XB[1][:, f * 1024:(f + 1) * 1024]

        def gate0(t):
            t.wait_ge(gsem.sem("mcast"), gsem.val("mcast"))
        em.on("tensor", gate0)

        for g in range(4):
            s, half = g // 2, g % 2
            t0 = half * 1024

            for e in range(8):
                for pc in range(8):
                    j = (e * 8 + pc) % 2
                    c_mm = ph.val("mmid")

                    def dma_w1(eng, e=e, pc=pc, j=j, c_mm=c_mm):
                        if c_mm >= 8:
                            eng.wait_ge(ph.sem("mmid"), c_mm - 4)
                        eng.dma_start(w1b[j][:, :, :],
                                      ew1b[e, :, pc * 512:(pc + 1) * 512]
                                      .rearrange("(c p) f -> p c f", p=128)
                                      ).then_inc(ph.sem("lw1"), 16)
                    em.on("sync", dma_w1)
                    n_lw1 = ph.bump("lw1", 16)

                    for fi in range(4):
                        f = pc * 4 + fi
                        b = f % 2
                        c_ev = ph.val("evm")

                        def t_mid(t, s=s, t0=t0, f=f, fi=fi, b=b, j=j, c_ev=c_ev,
                                  n_lw1=n_lw1, c_evo_rel=ph.val("evo")):
                            t.wait_ge(ph.sem("lw1"), n_lw1)
                            t.wait_ge(ph.sem("evo"), c_evo_rel)
                            if c_ev >= 2:
                                t.wait_ge(ph.sem("evm"), c_ev - 1)
                            for half2 in range(2):
                                for c in range(DC):
                                    k = t.matmul(
                                        ps_mid[b][:, half2 * 512:(half2 + 1) * 512],
                                        w1b[j][:, c, fi * 128:(fi + 1) * 128],
                                        xbfA_ap(s, c, t0 + half2 * 512, t0 + (half2 + 1) * 512),
                                        start=(c == 0), stop=(c == DC - 1))
                            k.then_inc(ph.sem("mmid"), 1)
                        em.on("tensor", t_mid)
                        n_mm = ph.bump("mmid")

                        def s_gelu(sc, e=e, f=f, b=b, n_mm=n_mm, c_eo=ph.val("meo")):
                            sc.wait_ge(ph.sem("mmid"), n_mm)
                            sc.wait_ge(ph.sem("meo"), c_eo)
                            sc.activation(midv(f), ps_mid[b][:, :], AF.Gelu,
                                          bias=sb_eb1[:, e * 32 + f:e * 32 + f + 1]
                                          ).then_inc(ph.sem("evm"), 1)
                        em.on("scalar", s_gelu)
                        ph.bump("evm")
                n_evm = ph.val("evm")

                c_gv = ph.val("gv")

                def t_gate(t, s=s, half=half, e=e, c_gv=c_gv):
                    if c_gv >= 1:
                        t.wait_ge(ph.sem("gv"), c_gv)
                    for st4 in range(8):
                        gi = s * 16 + half * 8 + st4
                        k = t.transpose(ps_w[0:1, st4 * 128:st4 * 128 + 128],
                                        wnat_sb[:, gi, e:e + 1], sb_identF[:, :])
                    k.then_inc(ph.sem("gt"), 1)
                em.on("tensor", t_gate)
                n_gt = ph.bump("gt")

                def v_grow(v, n_gt=n_gt):
                    v.wait_ge(ph.sem("gt"), n_gt)
                    v.tensor_copy(w_row[0:1, :], ps_w[0:1, :]).then_inc(ph.sem("gr"), 1)
                em.on("vector", v_grow)
                n_gr = ph.bump("gr")

                def t_gbc(t, n_gr=n_gr):
                    t.wait_ge(ph.sem("gr"), n_gr)
                    t.matmul(ps_w[:, 0:512], ones_row_f[0:1, :], w_row[0:1, 0:512],
                             start=True, stop=True)
                    t.matmul(ps_w[:, 512:1024], ones_row_f[0:1, :], w_row[0:1, 512:1024],
                             start=True, stop=True).then_inc(ph.sem("gb"), 1)
                em.on("tensor", t_gbc)
                n_gb = ph.bump("gb")

                def v_geb(v, n_gb=n_gb, c_mo=ph.val("mo")):
                    v.wait_ge(ph.sem("gb"), n_gb)
                    v.wait_ge(ph.sem("mo"), c_mo)
                    v.tensor_copy(w_eb[:, :], ps_w[:, :]).then_inc(ph.sem("gv"), 1)
                em.on("vector", v_geb)
                n_gv = ph.bump("gv")

                for oq in range(4):
                    for rc in range(4):
                        j2 = (oq * 4 + rc) % 2
                        c_meo = ph.val("meo")

                        def dma_w2(eng, e=e, oq=oq, rc=rc, j2=j2, c_meo=c_meo):
                            if c_meo >= 4:
                                eng.wait_ge(ph.sem("meo"), c_meo - 2)
                            eng.dma_start(w2b[j2][:, :, :],
                                          ew2b[e, rc * 1024:(rc + 1) * 1024,
                                               oq * 256:(oq + 1) * 256]
                                          .rearrange("(c p) o -> p c o", p=128)
                                          ).then_inc(ph.sem("lw2"), 16)
                        em.on("sync", dma_w2)
                        n_lw2 = ph.bump("lw2", 16)

                        for oi in range(2):
                            def t_eo(t, oi=oi, rc=rc, j2=j2, n_lw2=n_lw2, n_evm=n_evm,
                                     c_evo=ph.val("evo"), c_gelu_done=n_evm):
                                t.wait_ge(ph.sem("lw2"), n_lw2)
                                t.wait_ge(ph.sem("evm"), n_evm)
                                if rc == 0 and c_evo >= 2:
                                    t.wait_ge(ph.sem("evo"), c_evo)
                                for fc in range(DC):
                                    f = rc * 8 + fc
                                    for half2 in range(2):
                                        k = t.matmul(
                                            ps_mid[oi][:, half2 * 512:(half2 + 1) * 512],
                                            w2b[j2][:, fc, oi * 128:(oi + 1) * 128],
                                            midv(f)[:, half2 * 512:(half2 + 1) * 512],
                                            start=(f == 0), stop=(f == 31))
                                k.then_inc(ph.sem("meo"), 1)
                            em.on("tensor", t_eo)
                            ph.bump("meo")
                    n_meo = ph.val("meo")

                    for oi in range(2):
                        ot = oq * 2 + oi

                        def s_evo(sc, e=e, ot=ot, oi=oi, n_meo=n_meo,
                                  c_mo=ph.val("mo")):
                            sc.wait_ge(ph.sem("meo"), n_meo)
                            sc.wait_ge(ph.sem("mo"), c_mo)
                            sc.activation(eo_f[:, :], ps_mid[oi][:, :], AF.Identity,
                                          bias=sb_eb2[:, e * 8 + ot:e * 8 + ot + 1]
                                          ).then_inc(ph.sem("evo"), 1)
                        em.on("scalar", s_evo)
                        n_evo = ph.bump("evo")

                        def v_moe(v, e=e, ot=ot, n_evo=n_evo, n_gv=n_gv, g=g,
                                  c_stm=ph.val("stx")):
                            v.wait_ge(ph.sem("evo"), n_evo)
                            v.wait_ge(ph.sem("gv"), n_gv)
                            if e == 0 and c_stm:
                                v.wait_ge(ph.sem("stx"), c_stm)
                            if e == 0:
                                v.tensor_mul(moe[:, ot, :], eo_f[:, :], w_eb[:, :]
                                             ).then_inc(ph.sem("mo"), 1)
                            else:
                                v.tensor_mul(eo_f[:, :], eo_f[:, :], w_eb[:, :])
                                v.tensor_add(moe[:, ot, :], moe[:, ot, :], eo_f[:, :]
                                             ).then_inc(ph.sem("mo"), 1)
                        em.on("vector", v_moe)
                        ph.bump("mo")
            n_mo_g = ph.val("mo")

            def dma_acc_a(eng, s=s, t0=t0, n_mo_g=n_mo_g):
                eng.wait_ge(ph.sem("mo"), n_mo_g)
                eng.dma_start(moe[:, :, :], dtile(s_a, s, t0, t0 + 1024),
                              accum_op=ALU.add).then_inc(ph.sem("ac1"), 16)
            em.on("gpsimd", dma_acc_a)
            n_ac1 = ph.bump("ac1", 16)

            def dma_acc_x(eng, s=s, t0=t0, n_ac1=n_ac1):
                eng.wait_ge(ph.sem("ac1"), n_ac1)
                eng.dma_start(moe[:, :, :], dtile(src_xprev, s, t0, t0 + 1024),
                              accum_op=ALU.add).then_inc(ph.sem("ac2"), 16)
            em.on("gpsimd", dma_acc_x)
            n_ac2 = ph.bump("ac2", 16)

            def dma_stx(eng, s=s, t0=t0, n_ac2=n_ac2):
                eng.wait_ge(ph.sem("ac2"), n_ac2)
                eng.dma_start(dtile(dst_x, s, t0, t0 + 1024), moe[:, :, :]
                              ).then_inc(ph.sem("stx"), 16)
            em.on("sync", dma_stx)
            ph.bump("stx", 16)

        def fin(e):
            e.wait_ge(ph.sem("stx"), ph.val("stx"))
        em.on("sync", fin)
        em.replay(nc)
        pes.close()
    wnatF = S("wnatF", [128, NSEQ * 16, 8], F32)
    if not _SKIP_EARLY:
        openA("a3")
        with nc.named_scope("f1l"):
            phase_f_ln("f1l", s_xA)
    if not _SKIP_EARLY and _PHASES >= 2:
        openB("b3")
        with nc.named_scope("f1a"):
            phase_f_att("f1a")
        if _DEBUG_TAP == "xbfB":
            dump_xbfB(s_xC)
    if not _SKIP_EARLY and _PHASES >= 3:
        with nc.named_scope("f1o"):
            phase_f_out("f1o", wnatF)
        if _DEBUG_TAP == "wnat":
            ph4 = Sems(nc, es, "dmpw")
            em4 = Emitter()
            base_n4 = ph4.val("st")

            def gow(e, base_n4=base_n4):
                e.dma_start(s_xC[0, 0, :, 0:256], wnatF[:, :, :].rearrange("p g e -> p (g e)")
                            ).then_inc(ph4.sem("st"), 16)
                e.wait_ge(ph4.sem("st"), base_n4 + 16)
            em4.on("sync", gow)
            ph4.bump("st", 16)
            em4.replay(nc)
    if not _SKIP_EARLY and _PHASES >= 4:
        with nc.named_scope("f1m"):
            phase_f_moe("f1m", wnatF, s_xA, s_xB)
        closeA(); closeB()
    if _PHASES >= 5:
        openA("a4")
        with nc.named_scope("bln"):
            ln_phase("bln", s_xB, None, "XA")
        _build_b = True
    else:
        _build_b = False

    def phase_b_main():
        pes = ExitStack()
        Sl = lambda n, sh, dt: pes.enter_context(nc.sbuf_tensor(n, sh, dt))
        Pl = lambda n, sh, dt=F32: pes.enter_context(nc.psum_tensor(n, sh, dt))
        ph = Sems(nc, es, "bm")
        em = Emitter()
        NH = 8
        BL = 16
        wqkv = [Sl(f"bm_w{j}", [128, DC, 128], BF16) for j in range(6)]
        bwm_sb = Sl("bm_bwm", [128, DC, 16], BF16)
        f32w = Sl("bm_f32w", [128, T], F32)
        vf32 = Sl("bm_vf32", [128, T], F32)
        xp = Sl("bm_xp", [128, T + 3], F32)
        b16s = Sl("bm_b16s", [128, T], BF16)
        vb = Sl("bm_vb", [128, T], BF16)
        qn = Sl("bm_qn", [128, T], BF16)
        kn = Sl("bm_kn", [128, T], BF16)
        kbfm = Sl("bm_kbfm", [128, T], BF16)
        vkb = Sl("bm_vkb", [128, BL, 256], BF16)
        ktm = Sl("bm_ktm", [128, BL, 128], BF16)
        utm = Sl("bm_utm", [128, BL, 128], BF16)
        wTf = Sl("bm_wT", [128, BL, 128], BF16)
        aiT = Sl("bm_aiT", [128, BL, 128], BF16)
        odel = Sl("bm_odel", [128, T], F32)
        cvw = odel
        gbb = Sl("bm_gb", [128, T], F32)
        bcb = Sl("bm_bcb", [128, T], F32)
        bmn = Sl("bm_bmn", [128, BL, 16], F32)
        bsg = Sl("bm_bsg", [128, BL, 8], F32)
        rowA = Sl("bm_rowA", [1, T], F32)
        rowB = Sl("bm_rowB", [1, T], F32)
        Ssb = Sl("bm_S", [128, 128], BF16)
        Sf = Sl("bm_Sf", [128, 128], F32)
        uis = Sl("bm_uis", [128, 128], BF16)
        Xp2 = [[Sl(f"bm_X{B}_{j}", [128, 128], BF16) for j in range(2)] for B in range(2)]
        Yp2 = [[Sl(f"bm_Y{B}_{j}", [128, 128], BF16) for j in range(6)] for B in range(2)]
        tsb2 = [Sl(f"bm_ts{B}", [128, 256], BF16) for B in range(2)]
        tsc2 = [Sl(f"bm_tc{B}", [128, 256], BF16) for B in range(2)]
        btx2 = [Sl(f"bm_bt{B}", [128, 128], BF16) for B in range(2)]
        wtmp2 = [Sl(f"bm_wt{B}", [128, 128], BF16) for B in range(2)]
        wtmp = wtmp2[0]
        psA = Pl("bm_psA", [128, 512])
        psB = Pl("bm_psB", [128, 512])
        psC2 = [Pl(f"bm_psC{B}", [128, 256]) for B in range(2)]
        psD = Pl("bm_psD", [128, 128])
        psE = Pl("bm_psE", [128, 128], BF16)
        psF = Pl("bm_psF", [128, 128])
        psG = Pl("bm_psG", [1, 512])

        def pg(name, allow=0):
            return (name, ph.val(name) - allow)

        def pg_wait(t, w):
            if w[1] > 0:
                t.wait_ge(ph.sem(w[0]), w[1])

        def init(v):
            v.memset(xp[:, 0:3], 0.0)
            for B in range(2):
                v.memset(tsc2[B][:, :], 0.0)
                v.memset(btx2[B][:, :], 0.0)
        em.on("vector", init)

        def bwm_load(e):
            e.dma_start(bwm_sb[:, :, :], bwmix.rearrange("(c p) o -> p c o", p=128)
                        ).then_inc(ph.sem("bl"), 16)
        em.on("gpsimd", bwm_load)
        n_bl = ph.bump("bl", 16)

        units = [(s, h) for s in range(NSEQ) for h in range(NH)]
        for ui_, (s, h) in enumerate(units):
            if _B_PART < 1 or ui_ >= _B_UNITS:
                continue
            j3 = ui_ % 2

            def wload(e, s=s, h=h, j3=j3, c_pr=ph.val("prj")):
                if c_pr >= 24:
                    e.wait_ge(ph.sem("prj"), c_pr - 12)
                for wi, W in enumerate((qwB, kwB, vwB)):
                    e.dma_start(wqkv[3 * j3 + wi][:, :, :],
                                W[:, 128 * h:128 * h + 128].rearrange("(c p) o -> p c o", p=128)
                                ).then_inc(ph.sem("wl"), 16)
            em.on("gpsimd", wload)
            n_wl = ph.bump("wl", 48)

            if h == 0:
                for bt in range(BL):
                    def t_bm(t, s=s, bt=bt, c_uie=ph.val("uie"), n_bl=n_bl,
                             w_dd=pg("dd")):
                        t.wait_ge(ph.sem("bl"), n_bl)
                        t.wait_ge(ph.sem("uie"), c_uie)
                        pg_wait(t, w_dd)
                        for c in range(DC):
                            k = t.matmul(psD[:, 0:16],
                                         xbfA_ap(s, c, bt * 128, (bt + 1) * 128),
                                         bwm_sb[:, c, :], start=(c == 0), stop=(c == DC - 1))
                        k.then_inc(ph.sem("bmm"), 1)
                    em.on("tensor", t_bm)
                    n_bmm = ph.bump("bmm")

                    def v_bme(v, bt=bt, n_bmm=n_bmm):
                        v.wait_ge(ph.sem("bmm"), n_bmm)
                        v.tensor_copy(bmn[:, bt, :], psD[:, 0:16]).then_inc(ph.sem("dd"), 1)
                    em.on("vector", v_bme)
                    ph.bump("dd")
                n_dd = ph.val("dd")

                def s_bsg(sc, n_dd=n_dd):
                    sc.wait_ge(ph.sem("dd"), n_dd)
                    sc.activation(bsg[:, :, :], bmn[:, :, 0:8], AF.Sigmoid
                                  ).then_inc(ph.sem("bsg"), 1)
                em.on("scalar", s_bsg)
                ph.bump("bsg")
            n_bsg_u = ph.val("bsg")

            def proj_conv(widx, outf32, outbf):
                for tt in range(NTT):
                    ps = psA if tt % 2 == 0 else psB

                    def t_pj(t, s=s, tt=tt, ps=ps, widx=widx, n_wl=n_wl,
                             w_ab=pg("ab", 1)):
                        t.wait_ge(ph.sem("wl"), n_wl)
                        pg_wait(t, w_ab)
                        for c in range(DC):
                            k = t.matmul(ps[:, :], wqkv[widx][:, c, :],
                                         xbfA_ap(s, c, tt * TT, (tt + 1) * TT),
                                         start=(c == 0), stop=(c == DC - 1))
                        k.then_inc(ph.sem("prj"), 1)
                    em.on("tensor", t_pj)
                    n_pj = ph.bump("prj")

                    def v_pje(v, tt=tt, ps=ps, n_pj=n_pj, c_cv=ph.val("cv4")):
                        v.wait_ge(ph.sem("prj"), n_pj)
                        v.wait_ge(ph.sem("cv4"), c_cv)
                        v.tensor_copy(xp[:, 3 + tt * TT:3 + (tt + 1) * TT], ps[:, :]
                                      ).then_inc(ph.sem("ab"), 1)
                    em.on("vector", v_pje)
                    ph.bump("ab")
                n_ab = ph.val("ab")

                cw = {0: sb_convq, 1: sb_convk, 2: sb_convv}[widx % 3]

                def v_c0(v, n_ab=n_ab, h=h, cw=cw, c_sl=ph.val("sl"),
                         c_ost=ph.val("ost")):
                    v.wait_ge(ph.sem("ab"), n_ab)
                    v.wait_ge(ph.sem("sl"), c_sl)
                    v.wait_ge(ph.sem("ost"), c_ost)
                    v.tensor_single_scalar(cvw[:, :], xp[:, 0:T], cw[:, h, 0:1], ALU.mult
                                           ).then_inc(ph.sem("cv1"), 1)
                em.on("vector", v_c0)
                n_c1 = ph.bump("cv1")

                def v_c123(v, h=h, cw=cw, n_c1=n_c1):
                    v.wait_ge(ph.sem("cv1"), n_c1)
                    v.scalar_tensor_tensor(cvw[:, :], xp[:, 1:T + 1], cw[:, h, 1:2],
                                           cvw[:, :], ALU.mult, ALU.add)
                    v.scalar_tensor_tensor(cvw[:, :], xp[:, 2:T + 2], cw[:, h, 2:3],
                                           cvw[:, :], ALU.mult, ALU.add)
                    v.scalar_tensor_tensor(cvw[:, :], xp[:, 3:T + 3], cw[:, h, 3:4],
                                           cvw[:, :], ALU.mult, ALU.add
                                           ).then_inc(ph.sem("cv4"), 1)
                em.on("vector", v_c123)
                n_c4 = ph.bump("cv4")

                def s_silu(sc, outf32=outf32, outbf=outbf, n_c4=n_c4):
                    sc.wait_ge(ph.sem("cv4"), n_c4)
                    last = sc.activation(outf32, cvw[:, :], AF.Silu)
                    if outbf is not None:
                        last = sc.activation(outbf, cvw[:, :], AF.Silu)
                    last.then_inc(ph.sem("sl"), 1)
                em.on("scalar", s_silu)
                return ph.bump("sl")

            def emit_l2(srcf, dstb, n_sl):
                def s_sq(sc, n_sl=n_sl, srcf=srcf, c_gm=ph.val("l2m")):
                    sc.wait_ge(ph.sem("sl"), n_sl)
                    sc.wait_ge(ph.sem("l2m"), c_gm)
                    sc.activation(b16s[:, :], srcf, AF.Square).then_inc(ph.sem("l2s"), 1)
                em.on("scalar", s_sq)
                n_sq = ph.bump("l2s")

                for ct in range(NTT):
                    def t_sum(t, ct=ct, n_sq=n_sq, w_gg=pg("gg")):
                        t.wait_ge(ph.sem("l2s"), n_sq)
                        pg_wait(t, w_gg)
                        t.matmul(psG[0:1, :], ones_col_b[:, 0:1],
                                 b16s[:, ct * TT:(ct + 1) * TT], start=True, stop=True
                                 ).then_inc(ph.sem("l2m"), 1)
                    em.on("tensor", t_sum)
                    n_sm = ph.bump("l2m")

                    def v_row(v, ct=ct, n_sm=n_sm, c_rs=ph.val("l2r")):
                        v.wait_ge(ph.sem("l2m"), n_sm)
                        if ct == 0:
                            v.wait_ge(ph.sem("l2r"), c_rs)
                        v.tensor_copy(rowA[0:1, ct * TT:(ct + 1) * TT], psG[0:1, :]
                                      ).then_inc(ph.sem("gg"), 1)
                    em.on("vector", v_row)
                    ph.bump("gg")
                n_gg = ph.val("gg")

                def s_rstd(sc, n_gg=n_gg, c_bc=ph.val("l2c")):
                    sc.wait_ge(ph.sem("gg"), n_gg)
                    sc.wait_ge(ph.sem("l2c"), c_bc)
                    sc.activation(rowB[0:1, :], rowA[0:1, :], AF.Ln,
                                  bias=constf(1e-12)[0:1, 0:1])
                    sc.activation(rowB[0:1, :], rowB[0:1, :], AF.Exp, scale=-0.5
                                  ).then_inc(ph.sem("l2r"), 1)
                em.on("scalar", s_rstd)
                n_rs = ph.bump("l2r")

                for ct in range(NTT):
                    ps = psA if ct % 2 == 0 else psB

                    def t_bc(t, ct=ct, ps=ps, n_rs=n_rs, w_ab=pg("ab", 1)):
                        t.wait_ge(ph.sem("l2r"), n_rs)
                        pg_wait(t, w_ab)
                        t.matmul(ps[:, :], ones_row_f[0:1, :], rowB[0:1, ct * TT:(ct + 1) * TT],
                                 start=True, stop=True).then_inc(ph.sem("l2c"), 1)
                    em.on("tensor", t_bc)
                    n_bc = ph.bump("l2c")

                    def v_mul(v, ct=ct, ps=ps, n_bc=n_bc, srcf=srcf, dstb=dstb):
                        v.wait_ge(ph.sem("l2c"), n_bc)
                        v.tensor_mul(dstb[:, ct * TT:(ct + 1) * TT],
                                     srcf[:, ct * TT:(ct + 1) * TT], ps[:, :]
                                     ).then_inc(ph.sem("ab"), 1)
                    em.on("vector", v_mul)
                    ph.bump("ab")
                return ph.val("ab")

            n_v = proj_conv(3 * j3 + 2, vf32[:, :], vb[:, :])
            proj_conv(3 * j3 + 0, f32w[:, :], None)
            n_qn = emit_l2(f32w[:, :], qn[:, :], ph.val("sl"))
            proj_conv(3 * j3 + 1, f32w[:, :], None)
            n_kn = emit_l2(f32w[:, :], kn[:, :], ph.val("sl"))
            if _B_PART < 2:
                continue

            c_l2r_b = ph.val("l2r")
            for grp in range(4):
                def t_brow(t, s=s, h=h, grp=grp, n_bsg_u=n_bsg_u, c_l2r_b=c_l2r_b,
                           w_gg=pg("gg")):
                    t.wait_ge(ph.sem("bsg"), n_bsg_u)
                    t.wait_ge(ph.sem("l2r"), c_l2r_b)
                    pg_wait(t, w_gg)
                    for q4 in range(4):
                        bt = grp * 4 + q4
                        k = t.transpose(psG[0:1, q4 * 128:q4 * 128 + 128],
                                        bsg[:, bt, h:h + 1], sb_identF[:, :])
                    k.then_inc(ph.sem("btr"), 1)
                em.on("tensor", t_brow)
                n_btr = ph.bump("btr")

                def v_brow(v, grp=grp, n_btr=n_btr):
                    v.wait_ge(ph.sem("btr"), n_btr)
                    v.tensor_copy(rowA[0:1, grp * 512:(grp + 1) * 512], psG[0:1, :]
                                  ).then_inc(ph.sem("gg"), 1)
                em.on("vector", v_brow)
                ph.bump("gg")
            n_brow = ph.val("gg")

            for ct in range(NTT):
                ps = psA if ct % 2 == 0 else psB

                def t_bbc(t, ct=ct, ps=ps, n_brow=n_brow, w_ab=pg("ab", 1)):
                    t.wait_ge(ph.sem("gg"), n_brow)
                    pg_wait(t, w_ab)
                    t.matmul(ps[:, :], ones_row_f[0:1, :], rowA[0:1, ct * TT:(ct + 1) * TT],
                             start=True, stop=True).then_inc(ph.sem("bbc"), 1)
                em.on("tensor", t_bbc)
                n_bbc = ph.bump("bbc")

                def v_kbf(v, ct=ct, ps=ps, n_bbc=n_bbc, n_kn=n_kn):
                    v.wait_ge(ph.sem("bbc"), n_bbc)
                    v.tensor_mul(kbfm[:, ct * TT:(ct + 1) * TT],
                                 kn[:, ct * TT:(ct + 1) * TT], ps[:, :]
                                 ).then_inc(ph.sem("ab"), 1)
                em.on("vector", v_kbf)
                ph.bump("ab")
            n_kbf = ph.val("ab")

            for bl in range(BL):
                def t_tr(t, bl=bl, n_kn=n_kn, n_v=n_v, w_ee=pg("ee")):
                    t.wait_ge(ph.sem("ab"), n_kn)
                    t.wait_ge(ph.sem("sl"), n_v)
                    pg_wait(t, w_ee)
                    t.transpose(psE[:, :], kn[:, bl * 128:(bl + 1) * 128], sb_identB[:, :]
                                ).then_inc(ph.sem("ktr"), 1)
                em.on("tensor", t_tr)
                n_ktr = ph.bump("ktr")

                def v_ktr(v, bl=bl, n_ktr=n_ktr, c_kb=ph.val("kbm")):
                    v.wait_ge(ph.sem("ktr"), n_ktr)
                    v.wait_ge(ph.sem("kbm"), c_kb)
                    v.tensor_copy(ktm[:, bl, :], psE[:, :]).then_inc(ph.sem("ee"), 1)
                em.on("vector", v_ktr)
                n_ee = ph.bump("ee")

                def g_kbm(gp, s=s, h=h, bl=bl, n_ee=n_ee):
                    gp.wait_ge(ph.sem("ee"), n_ee)
                    gp.tensor_single_scalar(vkb[:, bl, 128:256], ktm[:, bl, :],
                                            bsg[:, bl, h:h + 1], ALU.mult
                                            ).then_inc(ph.sem("kbm"), 1)
                em.on("gpsimd", g_kbm)
                ph.bump("kbm")

                def t_tr2(t, bl=bl, w_ee=pg("ee"), n_v=n_v):
                    t.wait_ge(ph.sem("sl"), n_v)
                    pg_wait(t, w_ee)
                    t.transpose(psE[:, :], vb[:, bl * 128:(bl + 1) * 128], sb_identB[:, :]
                                ).then_inc(ph.sem("vtr"), 1)
                em.on("tensor", t_tr2)
                n_vtr = ph.bump("vtr")

                def v_vtr(v, bl=bl, n_vtr=n_vtr, c_kb=ph.val("kbm")):
                    v.wait_ge(ph.sem("vtr"), n_vtr)
                    v.wait_ge(ph.sem("kbm"), c_kb)
                    v.tensor_copy(wtmp[:, :], psE[:, :]).then_inc(ph.sem("ee"), 1)
                em.on("vector", v_vtr)
                n_ee2 = ph.bump("ee")

                def g_vbm(gp, s=s, h=h, bl=bl, n_ee2=n_ee2):
                    gp.wait_ge(ph.sem("ee"), n_ee2)
                    gp.tensor_single_scalar(vkb[:, bl, 0:128], wtmp[:, :],
                                            bsg[:, bl, h:h + 1], ALU.mult
                                            ).then_inc(ph.sem("kbm"), 1)
                em.on("gpsimd", g_vbm)
                ph.bump("kbm")
            n_tm_done = ph.val("kbm")

            if _B_PART < 3:
                continue

            def gen_block(bl, B):
                psCx = psC2[B]
                Xx = Xp2[B]
                Yx = Yp2[B]
                tsx = tsb2[B]
                tsc = tsc2[B]
                btx = btx2[B]
                wtx = wtmp2[B]

                def t_A(t, bl=bl, n_kbf=n_kbf, c_eapA=ph.val("eap"), psCx=psCx):
                    t.wait_ge(ph.sem("ab"), n_kbf)
                    t.wait_ge(ph.sem("eap"), c_eapA)
                    t.matmul(psCx[:, 0:128], kbfm[:, bl * 128:(bl + 1) * 128],
                             kn[:, bl * 128:(bl + 1) * 128], start=True, stop=True)
                    t.matmul(psCx[:, 128:256], kn[:, bl * 128:(bl + 1) * 128],
                             kbfm[:, bl * 128:(bl + 1) * 128], start=True, stop=True
                             ).then_inc(ph.sem("mA"), 1)
                em.on("tensor", t_A)
                n_mA = ph.bump("mA")
                yield

                def v_A(v, n_mA=n_mA, c_sq=ph.val("msq"), c_ap2=ph.val("map2"),
                        psCx=psCx, Xx=Xx, Yx=Yx, btx=btx):
                    v.wait_ge(ph.sem("mA"), n_mA)
                    v.wait_ge(ph.sem("msq"), c_sq)
                    v.wait_ge(ph.sem("map2"), c_ap2)
                    v.tensor_mul(Xx[0][:, :], psCx[:, 0:128], sb_maskA[:, :])
                    v.tensor_single_scalar(btx[0:64, 64:128], psCx[0:64, 192:256],
                                           -1.0, ALU.mult)
                    v.tensor_mul(Yx[0][:, :], psCx[:, 128:256], sb_maskAT[:, :]
                                 ).then_inc(ph.sem("cc"), 1)
                em.on("vector", v_A)
                n_cc = ph.bump("cc")
                yield

                def t_ai(t, bl=bl, n_kn=n_kn, n_cc=n_cc, psCx=psCx):
                    t.wait_ge(ph.sem("ab"), n_kn)
                    t.wait_ge(ph.sem("cc"), n_cc)
                    t.matmul(psCx[:, 0:128], kn[:, bl * 128:(bl + 1) * 128],
                             qn[:, bl * 128:(bl + 1) * 128], start=True, stop=True
                             ).then_inc(ph.sem("mai"), 1)
                em.on("tensor", t_ai)
                n_mai = ph.bump("mai")
                yield

                def v_ai(v, bl=bl, n_mai=n_mai, psCx=psCx):
                    v.wait_ge(ph.sem("mai"), n_mai)
                    v.tensor_mul(aiT[:, bl, :], psCx[:, 0:128], sb_maskTIT[:, :]
                                 ).then_inc(ph.sem("dd"), 1)
                em.on("vector", v_ai)
                n_dd_ai = ph.bump("dd")
                yield

                for lev in range(5):
                    def t_sq(t, lev=lev, n_cc=n_cc, c_esq=ph.val("esq"),
                             psCx=psCx, Xx=Xx, Yx=Yx, n_dd_ai=n_dd_ai):
                        if lev == 0:
                            t.wait_ge(ph.sem("dd"), n_dd_ai)
                        else:
                            t.wait_ge(ph.sem("esq"), c_esq)
                        xs = Xx[lev % 2]
                        ys = Yx[lev]
                        t.matmul(psCx[:, 0:128], ys[:, :], xs[:, :], start=True, stop=True)
                        t.matmul(psCx[:, 128:256], xs[:, :], ys[:, :], start=True, stop=True
                                 ).then_inc(ph.sem("msq"), 1)
                    em.on("tensor", t_sq)
                    n_msq = ph.bump("msq")
                    yield

                    def v_esq(v, lev=lev, n_msq=n_msq, psCx=psCx, Xx=Xx, Yx=Yx):
                        v.wait_ge(ph.sem("msq"), n_msq)
                        v.tensor_copy(Xx[(lev + 1) % 2][:, :], psCx[:, 0:128])
                        v.tensor_copy(Yx[lev + 1][:, :], psCx[:, 128:256]
                                      ).then_inc(ph.sem("esq"), 1)
                    em.on("vector", v_esq)
                    ph.bump("esq")
                    yield
                n_esq = ph.val("esq")

                def v_eap0(v, bl=bl, n_tm=n_tm_done, c_ap=ph.val("map2"), tsx=tsx):
                    v.wait_ge(ph.sem("kbm"), n_tm)
                    v.wait_ge(ph.sem("map2"), c_ap)
                    v.tensor_copy(tsx[:, :], vkb[:, bl, :]).then_inc(ph.sem("eap"), 1)
                em.on("vector", v_eap0)
                n_eap = ph.bump("eap")
                yield

                for lev in range(6):
                    def t_ap(t, lev=lev, n_eap=n_eap, n_esq=n_esq, psCx=psCx,
                             Yx=Yx, tsx=tsx):
                        t.wait_ge(ph.sem("eap"), n_eap)
                        t.wait_ge(ph.sem("esq"), n_esq)
                        t.matmul(psCx[:, :], Yx[lev][:, :], tsx[:, :], start=True, stop=True
                                 ).then_inc(ph.sem("map2"), 1)
                    em.on("tensor", t_ap)
                    n_map2 = ph.bump("map2")
                    yield

                    def v_eap(v, lev=lev, n_map2=n_map2, psCx=psCx, tsx=tsx):
                        v.wait_ge(ph.sem("map2"), n_map2)
                        v.tensor_add(tsx[:, :], tsx[:, :], psCx[:, :]
                                     ).then_inc(ph.sem("eap"), 1)
                    em.on("vector", v_eap)
                    n_eap = ph.bump("eap")
                    yield

                def t_c(t, n_eap=n_eap, psCx=psCx, btx=btx, tsx=tsx):
                    t.wait_ge(ph.sem("eap"), n_eap)
                    t.matmul(psCx[:, :], btx[:, :], tsx[:, :], start=True, stop=True
                             ).then_inc(ph.sem("map2"), 1)
                em.on("tensor", t_c)
                n_map2 = ph.bump("map2")
                yield

                def v_ce(v, n_map2=n_map2, psCx=psCx, tsc=tsc):
                    v.wait_ge(ph.sem("map2"), n_map2)
                    v.tensor_copy(tsc[64:128, :], psCx[64:128, :]).then_inc(ph.sem("eap"), 1)
                em.on("vector", v_ce)
                n_eap = ph.bump("eap")
                yield

                for lev in range(6):
                    def t_d(t, lev=lev, n_eap=n_eap, psCx=psCx, Yx=Yx, tsc=tsc):
                        t.wait_ge(ph.sem("eap"), n_eap)
                        t.matmul(psCx[:, :], Yx[lev][:, :], tsc[:, :], start=True, stop=True
                                 ).then_inc(ph.sem("map2"), 1)
                    em.on("tensor", t_d)
                    n_map2 = ph.bump("map2")
                    yield

                    def v_de(v, lev=lev, bl=bl, n_map2=n_map2, psCx=psCx, tsc=tsc,
                             tsx=tsx, wtx=wtx):
                        v.wait_ge(ph.sem("map2"), n_map2)
                        v.tensor_add(tsc[64:128, :], tsc[64:128, :], psCx[64:128, :])
                        if lev == 5:
                            v.tensor_add(tsx[64:128, :], tsx[64:128, :], tsc[64:128, :])
                            v.tensor_copy(utm[:, bl, :], tsx[:, 0:128])
                            v.tensor_copy(wtx[:, :], tsx[:, 128:256]).then_inc(ph.sem("eap"), 1)
                        else:
                            v.engine_nop().then_inc(ph.sem("eap"), 1)
                    em.on("vector", v_de)
                    n_eap = ph.bump("eap")
                    yield

                gen_out[B] = n_eap

            gen_out = [0, 0]
            for blp in range(0, BL, 2):
                g0 = gen_block(blp, 0)
                g1 = gen_block(blp + 1, 1)
                if _B_ILV:
                    alive = [g0, g1]
                    while alive:
                        for g in list(alive):
                            try:
                                next(g)
                            except StopIteration:
                                alive.remove(g)
                else:
                    for g in (g0, g1):
                        for _ in g:
                            pass
                for B2 in range(2):
                    def t_wT(t, bl=blp + B2, n_eap=gen_out[B2], w_ee=pg("ee"),
                             wtx=wtmp2[B2]):
                        t.wait_ge(ph.sem("eap"), n_eap)
                        pg_wait(t, w_ee)
                        t.transpose(psE[:, :], wtx[:, :], sb_identB[:, :]
                                    ).then_inc(ph.sem("mwt"), 1)
                    em.on("tensor", t_wT)
                    n_mwt = ph.bump("mwt")

                    def v_wT(v, bl=blp + B2, n_mwt=n_mwt):
                        v.wait_ge(ph.sem("mwt"), n_mwt)
                        v.tensor_copy(wTf[:, bl, :], psE[:, :]).then_inc(ph.sem("ee"), 1)
                    em.on("vector", v_wT)
                    ph.bump("ee")
            n_blocks = (ph.val("ee"), ph.val("dd"), ph.val("eap"))

            if _B_PART < 4:
                continue
            c_ab_scan = ph.val("ab")
            for cc2 in range(BL):
                bl = cc2
                po = psA if (cc2 // 4) % 2 == 0 else psB
                col = (cc2 % 4) * 128

                def t_ui(t, bl=bl, cc2=cc2, n_blocks=n_blocks,
                         c_sev=ph.val("scp"), c_uie=ph.val("uie")):
                    if cc2 == 0:
                        t.wait_ge(ph.sem("ee"), n_blocks[0])
                        t.wait_ge(ph.sem("dd"), n_blocks[1])
                        t.wait_ge(ph.sem("eap"), n_blocks[2])
                        return
                    t.wait_ge(ph.sem("scp"), c_sev)
                    t.wait_ge(ph.sem("uie"), c_uie)
                    t.matmul(psD[:, :], wTf[:, bl, :], Ssb[:, :],
                             start=True, stop=True).then_inc(ph.sem("mui"), 1)
                em.on("tensor", t_ui)
                n_mui = ph.bump("mui") if cc2 > 0 else ph.val("mui")

                def v_ui(v, bl=bl, cc2=cc2, n_mui=n_mui):
                    if cc2 > 0:
                        v.wait_ge(ph.sem("mui"), n_mui)
                        v.tensor_sub(uis[:, :], utm[:, bl, :], psD[:, :]
                                     ).then_inc(ph.sem("uie"), 1)
                    else:
                        v.tensor_copy(uis[:, :], utm[:, bl, :]).then_inc(ph.sem("uie"), 1)
                em.on("vector", v_ui)
                n_uie = ph.bump("uie")

                def t_sc(t, bl=bl, cc2=cc2, po=po, col=col, n_uie=n_uie,
                         w_oev=pg("oev", 1), c_ab_scan=c_ab_scan):
                    t.wait_ge(ph.sem("uie"), n_uie)
                    if cc2 == 0:
                        t.wait_ge(ph.sem("ab"), c_ab_scan)
                    if col == 0:
                        pg_wait(t, w_oev)
                    if cc2 > 0:
                        t.matmul(po[:, col:col + 128], Ssb[:, :],
                                 qn[:, cc2 * 128:(cc2 + 1) * 128], start=True, stop=False)
                    t.matmul(po[:, col:col + 128], uis[:, :],
                             aiT[:, bl, :], start=(cc2 == 0), stop=True)
                    t.matmul(psF[:, :], ktm[:, bl, :], uis[:, :],
                             start=True, stop=True).then_inc(ph.sem("msc"), 1)
                em.on("tensor", t_sc)
                n_msc = ph.bump("msc")

                def v_sev(v, cc2=cc2, n_msc=n_msc):
                    v.wait_ge(ph.sem("msc"), n_msc)
                    if cc2 == 0:
                        v.tensor_copy(Sf[:, :], psF[:, :]).then_inc(ph.sem("sev"), 1)
                    else:
                        v.tensor_add(Sf[:, :], Sf[:, :], psF[:, :]).then_inc(ph.sem("sev"), 1)
                em.on("vector", v_sev)
                n_sev = ph.bump("sev")

                def s_scp(sc, n_sev=n_sev):
                    sc.wait_ge(ph.sem("sev"), n_sev)
                    sc.activation(Ssb[:, :], Sf[:, :], AF.Identity).then_inc(ph.sem("scp"), 1)
                em.on("scalar", s_scp)
                ph.bump("scp")

                if cc2 % 4 == 3:
                    def v_oev(v, cc2=cc2, po=po, n_msc=ph.val("msc"),
                              c_ost=ph.val("ost")):
                        v.wait_ge(ph.sem("msc"), n_msc)
                        v.wait_ge(ph.sem("ost"), c_ost)
                        v.tensor_copy(odel[:, (cc2 - 3) * 128:(cc2 + 1) * 128], po[:, :]
                                      ).then_inc(ph.sem("oev"), 1)
                    em.on("vector", v_oev)
                    ph.bump("oev")
            n_oev = ph.val("oev")

            if _B_PART < 5:
                continue
            c_bbc_g = ph.val("bbc")
            for grp in range(4):
                def t_grow(t, s=s, h=h, grp=grp, c_bbc_g=c_bbc_g, w_gg=pg("gg")):
                    t.wait_ge(ph.sem("bbc"), c_bbc_g)
                    pg_wait(t, w_gg)
                    for q4 in range(4):
                        bt = grp * 4 + q4
                        k = t.transpose(psG[0:1, q4 * 128:q4 * 128 + 128],
                                        bmn[:, bt, 8 + h:9 + h], sb_identF[:, :])
                    k.then_inc(ph.sem("gtr"), 1)
                em.on("tensor", t_grow)
                n_gtr = ph.bump("gtr")

                def v_grow(v, grp=grp, n_gtr=n_gtr):
                    v.wait_ge(ph.sem("gtr"), n_gtr)
                    v.tensor_copy(rowA[0:1, grp * 512:(grp + 1) * 512], psG[0:1, :]
                                  ).then_inc(ph.sem("gg"), 1)
                em.on("vector", v_grow)
                ph.bump("gg")
            n_gr2 = ph.val("gg")

            def s_gsig(sc, h=h, n_gr2=n_gr2):
                sc.wait_ge(ph.sem("gg"), n_gr2)
                sc.activation(rowB[0:1, :], rowA[0:1, :], AF.Sigmoid,
                              bias=sb_mixsum[0:1, h:h + 1]).then_inc(ph.sem("gsg"), 1)
            em.on("scalar", s_gsig)
            n_gsg = ph.bump("gsg")

            for ct in range(NTT):
                ps = psA if ct % 2 == 0 else psB

                def t_gbc(t, ct=ct, ps=ps, n_gsg=n_gsg, n_oev=n_oev,
                          w_ab=pg("ab", 1)):
                    t.wait_ge(ph.sem("gsg"), n_gsg)
                    t.wait_ge(ph.sem("oev"), n_oev)
                    pg_wait(t, w_ab)
                    t.matmul(ps[:, :], ones_row_f[0:1, :], rowB[0:1, ct * TT:(ct + 1) * TT],
                             start=True, stop=True).then_inc(ph.sem("gbm"), 1)
                em.on("tensor", t_gbc)
                n_gbm = ph.bump("gbm")

                def v_gbe(v, ct=ct, ps=ps, n_gbm=n_gbm):
                    v.wait_ge(ph.sem("gbm"), n_gbm)
                    v.tensor_copy(gbb[:, ct * TT:(ct + 1) * TT], ps[:, :]
                                  ).then_inc(ph.sem("ab"), 1)
                em.on("vector", v_gbe)
                ph.bump("ab")
            n_gb = ph.val("ab")

            def v_og1(v, n_gb=n_gb, n_v=n_v):
                v.wait_ge(ph.sem("ab"), n_gb)
                v.tensor_sub(odel[:, :], odel[:, :], vf32[:, :]).then_inc(ph.sem("og1"), 1)
            em.on("vector", v_og1)
            n_og1 = ph.bump("og1")

            def g_og2(gp, n_og1=n_og1):
                gp.wait_ge(ph.sem("og1"), n_og1)
                gp.tensor_mul(odel[:, :], odel[:, :], gbb[:, :]).then_inc(ph.sem("og2"), 1)
            em.on("gpsimd", g_og2)
            n_og2 = ph.bump("og2")

            def v_og3(v, n_og2=n_og2):
                v.wait_ge(ph.sem("og2"), n_og2)
                v.tensor_add(odel[:, :], odel[:, :], vf32[:, :]).then_inc(ph.sem("og3"), 1)
            em.on("vector", v_og3)
            n_og3 = ph.bump("og3")

            def s_osq(sc, n_og3=n_og3, c_gm=ph.val("l2m")):
                sc.wait_ge(ph.sem("og3"), n_og3)
                sc.wait_ge(ph.sem("l2m"), c_gm)
                sc.activation(b16s[:, :], odel[:, :], AF.Square).then_inc(ph.sem("l2s"), 1)
            em.on("scalar", s_osq)
            n_osq = ph.bump("l2s")

            for ct in range(NTT):
                def t_osum(t, ct=ct, n_osq=n_osq, w_gg=pg("gg")):
                    t.wait_ge(ph.sem("l2s"), n_osq)
                    pg_wait(t, w_gg)
                    t.matmul(psG[0:1, :], ones_col_b[:, 0:1],
                             b16s[:, ct * TT:(ct + 1) * TT], start=True, stop=True
                             ).then_inc(ph.sem("l2m"), 1)
                em.on("tensor", t_osum)
                n_osm = ph.bump("l2m")

                def v_orow(v, ct=ct, n_osm=n_osm, c_rs=ph.val("orr")):
                    v.wait_ge(ph.sem("l2m"), n_osm)
                    if ct == 0:
                        v.wait_ge(ph.sem("orr"), c_rs)
                    v.tensor_copy(rowA[0:1, ct * TT:(ct + 1) * TT], psG[0:1, :]
                                  ).then_inc(ph.sem("gg"), 1)
                em.on("vector", v_orow)
                ph.bump("gg")
            n_ogg = ph.val("gg")

            def s_orstd(sc, n_ogg=n_ogg):
                sc.wait_ge(ph.sem("gg"), n_ogg)
                sc.activation(rowB[0:1, :], rowA[0:1, :], AF.Ln, scale=1.0 / 128,
                              bias=constf(EPS_LN)[0:1, 0:1])
                sc.activation(rowB[0:1, :], rowB[0:1, :], AF.Exp, scale=-0.5
                              ).then_inc(ph.sem("orr"), 1)
            em.on("scalar", s_orstd)
            n_orr = ph.bump("orr")

            for ct in range(NTT):
                ps = psA if ct % 2 == 0 else psB

                def t_obc(t, ct=ct, ps=ps, n_orr=n_orr, c_of=ph.val("ofin"),
                          w_ab=pg("ab", 1)):
                    t.wait_ge(ph.sem("orr"), n_orr)
                    t.wait_ge(ph.sem("ofin"), c_of)
                    pg_wait(t, w_ab)
                    t.matmul(ps[:, :], ones_row_f[0:1, :], rowB[0:1, ct * TT:(ct + 1) * TT],
                             start=True, stop=True).then_inc(ph.sem("obm"), 1)
                em.on("tensor", t_obc)
                n_obm = ph.bump("obm")

                def v_onm(v, ct=ct, ps=ps, n_obm=n_obm):
                    v.wait_ge(ph.sem("obm"), n_obm)
                    v.tensor_mul(bcb[:, ct * TT:(ct + 1) * TT],
                                 odel[:, ct * TT:(ct + 1) * TT], ps[:, :]
                                 ).then_inc(ph.sem("onm"), 1)
                em.on("vector", v_onm)
                ph.bump("onm")
            n_onm = ph.val("onm")

            def g_ofin(gp, s=s, h=h, n_onm=n_onm):
                gp.wait_ge(ph.sem("onm"), n_onm)
                gp.tensor_single_scalar(odel[:, :], bcb[:, :], sb_onorm[:, 0:1],
                                        ALU.mult).then_inc(ph.sem("ofin"), 1)
            em.on("gpsimd", g_ofin)
            n_ofin = ph.bump("ofin")

            def dma_on(e, s=s, h=h, n_ofin=n_ofin):
                e.wait_ge(ph.sem("ofin"), n_ofin)
                e.dma_start(s_r[s, h, :, :], odel[:, :]).then_inc(ph.sem("ost"), 16)
            em.on("sync", dma_on)
            ph.bump("ost", 16)

        def fin_s(e):
            e.wait_ge(ph.sem("ost"), ph.val("ost"))
        em.on("sync", fin_s)
        em.replay(nc)
        pes.close()
        return

    if _build_b:
        with nc.named_scope("bm"):
            phase_b_main()

    def phase_b_out():
        pes = ExitStack()
        Sl = lambda n, sh, dt: pes.enter_context(nc.sbuf_tensor(n, sh, dt))
        Pl = lambda n, sh: pes.enter_context(nc.psum_tensor(n, sh, F32))
        ph = Sems(nc, es, "bo")
        em = Emitter()
        ow_sb = Sl("bo_ow", [128, DC, D], BF16)
        u1 = Sl("bo_u1", [128, DC, TT], F32)
        onb = [Sl(f"bo_on{j}", [128, DC, TT], BF16) for j in range(2)]
        ps_u = [Pl(f"bo_ps{j}", [128, 512]) for j in range(2)]

        def wload(e):
            e.dma_start(ow_sb[:, :, :], owB.rearrange("(c p) o -> p c o", p=128)
                        ).then_inc(ph.sem("wld"), 16)
        em.on("gpsimd", wload)
        n_wld_b = ph.bump("wld", 16)

        tiles = [(s, tt) for s in range(NSEQ) for tt in range(NTT)]
        for i, (s, tt) in enumerate(tiles):
            jb = i % 2

            def dma_on(e, s=s, tt=tt, i=i, jb=jb, c_mm=ph.val("mm")):
                if i >= 2:
                    e.wait_ge(ph.sem("mm"), c_mm - DC)
                e.dma_start(onb[jb][:, :, :], dtile(s_r, s, tt * TT, (tt + 1) * TT)
                            ).then_inc(ph.sem("onl"), 16)
            em.on("gpsimd", dma_on)
            n_onl = ph.bump("onl", 16)

            for o in range(DC):
                b = o % 2

                def t_mm(t, s=s, tt=tt, o=o, b=b, jb=jb, n_onl=n_onl, c_ev=ph.val("ev"),
                         n_wld_b=n_wld_b):
                    t.wait_ge(ph.sem("wld"), n_wld_b)
                    t.wait_ge(ph.sem("onl"), n_onl)
                    if c_ev >= 2:
                        t.wait_ge(ph.sem("ev"), c_ev - 1)
                    for c in range(DC):
                        k = t.matmul(ps_u[b][:, :], ow_sb[:, c, o * 128:(o + 1) * 128],
                                     onb[jb][:, c, :], start=(c == 0), stop=(c == DC - 1))
                    k.then_inc(ph.sem("mm"), 1)
                em.on("tensor", t_mm)
                n_mm = ph.bump("mm")

                def v_ev(v, o=o, b=b, n_mm=n_mm, i=i, c_acc=ph.val("acc"),
                         c_st=ph.val("st")):
                    v.wait_ge(ph.sem("mm"), n_mm)
                    if o == 0 and i >= 1:
                        v.wait_ge(ph.sem("acc"), c_acc)
                        v.wait_ge(ph.sem("st"), c_st)
                    v.tensor_copy(u1[:, o, :], ps_u[b][:, :]).then_inc(ph.sem("ev"), 1)
                em.on("vector", v_ev)
                ph.bump("ev")
            n_ev = ph.val("ev")

            def dma_acc(e, s=s, tt=tt, n_ev=n_ev):
                e.wait_ge(ph.sem("ev"), n_ev)
                e.dma_start(u1[:, :, :], dtile(s_xB, s, tt * TT, (tt + 1) * TT),
                            accum_op=ALU.add).then_inc(ph.sem("acc"), 16)
            em.on("gpsimd", dma_acc)
            n_acc = ph.bump("acc", 16)

            def dma_st(e, s=s, tt=tt, n_acc=n_acc):
                e.wait_ge(ph.sem("acc"), n_acc)
                e.dma_start(dtile(s_xC, s, tt * TT, (tt + 1) * TT), u1[:, :, :]
                            ).then_inc(ph.sem("st"), 16)
            em.on("sync", dma_st)
            ph.bump("st", 16)

        def fin(e):
            e.wait_ge(ph.sem("st"), ph.val("st"))
        em.on("sync", fin)
        em.replay(nc)
        pes.close()

    if _build_b:
        with nc.named_scope("bo"):
            phase_b_out()
        closeA()

    if _PHASES >= 6:
        openA("a5")
        with nc.named_scope("f2l"):
            phase_f_ln("f2l", s_xC)
        openB("b5")
        with nc.named_scope("f2a"):
            phase_f_att("f2a")
        with nc.named_scope("f2o"):
            phase_f_out("f2o", wnatF)
        with nc.named_scope("f2m"):
            phase_f_moe("f2m", wnatF, s_xC, y_out)
        closeA(); closeB()


    _dram_by_name = {"s_r": s_r, "s_u2": s_u2, "s_xA": s_xA, "s_ffn_in": s_ffn_in,
                     "s_a": s_a, "s_xB": s_xB, "s_xC": s_xC, "xbfB": s_xC, "wnat": s_xC}

    def emit_copy_dram_to_y(src):
        pes2 = ExitStack()
        ph2 = Sems(nc, es, "fin")
        em2 = Emitter()
        buf = pes2.enter_context(nc.sbuf_tensor("finbuf", [128, DC, TT], F32))
        tiles = [(s, tt) for s in range(NSEQ) for tt in range(NTT)]
        base_n = ph2.val("st")

        def go(e, base_n=base_n):
            n = base_n
            for i, (s, tt) in enumerate(tiles):
                e.dma_start(buf[:, :, :], dtile(src, s, tt * TT, (tt + 1) * TT)
                            ).then_inc(ph2.sem("st"), 16)
                n += 16
                e.wait_ge(ph2.sem("st"), n)
                e.dma_start(dtile(y_out, s, tt * TT, (tt + 1) * TT), buf[:, :, :]
                            ).then_inc(ph2.sem("st"), 16)
                n += 16
            e.wait_ge(ph2.sem("st"), n)
        em2.on("sync", go)
        for _ in range(2 * len(tiles)):
            ph2.bump("st", 16)
        em2.replay(nc)
        pes2.close()

    if _DEBUG_TAP is not None:
        emit_copy_dram_to_y(_dram_by_name[_DEBUG_TAP])
        return nc

    if _PHASES < 6:
        emit_copy_dram_to_y(s_r)
    return nc



def _host_inputs(inputs):
    f32 = lambda a: np.ascontiguousarray(np.asarray(a), dtype=np.float32)
    shared = {}
    shared["wqkvT"] = f32(np.asarray(inputs["L0_attn_a_w_qkv"]).T)
    shared["woutT"] = f32(np.asarray(inputs["L0_attn_a_w_out"]).T)
    shared["w1T"] = f32(np.asarray(inputs["L0_attn_a_w1"]).T)
    shared["w2T"] = f32(np.asarray(inputs["L0_attn_a_w2"]).T)
    shared["projW"] = f32(inputs["L0_attn_a_proj"])
    vec2d = lambda v, c: f32(np.asarray(v).reshape(c, 128).T)
    shared["gammaA"] = vec2d(inputs["L0_attn_a_gamma"], DC)
    shared["betaA"] = vec2d(inputs["L0_attn_a_beta"], DC)
    shared["b_outA"] = vec2d(inputs["L0_attn_a_b_out"], DC)
    shared["b1A"] = vec2d(inputs["L0_attn_a_b1"], 32)
    shared["b2A"] = vec2d(inputs["L0_attn_a_b2"], DC)
    shared["Wq"] = f32(inputs["L0_ffn_Wq"]); shared["Wk"] = f32(inputs["L0_ffn_Wk"])
    shared["Wv"] = f32(inputs["L0_ffn_Wv"]); shared["Wo"] = f32(inputs["L0_ffn_Wo"])
    shared["bqF"] = vec2d(inputs["L0_ffn_bq"], DC); shared["bkF"] = vec2d(inputs["L0_ffn_bk"], DC)
    shared["bvF"] = vec2d(inputs["L0_ffn_bv"], DC); shared["boF"] = vec2d(inputs["L0_ffn_bo"], DC)
    shared["ln1g"] = vec2d(inputs["L0_ffn_ln1_g"], DC); shared["ln1b"] = vec2d(inputs["L0_ffn_ln1_b"], DC)
    shared["ln2g"] = vec2d(inputs["L0_ffn_ln2_g"], DC); shared["ln2b"] = vec2d(inputs["L0_ffn_ln2_b"], DC)
    shared["WrD"] = f32(inputs["L0_ffn_Wr"])
    shared["brD"] = f32(np.asarray(inputs["L0_ffn_br"]).reshape(1, 8))
    shared["ew1"] = f32(inputs["L0_ffn_e_w1"])
    shared["ew2"] = f32(inputs["L0_ffn_e_w2"])
    shared["eb1"] = f32(np.asarray(inputs["L0_ffn_e_b1"]).reshape(8, 32, 128).transpose(2, 0, 1).reshape(128, 256))
    shared["eb2"] = f32(np.asarray(inputs["L0_ffn_e_b2"]).reshape(8, 8, 128).transpose(2, 0, 1).reshape(128, 64))
    shared["qwB"] = f32(inputs["L0_attn_b_q_w"]); shared["kwB"] = f32(inputs["L0_attn_b_k_w"])
    shared["vwB"] = f32(inputs["L0_attn_b_v_w"]); shared["owB"] = f32(inputs["L0_attn_b_o_w"])
    shared["bwmix"] = f32(np.concatenate([np.asarray(inputs["L0_attn_b_b_w"]),
                                          np.asarray(inputs["L0_attn_b_mix_w"])], axis=1))
    cw = lambda v: f32(np.asarray(v).reshape(DC, 128, 4).transpose(1, 0, 2))
    shared["convq"] = cw(inputs["L0_attn_b_conv_q_w"])
    shared["convk"] = cw(inputs["L0_attn_b_conv_k_w"])
    shared["convv"] = cw(inputs["L0_attn_b_conv_v_w"])
    shared["mixb_in"] = f32(np.asarray(inputs["L0_attn_b_mix_b"]).reshape(1, 8))
    shared["mixbias_in"] = f32(np.asarray(inputs["L0_attn_b_mix_bias"]).reshape(1, 8))
    shared["onormW"] = f32(np.asarray(inputs["L0_attn_b_o_norm_w"]).reshape(128, 1))

    bf = ml_dtypes.bfloat16
    shared["identB"] = np.eye(128, dtype=bf)
    shared["identF"] = np.eye(128, dtype=np.float32)
    kq = (np.arange(128)[None, :] >= np.arange(128)[:, None])
    shared["mask_kq"] = kq.astype(bf)
    ii = np.arange(128)
    strict_low = (ii[None, :] < ii[:, None]) & (ii[None, :] // 64 == ii[:, None] // 64)
    shared["maskA_neg"] = (-1.0 * strict_low).astype(bf)
    shared["maskAT_neg"] = (-1.0 * strict_low.T).astype(bf)
    shared["maskTriIncT"] = (np.arange(128)[:, None] <= np.arange(128)[None, :]).astype(bf)
    for m in range(4):
        shared[f"m512_{m}"] = (np.arange(512)[None, :] >= (m * 128 + np.arange(128)[:, None])).astype(bf)

    x = np.asarray(inputs["L0_attn_a_x"], dtype=np.float32)
    in_maps = []
    for c in range(NCORE):
        m = dict(shared)
        xs = np.ascontiguousarray(x[2 * c:2 * c + 2].transpose(0, 2, 1))
        m["x"] = xs.reshape(NSEQ, DC, 128, T)
        in_maps.append(m)
    return in_maps


def _assemble(res):
    outs = []
    for c in range(NCORE):
        y = res.results[c]["y"].reshape(NSEQ, D, T)
        outs.append(y.transpose(0, 2, 1))
    return np.ascontiguousarray(np.concatenate(outs, axis=0), dtype=np.float32)


def kernel(**inputs):
    in_maps = _host_inputs(inputs)
    nc = build_program()
    res = run_bass_kernel_spmd(nc, in_maps, core_ids=list(range(NCORE)))
    return _assemble(res)
